# version 03
# baseline (speedup 1.0000x reference)
import math
import numpy as np
import ml_dtypes
from contextlib import ExitStack

import concourse.bass as bass
import concourse.mybir as mybir
from concourse.bass_utils import run_bass_kernel_spmd

F32 = mybir.dt.float32
BF16 = mybir.dt.bfloat16
AF = mybir.ActivationFunctionType
ALU = mybir.AluOpType
BF = ml_dtypes.bfloat16

B, T, D = 4, 4096, 1024
DI, NST, DTR = 2048, 16, 64
H, HD = 16, 64
EPS = 1e-5
TLOC, NTILE, DHALF = 2048, 16, 1024
NEG = -1e30
NCHIP = 8

def bound_of(q):
    return min(T - 1, max(100 * (q // 100) + 99, q + 50))

def attn_tables():
    n_kt, ffa = [], []
    for i in range(NTILE):
        g_hi = 2 * i + 1
        n_kt.append((bound_of(128 * g_hi + 127) + 1 + 127) // 128)
        g_lo = 2 * i
        ffa.append((bound_of(128 * g_lo) + 1) // 128)
    return n_kt, ffa

N_KT, FFA = attn_tables()
MAXMASK = max(n - f for n, f in zip(N_KT, FFA))


_STREAM = {"t": "tensor", "v": "vector", "s": "scalar", "p": "gpsimd",
           "yd": "sync", "pd": "gpsimd", "sd": "scalar", "cc": "gpsimd"}
_AMT = {"t": 1, "v": 1, "s": 1, "p": 1, "yd": 16, "pd": 16, "sd": 16, "cc": 1}
_COMPUTE = {"t", "v", "s", "p"}


NDMASLOT = 24


class Prog:

    def __init__(self, name="prog"):
        self.name = name
        self.nc = bass.Bass("TRN2", target_bir_lowering=False, debug=False)
        self.ops = []
        self.lastw = {}
        self.readers = {}
        self.barrier_idx = None
        self.slot_rr = 0
        self.slot_prev = {}
        self.slot_uses = {}

    def _record(self, eng, fn, deps):
        i = len(self.ops)
        o = dict(eng=eng, fn=fn, deps=deps, inc=False, cnt=None,
                 sem=eng, amt=1, consumers=[])
        if eng == "cc":
            n = sum(1 for oo in self.ops if oo["eng"] == "cc")
            o["sem"] = f"cc{n}"
            o["amt"] = None
            o["inc"] = True
            o["cnt"] = 1
        elif eng not in _COMPUTE:
            slot = self.slot_rr % NDMASLOT
            self.slot_rr += 1
            prev = self.slot_prev.get(slot)
            if prev is not None:
                deps.add(prev)
                deps.update(self.ops[prev]["consumers"])
            self.slot_prev[slot] = i
            self.slot_uses[slot] = self.slot_uses.get(slot, 0) + 1
            o["sem"] = f"d{slot}"
            o["amt"] = 16
            o["inc"] = True
            o["cnt"] = self.slot_uses[slot] * o["amt"]
        deps.discard(i)
        self.ops.append(o)
        for j in deps:
            self.ops[j]["consumers"].append(i)
        return i

    def op(self, eng, fn, reads=(), writes=()):
        i = len(self.ops)
        deps = set()

        def add_dep(j):
            if j is None or j == i:
                return
            je = self.ops[j]["eng"]
            if je == eng and eng in _COMPUTE and je in _COMPUTE:
                return
            deps.add(j)

        add_dep(self.barrier_idx)
        for r in reads:
            add_dep(self.lastw.get(r))
        for w in writes:
            add_dep(self.lastw.get(w))
            for j in self.readers.get(w, ()):
                add_dep(j)
        idx = self._record(eng, fn, deps)
        assert idx == i
        for r in reads:
            self.readers.setdefault(r, []).append(i)
        for w in writes:
            self.lastw[w] = i
            self.readers[w] = []
        return i

    def barrier(self):
        deps = set()
        lastc = {}
        start = -1 if self.barrier_idx is None else self.barrier_idx
        for j, o in enumerate(self.ops):
            if o["eng"] in _COMPUTE:
                lastc[o["eng"]] = j
            elif j > start:
                deps.add(j)
        deps.update(lastc.values())
        i = self._record("v", lambda e: e.engine_nop(), deps)
        self.barrier_idx = i
        self.lastw = {}
        self.readers = {}
        return i

    def fence_outputs(self):
        self.barrier()

    def emit(self, stack):
        nc = self.nc
        for o in self.ops:
            for j in o["deps"]:
                self.ops[j]["inc"] = True
        cnt = {}
        for o in self.ops:
            if o["inc"] and o["eng"] in _COMPUTE:
                cnt[o["sem"]] = cnt.get(o["sem"], 0) + o["amt"]
                o["cnt"] = cnt[o["sem"]]
        semnames = sorted({o["sem"] for o in self.ops if o["inc"]})
        sems = {s: stack.enter_context(nc.semaphore("sem_" + s))
                for s in semnames}
        streams = {}
        for idx, o in enumerate(self.ops):
            streams.setdefault(_STREAM[o["eng"]], []).append(idx)
        block = stack.enter_context(nc.Block())
        for sname, idxs in streams.items():
            def body(e, idxs=idxs):
                water = {}
                for idx in idxs:
                    o = self.ops[idx]
                    for j in sorted(o["deps"]):
                        dj = self.ops[j]
                        s, need = dj["sem"], dj["cnt"]
                        assert need is not None, (idx, j, dj)
                        if water.get(s, 0) < need:
                            e.wait_ge(sems[s], need)
                            water[s] = need
                    inst = o["fn"](e)
                    if o["inc"]:
                        if o["eng"] == "cc":
                            inst.then_inc(sems[o["sem"]])
                        else:
                            inst.then_inc(sems[o["sem"]], o["amt"])
            getattr(block, sname)(body)


class NcProxy:

    def __init__(self, nc, prefix):
        self._nc = nc
        self._prefix = prefix

    def __getattr__(self, name):
        return getattr(self._nc, name)

    def sbuf_tensor(self, name, *a, **k):
        return self._nc.sbuf_tensor(self._prefix + name, *a, **k)

    def psum_tensor(self, name, *a, **k):
        return self._nc.psum_tensor(self._prefix + name, *a, **k)


def ln_rows(P, nc, st, tag, x_tiles, xkeys, Tn, want_rms=False, in_bf16=False):
    E = st.enter_context
    nck = len(x_tiles)
    m_sb = E(nc.sbuf_tensor(f"{tag}_m", [1, Tn], F32))
    q_sb = E(nc.sbuf_tensor(f"{tag}_q", [1, Tn], F32))
    r_sb = E(nc.sbuf_tensor(f"{tag}_r", [1, Tn], F32))
    mr_sb = E(nc.sbuf_tensor(f"{tag}_mr", [1, Tn], F32))
    rms_sb = E(nc.sbuf_tensor(f"{tag}_rms", [1, Tn], F32)) if want_rms else None
    var_sb = q_sb
    tmp_sb = E(nc.sbuf_tensor(f"{tag}_tmp", [1, Tn], F32))
    xsq = [E(nc.sbuf_tensor(f"{tag}_xsq{i}", [128, 512], F32)) for i in range(2)]
    pm = E(nc.psum_tensor(f"{tag}_pm", [1, 512], F32))
    pq = E(nc.psum_tensor(f"{tag}_pq", [1, 512], F32))
    ones = E(nc.sbuf_tensor(f"{tag}_ones", [128, 1], BF16 if in_bf16 else F32))
    onesf = E(nc.sbuf_tensor(f"{tag}_onesf", [128, 1], F32))
    epsc = E(nc.sbuf_tensor(f"{tag}_eps", [1, 1], F32))
    P.op("v", lambda e: e.memset(ones[:], 1.0 / D), writes=[f"{tag}ones"])
    P.op("v", lambda e: e.memset(onesf[:], 1.0 / D), writes=[f"{tag}onesf"])
    P.op("v", lambda e: e.memset(epsc[:], EPS), writes=[f"{tag}eps"])

    ntc = Tn // 512
    for tc in range(ntc):
        for kc in range(nck):
            xi = x_tiles[kc][:, tc * 512:(tc + 1) * 512]
            sq = xsq[kc % 2]
            P.op("s", lambda e, xi=xi, sq=sq: e.activation(sq[:], xi, AF.Square),
                 reads=[xkeys[kc]], writes=[f"{tag}sq{kc % 2}"])
            P.op("t", lambda e, xi=xi, kc=kc: e.matmul(
                pm[:], ones[:], xi, start=(kc == 0), stop=(kc == nck - 1)),
                reads=[xkeys[kc], f"{tag}ones"], writes=[f"{tag}pm"])
            P.op("t", lambda e, sq=sq, kc=kc: e.matmul(
                pq[:], onesf[:], sq[:], start=(kc == 0), stop=(kc == nck - 1)),
                reads=[f"{tag}sq{kc % 2}", f"{tag}onesf"], writes=[f"{tag}pq"])
        sl = slice(tc * 512, (tc + 1) * 512)
        P.op("v", lambda e, sl=sl: e.tensor_copy(m_sb[:, sl], pm[:]),
             reads=[f"{tag}pm"], writes=[f"{tag}m"])
        P.op("v", lambda e, sl=sl: e.tensor_copy(q_sb[:, sl], pq[:]),
             reads=[f"{tag}pq"], writes=[f"{tag}q"])
    P.op("v", lambda e: e.tensor_mul(tmp_sb[:], m_sb[:], m_sb[:]),
         reads=[f"{tag}m"], writes=[f"{tag}tmp"])
    P.op("v", lambda e: e.tensor_sub(var_sb[:], q_sb[:], tmp_sb[:]),
         reads=[f"{tag}q", f"{tag}tmp"], writes=[f"{tag}q", f"{tag}var"])
    P.op("s", lambda e: e.activation(tmp_sb[:], var_sb[:], AF.Sqrt,
         bias=epsc[0:1, 0:1]), reads=[f"{tag}var", f"{tag}eps"],
         writes=[f"{tag}tmp"])
    P.op("v", lambda e: e.reciprocal(r_sb[:], tmp_sb[:]),
         reads=[f"{tag}tmp"], writes=[f"{tag}r"])
    P.op("v", lambda e: e.tensor_mul(mr_sb[:], m_sb[:], r_sb[:]),
         reads=[f"{tag}m", f"{tag}r"], writes=[f"{tag}mr"])
    if want_rms:
        P.op("v", lambda e: e.tensor_scalar(tmp_sb[:], var_sb[:],
             epsc[0:1, 0:1], None, ALU.add),
             reads=[f"{tag}var", f"{tag}eps"], writes=[f"{tag}tmp"])
        P.op("v", lambda e: e.reciprocal(tmp_sb[:], tmp_sb[:]),
             reads=[f"{tag}tmp"], writes=[f"{tag}tmp"])
        P.op("v", lambda e: e.tensor_mul(tmp_sb[:], var_sb[:], tmp_sb[:]),
             reads=[f"{tag}var", f"{tag}tmp"], writes=[f"{tag}tmp"])
        P.op("s", lambda e: e.activation(tmp_sb[:], tmp_sb[:], AF.Sqrt,
             bias=epsc[0:1, 0:1]), reads=[f"{tag}tmp", f"{tag}eps"],
             writes=[f"{tag}tmp"])
        P.op("v", lambda e: e.reciprocal(rms_sb[:], tmp_sb[:]),
             reads=[f"{tag}tmp"], writes=[f"{tag}rms"])
    rows = dict(r=(r_sb, f"{tag}r"), mr=(mr_sb, f"{tag}mr"))
    if want_rms:
        rows["rms"] = (rms_sb, f"{tag}rms")
    return rows


def bcast_rows(P, nc, st, tag, rows, Tn, names):
    E = st.enter_context
    onesr = E(nc.sbuf_tensor(f"{tag}_onesr", [1, 128], F32))
    P.op("v", lambda e: e.memset(onesr[:], 1.0), writes=[f"{tag}onesr"])
    pb = E(nc.psum_tensor(f"{tag}_pb", [128, 512], F32))
    out = {}
    for nm in names:
        row, rkey = rows[nm]
        dst = E(nc.sbuf_tensor(f"{tag}_b_{nm}", [128, Tn], BF16))
        for tc in range(Tn // 512):
            sl = slice(tc * 512, (tc + 1) * 512)
            P.op("t", lambda e, row=row, sl=sl: e.matmul(
                pb[:], onesr[:], row[:, sl], start=True, stop=True),
                reads=[rkey, f"{tag}onesr"], writes=[f"{tag}pb"])
            P.op("s", lambda e, dst=dst, sl=sl: e.copy(dst[:, sl], pb[:]),
                 reads=[f"{tag}pb"], writes=[f"{tag}b{nm}"])
        out[nm] = (dst, f"{tag}b{nm}")
    return out


def decl_LA(nc, I, O):
    return dict(
        xfm=nc.dram_tensor("xfm", [D, TLOC], F32, kind=I).ap(),
        norm_w_r=nc.dram_tensor("norm_w_r", [128, 8], F32, kind=I).ap(),
        W_in=nc.dram_tensor("W_in", [D, 2 * DI], F32, kind=I).ap(),
        t1=nc.dram_tensor("t1", [D, TLOC], BF16, kind=O).ap(),
        xin=nc.dram_tensor("xin", [DI, TLOC], BF16, kind=O).ap(),
        res=nc.dram_tensor("res", [DI, TLOC], BF16, kind=O).ap(),
    )


def build_LA():
    P = Prog("LA")
    st = ExitStack()
    body_LA(P, st, decl_LA(P.nc, "ExternalInput", "ExternalOutput"))
    P.fence_outputs()
    return P, st


def body_LA(P, st, dd):
    nc = NcProxy(P.nc, "la_")
    s1 = ExitStack()
    E = st.enter_context
    E1 = s1.enter_context
    xfm, nw, w_in = dd["xfm"], dd["norm_w_r"], dd["W_in"]
    t1_o, xin_o, res_o = dd["t1"], dd["xin"], dd["res"]

    nw_sb = E(nc.sbuf_tensor("nw", [128, 8], F32))
    t1 = [E(nc.sbuf_tensor(f"t1_{c}", [128, TLOC], BF16)) for c in range(8)]
    h2 = [E(nc.sbuf_tensor(f"h2_{c}", [128, TLOC], BF16)) for c in range(8)]
    x = [E1(nc.sbuf_tensor(f"x{c}", [128, TLOC], F32)) for c in range(8)]
    tmp = [E1(nc.sbuf_tensor(f"tmp{i}", [128, 512], F32)) for i in range(2)]

    for c in range(8):
        P.op("yd", lambda e, c=c: e.dma_start(
            out=x[c][:], in_=xfm[c * 128:(c + 1) * 128, :]), writes=[f"x{c}"])
    P.op("yd", lambda e: e.dma_start(out=nw_sb[:], in_=nw), writes=["nw"])

    rows = ln_rows(P, nc, s1, "la", [x[c][:] for c in range(8)],
                   [f"x{c}" for c in range(8)], TLOC, want_rms=True)
    bc = bcast_rows(P, nc, s1, "la", rows, TLOC, ["r", "mr", "rms"])
    rb, rbk = bc["r"]
    mrb, mrbk = bc["mr"]
    rmsb, rmsbk = bc["rms"]

    for c in range(8):
        for tc in range(TLOC // 512):
            sl = slice(tc * 512, (tc + 1) * 512)
            tb = tmp[tc % 2]
            P.op("v", lambda e, c=c, sl=sl, tb=tb: e.tensor_mul(
                tb[:], x[c][:, sl], rb[:, sl]),
                reads=[f"x{c}", rbk], writes=[f"tmp{tc % 2}"])
            P.op("v", lambda e, c=c, sl=sl, tb=tb: e.tensor_sub(
                t1[c][:, sl], tb[:], mrb[:, sl]),
                reads=[f"tmp{tc % 2}", mrbk], writes=[f"t1_{c}"])
            P.op("v", lambda e, c=c, sl=sl: e.scalar_tensor_tensor(
                h2[c][:, sl], t1[c][:, sl], nw_sb[:, c:c + 1], rmsb[:, sl],
                ALU.mult, ALU.mult),
                reads=[f"t1_{c}", "nw", rmsbk], writes=[f"h2_{c}"])
        P.op("yd", lambda e, c=c: e.dma_start(
            out=t1_o[c * 128:(c + 1) * 128, :], in_=t1[c][:]), reads=[f"t1_{c}"])

    s1.close()
    P.barrier()
    wbf = E(nc.sbuf_tensor("wbf", [128, 8 * 2048], BF16))
    stage = [E(nc.sbuf_tensor(f"stage{i}", [128, TLOC], BF16)) for i in range(2)]
    pw = [E(nc.psum_tensor(f"pw{i}", [128, 512], F32)) for i in range(2)]
    for half, dst in ((0, xin_o), (1, res_o)):
        for kc in range(8):
            P.op("pd", lambda e, kc=kc, half=half: e.dma_start(
                out=wbf[:, kc * 2048:(kc + 1) * 2048],
                in_=w_in[kc * 128:(kc + 1) * 128,
                         half * 2048:(half + 1) * 2048]),
                writes=[f"wbf{kc}"])
        for oc in range(16):
            sg = stage[oc % 2]
            for tc in range(4):
                ps = pw[tc % 2]
                for kc in range(8):
                    P.op("t", lambda e, kc=kc, oc=oc, tc=tc, ps=ps: e.matmul(
                        ps[:], wbf[:, kc * 2048 + oc * 128: kc * 2048 + oc * 128 + 128],
                        h2[kc][:, tc * 512:(tc + 1) * 512],
                        start=(kc == 0), stop=(kc == 7)),
                        reads=[f"wbf{kc}", f"h2_{kc}"], writes=[f"pw{tc % 2}"])
                P.op("s", lambda e, tc=tc, sg=sg, ps=ps: e.copy(
                    sg[:, tc * 512:(tc + 1) * 512], ps[:]),
                    reads=[f"pw{tc % 2}"], writes=[f"stage{oc % 2}"])
            P.op("yd", lambda e, oc=oc, dst=dst, sg=sg: e.dma_start(
                out=dst[oc * 128:(oc + 1) * 128, :], in_=sg[:]),
                reads=[f"stage{oc % 2}"])


def decl_LB(nc, I, O):
    d = dict(
        xin_pad=nc.dram_tensor("xin_pad", [DI, T + 3], BF16, kind=I).ap(),
        conv_w_r=nc.dram_tensor("conv_w_r", [128, 64], F32, kind=I).ap(),
        conv_b_r=nc.dram_tensor("conv_b_r", [128, 16], F32, kind=I).ap(),
        wx_dlt=nc.dram_tensor("wx_dlt", [DI, 64], F32, kind=I).ap(),
        wx_b=nc.dram_tensor("wx_b", [DI, 16], F32, kind=I).ap(),
        wx_c=nc.dram_tensor("wx_c", [DI, 16], F32, kind=I).ap(),
        wdt_half=nc.dram_tensor("wdt_half", [64, DHALF], F32, kind=I).ap(),
        bdt_r=nc.dram_tensor("bdt_r", [128, 8], F32, kind=I).ap(),
        A_sc=nc.dram_tensor("A_sc", [128, 128], F32, kind=I).ap(),
        D_sc=nc.dram_tensor("D_sc", [128, 8], F32, kind=I).ap(),
        R16=nc.dram_tensor("R16", [128, 16 * 128], F32, kind=I).ap(),
        RB0=nc.dram_tensor("RB0", [16, 128], F32, kind=I).ap(),
        G4=nc.dram_tensor("G4", [128, 32], F32, kind=I).ap(),
        msel=nc.dram_tensor("msel", [128, 2], F32, kind=I).ap(),
        y2=nc.dram_tensor("y2", [DHALF, T], BF16, kind=O).ap(),
    )
    return d


def build_LB():
    P = Prog("LB")
    st = ExitStack()
    dd = decl_LB(P.nc, "ExternalInput", "ExternalOutput")
    body_LB(P, st, dd)
    P.fence_outputs()
    return P, st


def body_LB(P, st, dd):
    nc = NcProxy(P.nc, "lb_")
    E = st.enter_context
    xin = dd["xin_pad"]
    cwr, cbr = dd["conv_w_r"], dd["conv_b_r"]
    wxd, wxb, wxc = dd["wx_dlt"], dd["wx_b"], dd["wx_c"]
    wdt, bdt = dd["wdt_half"], dd["bdt_r"]
    asc, dsc = dd["A_sc"], dd["D_sc"]
    r16, rb0, g4 = dd["R16"], dd["RB0"], dd["G4"]
    mseli = dd["msel"]
    y2_o = dd["y2"]
    xc_d = nc.dram_tensor("lb_xc_d", [DI, T], BF16).ap()
    dl_d = nc.dram_tensor("lb_dl_d", [DHALF, T], BF16).ap()
    u_d = nc.dram_tensor("lb_u_d", [DHALF, T], BF16).ap()
    y_d = nc.dram_tensor("lb_y_d", [DHALF, T], BF16).ap()
    msel = E(nc.sbuf_tensor("msel", [128, 2], F32))
    P.op("yd", lambda e: e.dma_start(out=msel[:], in_=mseli), writes=["msel"])

    cw_sb = E(nc.sbuf_tensor("cw", [128, 64], F32))
    cb_sb = E(nc.sbuf_tensor("cb", [128, 16], F32))
    bdt_sb = E(nc.sbuf_tensor("bdt", [128, 8], F32))
    asc_sb = E(nc.sbuf_tensor("asc", [128, 128], F32))
    dsc_sb = E(nc.sbuf_tensor("dsc", [128, 8], F32))
    r16_sb = E(nc.sbuf_tensor("r16", [128, 16 * 128], BF16))
    rb0_sb = E(nc.sbuf_tensor("rb0", [16, 128], BF16))
    g4_sb = E(nc.sbuf_tensor("g4", [128, 32], BF16))
    brep = E(nc.sbuf_tensor("brep", [128, T], BF16))
    crep = E(nc.sbuf_tensor("crep", [128, T], BF16))
    for t_, ap_, nm in [(cw_sb, cwr, "cw"), (cb_sb, cbr, "cb"),
                        (bdt_sb, bdt, "bdt"), (asc_sb, asc, "asc"),
                        (dsc_sb, dsc, "dsc")]:
        P.op("yd", lambda e, t_=t_, ap_=ap_: e.dma_start(out=t_[:], in_=ap_),
             writes=[nm])
    for t_, ap_, nm in [(r16_sb, r16, "r16"), (rb0_sb, rb0, "rb0"),
                        (g4_sb, g4, "g4")]:
        P.op("pd", lambda e, t_=t_, ap_=ap_: e.dma_start(out=t_[:], in_=ap_),
             writes=[nm])

    s1 = ExitStack()
    E1 = s1.enter_context
    dlt_bf = E(nc.sbuf_tensor("dlt_bf", [64, T], BF16))
    b_bf = E(nc.sbuf_tensor("b_bf", [16, T], BF16))
    c_bf = E(nc.sbuf_tensor("c_bf", [16, T], BF16))
    wxd_sb = E1(nc.sbuf_tensor("wxd", [128, 16 * 64], BF16))
    wxb_sb = E1(nc.sbuf_tensor("wxb", [128, 16 * 16], BF16))
    wxc_sb = E1(nc.sbuf_tensor("wxc", [128, 16 * 16], BF16))
    for kc in range(16):
        P.op("pd", lambda e, kc=kc: e.dma_start(
            out=wxd_sb[:, kc * 64:(kc + 1) * 64],
            in_=wxd[kc * 128:(kc + 1) * 128, :]), writes=["wxd"])
        P.op("pd", lambda e, kc=kc: e.dma_start(
            out=wxb_sb[:, kc * 16:(kc + 1) * 16],
            in_=wxb[kc * 128:(kc + 1) * 128, :]), writes=["wxb"])
        P.op("pd", lambda e, kc=kc: e.dma_start(
            out=wxc_sb[:, kc * 16:(kc + 1) * 16],
            in_=wxc[kc * 128:(kc + 1) * 128, :]), writes=["wxc"])
    xw = [E1(nc.sbuf_tensor(f"xw{i}", [128, 2048 + 3], BF16)) for i in range(2)]
    acc = [E1(nc.sbuf_tensor(f"acc{i}", [128, 2048], F32)) for i in range(2)]
    xct = [E1(nc.sbuf_tensor(f"xct{k}", [128, 2048], BF16)) for k in range(16)]
    pdlt = E1(nc.psum_tensor("pdlt", [64, 512], F32))
    pbx = E1(nc.psum_tensor("pbx", [16, 512], F32))
    pcx = E1(nc.psum_tensor("pcx", [16, 512], F32))
    for th in range(2):
        bx = th * 2048
        for kc in range(16):
            w = xw[kc % 2]
            a = acc[kc % 2]
            xc = xct[kc]
            P.op("yd", lambda e, kc=kc, bx=bx, w=w: e.dma_start(
                out=w[:], in_=xin[kc * 128:(kc + 1) * 128, bx:bx + 2051]),
                writes=[f"xw{kc % 2}"])
            P.op("v", lambda e, w=w, a=a, kc=kc: e.tensor_scalar(
                a[:], w[:, 0:2048], cw_sb[:, kc * 4:kc * 4 + 1], None, ALU.mult),
                reads=[f"xw{kc % 2}", "cw"], writes=[f"acc{kc % 2}"])
            for j in range(1, 4):
                P.op("v", lambda e, w=w, a=a, kc=kc, j=j: e.scalar_tensor_tensor(
                    a[:], w[:, j:j + 2048], cw_sb[:, kc * 4 + j:kc * 4 + j + 1],
                    a[:], ALU.mult, ALU.add),
                    reads=[f"xw{kc % 2}", "cw", f"acc{kc % 2}"],
                    writes=[f"acc{kc % 2}"])
            P.op("s", lambda e, a=a, xc=xc, kc=kc: e.activation(
                xc[:], a[:], AF.Silu, bias=cb_sb[:, kc:kc + 1]),
                reads=[f"acc{kc % 2}", "cb"], writes=[f"xct{kc}"])
            P.op("yd", lambda e, kc=kc, bx=bx, xc=xc: e.dma_start(
                out=xc_d[kc * 128:(kc + 1) * 128, bx:bx + 2048], in_=xc[:]),
                reads=[f"xct{kc}"])
        for tq in range(4):
            sl = slice(tq * 512, (tq + 1) * 512)
            gsl = slice(bx + tq * 512, bx + (tq + 1) * 512)
            for kc in range(16):
                st_, sp_ = (kc == 0), (kc == 15)
                for ps, wsb, wk, pk, wd in [
                        (pdlt, wxd_sb, "wxd", "pdlt", 64),
                        (pbx, wxb_sb, "wxb", "pbx", 16),
                        (pcx, wxc_sb, "wxc", "pcx", 16)]:
                    P.op("t", lambda e, ps=ps, wsb=wsb, kc=kc, sl=sl, wd=wd,
                         st_=st_, sp_=sp_: e.matmul(
                             ps[:], wsb[:, kc * wd:(kc + 1) * wd],
                             xct[kc][:, sl], start=st_, stop=sp_),
                         reads=[f"xct{kc}", wk], writes=[pk])
            P.op("s", lambda e, gsl=gsl: e.copy(dlt_bf[:, gsl], pdlt[:]),
                 reads=["pdlt"], writes=["dlt_bf"])
            P.op("s", lambda e, gsl=gsl: e.copy(b_bf[:, gsl], pbx[:]),
                 reads=["pbx"], writes=["b_bf"])
            P.op("s", lambda e, gsl=gsl: e.copy(c_bf[:, gsl], pcx[:]),
                 reads=["pcx"], writes=["c_bf"])
    s1.close()
    P.barrier()

    s2 = ExitStack()
    E2 = s2.enter_context
    wdt_sb = E2(nc.sbuf_tensor("wdt", [64, DHALF], BF16))
    P.op("pd", lambda e: e.dma_start(out=wdt_sb[:], in_=wdt), writes=["wdt"])
    u0 = [E2(nc.sbuf_tensor(f"u0_{i}", [128, 512], F32)) for i in range(2)]
    spl = [E2(nc.sbuf_tensor(f"spl{i}", [128, 512], F32)) for i in range(2)]
    rl = [E2(nc.sbuf_tensor(f"rl{i}", [128, 512], F32)) for i in range(2)]
    dstage = [E2(nc.sbuf_tensor(f"dstage{i}", [128, T], BF16)) for i in range(2)]
    ustage = [E2(nc.sbuf_tensor(f"ustage{i}", [128, T], BF16)) for i in range(2)]
    xcre = [E2(nc.sbuf_tensor(f"xcre{i}", [128, T], BF16)) for i in range(2)]
    pdl = [E2(nc.psum_tensor(f"pdl{i}", [128, 512], F32)) for i in range(2)]
    pbr = [E2(nc.psum_tensor(f"pbr{i}", [128, 512], F32)) for i in range(2)]
    for tc in range(8):
        sl = slice(tc * 512, (tc + 1) * 512)
        ps = pbr[0]
        P.op("t", lambda e, sl=sl, ps=ps: e.matmul(
            ps[:], rb0_sb[:], b_bf[:, sl], start=True, stop=True),
            reads=["rb0", "b_bf"], writes=["pbr0"])
        P.op("v", lambda e, sl=sl, ps=ps: e.tensor_copy(brep[:, sl], ps[:]),
             reads=["pbr0"], writes=["brep"])
        ps2_ = pbr[1]
        P.op("t", lambda e, sl=sl, ps2_=ps2_: e.matmul(
            ps2_[:], rb0_sb[:], c_bf[:, sl], start=True, stop=True),
            reads=["rb0", "c_bf"], writes=["pbr1"])
        P.op("v", lambda e, sl=sl, ps2_=ps2_: e.tensor_copy(crep[:, sl], ps2_[:]),
             reads=["pbr1"], writes=["crep"])
    xlo = [E2(nc.sbuf_tensor(f"xlo{i}", [128, T], BF16)) for i in range(2)]
    xhi = [E2(nc.sbuf_tensor(f"xhi{i}", [128, T], BF16)) for i in range(2)]
    xtm = [E2(nc.sbuf_tensor(f"xtm{i}", [128, T], BF16)) for i in range(2)]

    def emit_xc_own(oc, xr, xrkey):
        i2 = oc % 2
        P.op("yd", lambda e, oc=oc, i2=i2: e.dma_start(
            out=xlo[i2][:], in_=xc_d[oc * 128:(oc + 1) * 128, :]),
            writes=[f"xlo{i2}"])
        P.op("yd", lambda e, oc=oc, i2=i2: e.dma_start(
            out=xhi[i2][:], in_=xc_d[(8 + oc) * 128:(9 + oc) * 128, :]),
            writes=[f"xhi{i2}"])
        P.op("v", lambda e, i2=i2: e.tensor_scalar(
            xtm[i2][:], xlo[i2][:], msel[:, 0:1], None, ALU.mult),
            reads=[f"xlo{i2}", "msel"], writes=[f"xtm{i2}"])
        P.op("v", lambda e, i2=i2, xr=xr: e.scalar_tensor_tensor(
            xr[:], xhi[i2][:], msel[:, 1:2], xtm[i2][:], ALU.mult, ALU.add),
            reads=[f"xhi{i2}", f"xtm{i2}", "msel"], writes=[xrkey])

    for oc in range(8):
        ds = dstage[oc % 2]
        us = ustage[oc % 2]
        xr = xcre[oc % 2]
        emit_xc_own(oc, xr, f"xcre{oc % 2}")
        for tc in range(8):
            sl = slice(tc * 512, (tc + 1) * 512)
            ps = pdl[tc % 2]
            b0, b1 = u0[tc % 2], spl[tc % 2]
            r0 = rl[tc % 2]
            P.op("t", lambda e, oc=oc, sl=sl, ps=ps: e.matmul(
                ps[:], wdt_sb[:, oc * 128:(oc + 1) * 128], dlt_bf[:, sl],
                start=True, stop=True),
                reads=["wdt", "dlt_bf"], writes=[f"pdl{tc % 2}"])
            P.op("v", lambda e, ps=ps, b0=b0, oc=oc: e.tensor_scalar(
                b0[:], ps[:], bdt_sb[:, oc:oc + 1], None, ALU.add),
                reads=[f"pdl{tc % 2}", "bdt"], writes=[f"u0_{tc % 2}"])
            P.op("s", lambda e, b0=b0, b1=b1: e.activation(b1[:], b0[:], AF.Abs),
                 reads=[f"u0_{tc % 2}"], writes=[f"spl{tc % 2}"])
            P.op("s", lambda e, b1=b1: e.activation(b1[:], b1[:], AF.Exp,
                                                    scale=-1.0),
                 reads=[f"spl{tc % 2}"], writes=[f"spl{tc % 2}"])
            P.op("s", lambda e, b1=b1: e.activation(b1[:], b1[:], AF.Ln,
                                                    bias=1.0),
                 reads=[f"spl{tc % 2}"], writes=[f"spl{tc % 2}"])
            P.op("v", lambda e, b0=b0, r0=r0: e.tensor_scalar_max(
                r0[:], b0[:], 0.0),
                reads=[f"u0_{tc % 2}"], writes=[f"rl{tc % 2}"])
            P.op("v", lambda e, ds=ds, sl=sl, r0=r0, b1=b1: e.tensor_add(
                ds[:, sl], r0[:], b1[:]),
                reads=[f"rl{tc % 2}", f"spl{tc % 2}"], writes=[f"dstage{oc % 2}"])
            P.op("v", lambda e, us=us, ds=ds, xr=xr, sl=sl: e.tensor_mul(
                us[:, sl], ds[:, sl], xr[:, sl]),
                reads=[f"dstage{oc % 2}", f"xcre{oc % 2}"],
                writes=[f"ustage{oc % 2}"])
        P.op("yd", lambda e, oc=oc, ds=ds: e.dma_start(
            out=dl_d[oc * 128:(oc + 1) * 128, :], in_=ds[:]),
            reads=[f"dstage{oc % 2}"])
        P.op("yd", lambda e, oc=oc, us=us: e.dma_start(
            out=u_d[oc * 128:(oc + 1) * 128, :], in_=us[:]),
            reads=[f"ustage{oc % 2}"])
    s2.close()
    P.barrier()

    s3 = ExitStack()
    E3 = s3.enter_context
    dch = [E3(nc.sbuf_tensor(f"dch{i}", [128, T], BF16)) for i in range(2)]
    uch = [E3(nc.sbuf_tensor(f"uch{i}", [128, T], BF16)) for i in range(2)]
    dA = [E3(nc.sbuf_tensor(f"dA{i}", [128, T], BF16)) for i in range(2)]
    db = [E3(nc.sbuf_tensor(f"db{i}", [128, T], BF16)) for i in range(2)]
    s_t = [E3(nc.sbuf_tensor(f"s_t{i}", [128, T], BF16)) for i in range(2)]
    sC = [E3(nc.sbuf_tensor(f"sC{v}", [128, T], BF16)) for v in range(4)]
    ygrp = [E3(nc.sbuf_tensor(f"ygrp{i}", [128, T], BF16)) for i in range(2)]
    prd = [E3(nc.psum_tensor("prd0", [128, 1024], F32))] * 2
    pru = [E3(nc.psum_tensor("pru0", [128, 1024], F32))] * 2
    pG = [E3(nc.psum_tensor(f"pG{i}", [128, 512], F32)) for i in range(2)]
    for tr in range(128):
        ch = tr // 16
        if tr % 16 == 0:
            P.op("yd", lambda e, ch=ch: e.dma_start(
                out=dch[ch % 2][:], in_=dl_d[ch * 128:(ch + 1) * 128, :]),
                writes=[f"dch{ch % 2}"])
            P.op("yd", lambda e, ch=ch: e.dma_start(
                out=uch[ch % 2][:], in_=u_d[ch * 128:(ch + 1) * 128, :]),
                writes=[f"uch{ch % 2}"])
        bb = tr % 2
        rsl = slice((tr % 16) * 128, (tr % 16) * 128 + 128)
        dc, uc = dch[ch % 2], uch[ch % 2]
        dck, uck = f"dch{ch % 2}", f"uch{ch % 2}"
        for q in range(4):
            for hh in range(2):
                s0 = slice(q * 1024 + hh * 512, q * 1024 + hh * 512 + 512)
                po = slice(hh * 512, hh * 512 + 512)
                P.op("t", lambda e, s0=s0, po=po, bb=bb, rsl=rsl, dc=dc:
                     e.matmul(prd[bb][:, po], r16_sb[:, rsl], dc[:, s0],
                              start=True, stop=True),
                     reads=["r16", dck], writes=[f"prd0_{hh}"])
                P.op("t", lambda e, s0=s0, po=po, bb=bb, rsl=rsl, uc=uc:
                     e.matmul(pru[bb][:, po], r16_sb[:, rsl], uc[:, s0],
                              start=True, stop=True),
                     reads=["r16", uck], writes=[f"pru0_{hh}"])
            qsl = slice(q * 1024, (q + 1) * 1024)
            P.op("s", lambda e, bb=bb, qsl=qsl, tr=tr: e.activation(
                dA[bb][:, qsl], prd[bb][:], AF.Exp,
                scale=asc_sb[:, tr:tr + 1]),
                reads=["prd0_0", "prd0_1", "asc"], writes=[f"dA{bb}"])
            P.op("v", lambda e, bb=bb, qsl=qsl: e.tensor_mul(
                db[bb][:, qsl], pru[bb][:], brep[:, qsl]),
                reads=["pru0_0", "pru0_1", "brep"], writes=[f"db{bb}"])
        P.op("v", lambda e, bb=bb: e.tensor_tensor_scan(
            s_t[bb][:], dA[bb][:], db[bb][:], 0.0, ALU.mult, ALU.add),
            reads=[f"dA{bb}", f"db{bb}"], writes=[f"s_t{bb}"])
        v = tr % 4
        P.op("p", lambda e, bb=bb, v=v: e.tensor_mul(
            sC[v][:], s_t[bb][:], crep[:]),
            reads=[f"s_t{bb}", "crep"], writes=[f"sC{v}"])
        if v == 3:
            grp = tr // 4
            yg = ygrp[grp % 2]
            for tc in range(8):
                sl = slice(tc * 512, (tc + 1) * 512)
                pg = pG[tc % 2]
                for vv in range(4):
                    P.op("t", lambda e, pg=pg, vv=vv, sl=sl: e.matmul(
                        pg[32 * vv:32 * vv + 32, :], g4_sb[:], sC[vv][:, sl],
                        start=True, stop=True, tile_position=(0, 32 * vv)),
                        reads=[f"sC{vv}", "g4"], writes=[f"pG{tc % 2}"])
                P.op("v", lambda e, yg=yg, sl=sl, pg=pg: e.tensor_copy(
                    yg[:, sl], pg[:]),
                    reads=[f"pG{tc % 2}"], writes=[f"ygrp{grp % 2}"])
            for vv in range(4):
                P.op("yd", lambda e, grp=grp, vv=vv, yg=yg: e.dma_start(
                    out=y_d[(grp * 4 + vv) * 8:(grp * 4 + vv) * 8 + 8, :],
                    in_=yg[32 * vv:32 * vv + 8, :]),
                    reads=[f"ygrp{grp % 2}"])
    s3.close()
    P.barrier()

    s4 = ExitStack()
    E4 = s4.enter_context
    ya = [E4(nc.sbuf_tensor(f"ya{i}", [128, T], BF16)) for i in range(2)]
    xa = [E4(nc.sbuf_tensor(f"xa{i}", [128, T], BF16)) for i in range(2)]
    y2s = [E4(nc.sbuf_tensor(f"y2s{i}", [128, T], BF16)) for i in range(2)]
    xlo4 = [E4(nc.sbuf_tensor(f"xlo4{i}", [128, T], BF16)) for i in range(2)]
    xtm4 = [E4(nc.sbuf_tensor(f"xtm4{i}", [128, T], BF16)) for i in range(2)]
    for oc in range(8):
        i2 = oc % 2
        P.op("yd", lambda e, oc=oc, i2=i2: e.dma_start(
            out=ya[i2][:], in_=y_d[oc * 128:(oc + 1) * 128, :]),
            writes=[f"ya{i2}"])
        P.op("yd", lambda e, oc=oc, i2=i2: e.dma_start(
            out=xlo4[i2][:], in_=xc_d[oc * 128:(oc + 1) * 128, :]),
            writes=[f"xlo4{i2}"])
        P.op("yd", lambda e, oc=oc, i2=i2: e.dma_start(
            out=xtm4[i2][:], in_=xc_d[(8 + oc) * 128:(9 + oc) * 128, :]),
            writes=[f"xtm4{i2}"])
        P.op("v", lambda e, i2=i2: e.tensor_scalar(
            xlo4[i2][:], xlo4[i2][:], msel[:, 0:1], None, ALU.mult),
            reads=[f"xlo4{i2}", "msel"], writes=[f"xlo4{i2}"])
        P.op("v", lambda e, i2=i2: e.scalar_tensor_tensor(
            xa[i2][:], xtm4[i2][:], msel[:, 1:2], xlo4[i2][:],
            ALU.mult, ALU.add),
            reads=[f"xtm4{i2}", f"xlo4{i2}", "msel"], writes=[f"xa{i2}"])
        P.op("v", lambda e, oc=oc, i2=i2: e.scalar_tensor_tensor(
            y2s[i2][:], xa[i2][:], dsc_sb[:, oc:oc + 1], ya[i2][:],
            ALU.mult, ALU.add),
            reads=[f"ya{i2}", f"xa{i2}", "dsc"], writes=[f"y2s{i2}"])
        P.op("yd", lambda e, oc=oc, i2=i2: e.dma_start(
            out=y2_o[oc * 128:(oc + 1) * 128, :], in_=y2s[i2][:]),
            reads=[f"y2s{i2}"])
    s4.close()


def own_token_idx(sub):
    idx = []
    for i in range(NTILE):
        g = 2 * i + sub
        idx.extend(range(128 * g, 128 * g + 128))
    return np.array(idx)

OWN_IDX = [own_token_idx(0), own_token_idx(1)]


def decl_LC(nc, I, O):
    return dict(
        y2g=nc.dram_tensor("y2g", [DI, T], BF16, kind=I).ap(),
        res=nc.dram_tensor("res", [DI, TLOC], BF16, kind=I).ap(),
        t1=nc.dram_tensor("t1", [D, TLOC], BF16, kind=I).ap(),
        xfm=nc.dram_tensor("xfm", [D, TLOC], F32, kind=I).ap(),
        W_out=nc.dram_tensor("W_out", [DI, D], F32, kind=I).ap(),
        ffn_wT=nc.dram_tensor("ffn_wT", [D, D], F32, kind=I).ap(),
        msel=nc.dram_tensor("msel", [128, 2], F32, kind=I).ap(),
        x2=nc.dram_tensor("x2", [D, TLOC], F32, kind=O).ap(),
        t2a=nc.dram_tensor("t2a", [D, TLOC], BF16, kind=O).ap(),
    )


def build_LC():
    P = Prog("LC")
    st = ExitStack()
    dd = decl_LC(P.nc, "ExternalInput", "ExternalOutput")
    dd["msel_shared"] = False
    body_LC(P, st, dd)
    P.fence_outputs()
    return P, st


def body_LC(P, st, dd):
    nc = NcProxy(P.nc, "lc_")
    E = st.enter_context
    y2gi, resi, t1i, xfm = dd["y2g"], dd["res"], dd["t1"], dd["xfm"]
    wout, fwt = dd["W_out"], dd["ffn_wT"]
    x2_o, t2a_o = dd["x2"], dd["t2a"]
    mselc = E(nc.sbuf_tensor("mselc", [128, 2], F32))
    P.op("yd", lambda e: e.dma_start(out=mselc[:], in_=dd["msel"]),
         writes=["mselc"])

    mix = [E(nc.sbuf_tensor(f"mix{c}", [128, TLOC], F32)) for c in range(8)]
    mixk = [f"mix{c}" for c in range(8)]

    s1 = ExitStack()
    E1 = s1.enter_context
    t1 = [E1(nc.sbuf_tensor(f"t1_{c}", [128, TLOC], BF16)) for c in range(8)]
    y3 = [E1(nc.sbuf_tensor(f"y3_{k}", [128, TLOC], BF16)) for k in range(16)]
    wo_sb = E1(nc.sbuf_tensor("wo", [128, 16 * 1024], BF16))
    ld = [E1(nc.sbuf_tensor("ld0", [128, TLOC], BF16))] * 2
    ld2 = E1(nc.sbuf_tensor("ld2", [128, TLOC], BF16))
    sr = [E1(nc.sbuf_tensor("sr0", [128, TLOC], BF16))] * 2
    pw = [E1(nc.psum_tensor(f"pw{i}", [128, 512], F32)) for i in range(2)]
    for c in range(8):
        P.op("yd", lambda e, c=c: e.dma_start(
            out=t1[c][:], in_=t1i[c * 128:(c + 1) * 128, :]), writes=[f"t1_{c}"])
    for kc in range(16):
        P.op("pd", lambda e, kc=kc: e.dma_start(
            out=wo_sb[:, kc * 1024:(kc + 1) * 1024],
            in_=wout[kc * 128:(kc + 1) * 128, :]), writes=["wo"])
        P.op("yd", lambda e, kc=kc: e.dma_start(
            out=ld[kc % 2][:], in_=resi[kc * 128:(kc + 1) * 128, :]),
            writes=["ld0"])
        P.op("s", lambda e, kc=kc: e.activation(
            sr[kc % 2][:], ld[kc % 2][:], AF.Silu),
            reads=["ld0"], writes=["sr0"])
        P.op("yd", lambda e, kc=kc: e.dma_start(
            out=ld[kc % 2][:],
            in_=y2gi[kc * 128:(kc + 1) * 128, :].rearrange(
                "p (a c) -> p a c", c=256)[:, :, 0:128],
            ),
            writes=["ld0"])
        P.op("yd", lambda e, kc=kc: e.dma_start(
            out=ld2[:],
            in_=y2gi[kc * 128:(kc + 1) * 128, :].rearrange(
                "p (a c) -> p a c", c=256)[:, :, 128:256],
            ),
            writes=["ld2"])
        P.op("v", lambda e, kc=kc: e.tensor_scalar(
            ld[kc % 2][:], ld[kc % 2][:], mselc[:, 0:1], None, ALU.mult),
            reads=["ld0", "mselc"], writes=["ld0"])
        P.op("v", lambda e, kc=kc: e.scalar_tensor_tensor(
            ld2[:], ld2[:], mselc[:, 1:2], ld[kc % 2][:], ALU.mult, ALU.add),
            reads=["ld2", "ld0", "mselc"], writes=["ld2"])
        P.op("v", lambda e, kc=kc: e.tensor_mul(
            y3[kc][:], ld2[:], sr[kc % 2][:]),
            reads=["ld2", "sr0"], writes=[f"y3_{kc}"])
    for oc in range(8):
        for tc in range(4):
            ps = pw[tc % 2]
            sl = slice(tc * 512, (tc + 1) * 512)
            for kc in range(16):
                P.op("t", lambda e, ps=ps, kc=kc, oc=oc, sl=sl: e.matmul(
                    ps[:], wo_sb[:, kc * 1024 + oc * 128:kc * 1024 + oc * 128 + 128],
                    y3[kc][:, sl], start=(kc == 0), stop=(kc == 15)),
                    reads=[f"y3_{kc}", "wo"], writes=[f"pw{tc % 2}"])
            P.op("v", lambda e, ps=ps, oc=oc, sl=sl: e.tensor_add(
                mix[oc][:, sl], ps[:], t1[oc][:, sl]),
                reads=[f"pw{tc % 2}", f"t1_{oc}"], writes=[mixk[oc]])
    s1.close()
    P.barrier()

    s2 = ExitStack()
    rows = ln_rows(P, nc, s2, "lc1", [m[:] for m in mix], mixk, TLOC)
    bc = bcast_rows(P, nc, s2, "lc1", rows, TLOC, ["r", "mr"])
    rb, rbk = bc["r"]
    mrb, mrbk = bc["mr"]
    xl = [s2.enter_context(nc.sbuf_tensor(f"xl{i}", [128, TLOC], F32))
          for i in range(2)]
    tm2 = [s2.enter_context(nc.sbuf_tensor(f"tm2_{i}", [128, TLOC], F32))
           for i in range(2)]
    for c in range(8):
        tb = tm2[c % 2]
        P.op("yd", lambda e, c=c: e.dma_start(
            out=xl[c % 2][:], in_=xfm[c * 128:(c + 1) * 128, :]),
            writes=[f"xl{c % 2}"])
        P.op("v", lambda e, c=c, tb=tb, rb=rb: e.tensor_mul(
            tb[:], mix[c][:], rb[:]),
             reads=[mixk[c], rbk], writes=[f"tm2_{c % 2}"])
        P.op("v", lambda e, c=c, tb=tb, mrb=mrb: e.tensor_sub(
            tb[:], tb[:], mrb[:]),
             reads=[f"tm2_{c % 2}", mrbk], writes=[f"tm2_{c % 2}"])
        P.op("v", lambda e, c=c, tb=tb: e.tensor_add(
            mix[c][:], tb[:], xl[c % 2][:]),
            reads=[f"tm2_{c % 2}", f"xl{c % 2}"], writes=[mixk[c]])
    s2.close()
    P.barrier()

    t2 = [E(nc.sbuf_tensor(f"t2_{c}", [128, TLOC], BF16)) for c in range(8)]
    s3a = ExitStack()
    rows = ln_rows(P, nc, s3a, "lc2", [m[:] for m in mix], mixk, TLOC)
    bc = bcast_rows(P, nc, s3a, "lc2", rows, TLOC, ["r", "mr"])
    rb, rbk = bc["r"]
    mrb, mrbk = bc["mr"]
    tmp = [s3a.enter_context(nc.sbuf_tensor(f"tmpc{i}", [128, TLOC], F32))
           for i in range(2)]
    for c in range(8):
        tb = tmp[c % 2]
        P.op("v", lambda e, c=c, tb=tb, rb=rb: e.tensor_mul(
            tb[:], mix[c][:], rb[:]),
             reads=[mixk[c], rbk], writes=[f"tmpc{c % 2}"])
        P.op("v", lambda e, c=c, tb=tb, mrb=mrb: e.tensor_sub(
            t2[c][:], tb[:], mrb[:]),
             reads=[f"tmpc{c % 2}", mrbk], writes=[f"t2_{c}"])
    s3a.close()
    P.barrier()

    s3 = ExitStack()
    E3 = s3.enter_context
    wsg = E3(nc.sbuf_tensor("wsg", [128, 8 * 1024], BF16))
    wld = [E3(nc.sbuf_tensor(f"wld{i}", [128, 1024], F32)) for i in range(2)]
    z = [E3(nc.sbuf_tensor(f"z{c}", [128, TLOC], BF16)) for c in range(8)]
    zk = [f"z{c}" for c in range(8)]
    pz = [E3(nc.psum_tensor(f"pz{i}", [128, 512], F32)) for i in range(2)]
    for kc in range(8):
        P.op("yd", lambda e, kc=kc: e.dma_start(
            out=wld[kc % 2][:], in_=fwt[kc * 128:(kc + 1) * 128, :]),
            writes=[f"wld{kc % 2}"])
        P.op("s", lambda e, kc=kc: e.sign(
            wsg[:, kc * 1024:(kc + 1) * 1024], wld[kc % 2][:]),
            reads=[f"wld{kc % 2}"], writes=["wsg"])
    for oc in range(8):
        for tc in range(4):
            ps = pz[tc % 2]
            sl = slice(tc * 512, (tc + 1) * 512)
            for kc in range(8):
                P.op("t", lambda e, ps=ps, kc=kc, oc=oc, sl=sl: e.matmul(
                    ps[:], wsg[:, kc * 1024 + oc * 128:kc * 1024 + oc * 128 + 128],
                    t2[kc][:, sl], start=(kc == 0), stop=(kc == 7)),
                    reads=[f"t2_{kc}", "wsg"], writes=[f"pz{tc % 2}"])
            P.op("v", lambda e, ps=ps, oc=oc, sl=sl: e.tensor_copy(
                z[oc][:, sl], ps[:]),
                reads=[f"pz{tc % 2}"], writes=[zk[oc]])
    rows = ln_rows(P, nc, s3, "lc3", [zz[:] for zz in z], zk, TLOC,
                   in_bf16=True)
    bc = bcast_rows(P, nc, s3, "lc3", rows, TLOC, ["r", "mr"])
    rb3, rbk3 = bc["r"]
    mrb3, mrbk3 = bc["mr"]
    for c in range(8):
        P.op("v", lambda e, c=c: e.tensor_mul(z[c][:], z[c][:], rb3[:]),
             reads=[zk[c], rbk3], writes=[zk[c]])
        P.op("v", lambda e, c=c: e.tensor_sub(z[c][:], z[c][:], mrb3[:]),
             reads=[zk[c], mrbk3], writes=[zk[c]])
        P.op("v", lambda e, c=c: e.tensor_add(mix[c][:], mix[c][:], z[c][:]),
             reads=[mixk[c], zk[c]], writes=[mixk[c]])
        P.op("yd", lambda e, c=c: e.dma_start(
            out=x2_o[c * 128:(c + 1) * 128, :], in_=mix[c][:]),
            reads=[mixk[c]])
    s3.close()
    P.barrier()

    s4 = ExitStack()
    E4 = s4.enter_context
    rows = ln_rows(P, nc, s4, "lc4", [m[:] for m in mix], mixk, TLOC)
    bc = bcast_rows(P, nc, s4, "lc4", rows, TLOC, ["r", "mr"])
    rb4, rbk4 = bc["r"]
    mrb4, mrbk4 = bc["mr"]
    tmp4 = [E4(nc.sbuf_tensor(f"tmp4_{i}", [128, TLOC], F32)) for i in range(2)]
    t2a = [E4(nc.sbuf_tensor(f"t2a{i}", [128, TLOC], BF16)) for i in range(2)]
    for c in range(8):
        tb = tmp4[c % 2]
        ta = t2a[c % 2]
        P.op("v", lambda e, c=c, tb=tb: e.tensor_mul(tb[:], mix[c][:], rb4[:]),
             reads=[mixk[c], rbk4], writes=[f"tmp4_{c % 2}"])
        P.op("v", lambda e, c=c, tb=tb, ta=ta: e.tensor_sub(ta[:], tb[:], mrb4[:]),
             reads=[f"tmp4_{c % 2}", mrbk4], writes=[f"t2a{c % 2}"])
        P.op("yd", lambda e, c=c, ta=ta: e.dma_start(
            out=t2a_o[c * 128:(c + 1) * 128, :], in_=ta[:]),
            reads=[f"t2a{c % 2}"])
    s4.close()


def decl_LD(nc, I, O):
    return dict(
        t2ag=nc.dram_tensor("t2ag", [DI, TLOC], BF16, kind=I).ap(),
        t2a_own=nc.dram_tensor("t2a_own", [D, TLOC], BF16, kind=I).ap(),
        x2=nc.dram_tensor("x2_ld", [D, TLOC], F32, kind=I).ap(),
        Wq=nc.dram_tensor("Wq", [D, D], F32, kind=I).ap(),
        Wk=nc.dram_tensor("Wk", [D, D], F32, kind=I).ap(),
        Wv=nc.dram_tensor("Wv", [D, D], F32, kind=I).ap(),
        Wo=nc.dram_tensor("Wo", [D, D], F32, kind=I).ap(),
        bq_r=nc.dram_tensor("bq_r", [128, 8], F32, kind=I).ap(),
        bo_r=nc.dram_tensor("bo_r", [128, 8], F32, kind=I).ap(),
        bv_row=nc.dram_tensor("bv_row", [1, D], F32, kind=I).ap(),
        masks=nc.dram_tensor("masks", [NTILE * MAXMASK * 128, 128], BF16,
                             kind=I).ap(),
        x3=nc.dram_tensor("x3", [D, TLOC], F32, kind=O).ap(),
        t3=nc.dram_tensor("t3", [D, TLOC], BF16, kind=O).ap(),
    )


def build_LD():
    P = Prog("LD")
    st = ExitStack()
    dd = decl_LD(P.nc, "ExternalInput", "ExternalOutput")
    body_LD(P, st, dd)
    P.fence_outputs()
    return P, st


def body_LD(P, st, dd):
    nc = NcProxy(P.nc, "ld_")
    E = st.enter_context
    t2ag = dd["t2ag"]
    t2ao = dd["t2a_own"]
    x2i = dd["x2"]
    wq_i, wk_i, wv_i, wo_i = dd["Wq"], dd["Wk"], dd["Wv"], dd["Wo"]
    bq_r, bo_r, bv_row = dd["bq_r"], dd["bo_r"], dd["bv_row"]
    maskin = dd["masks"]
    x3_o, t3_o = dd["x3"], dd["t3"]
    k_d = nc.dram_tensor("k_d", [D, T], BF16).ap()
    ve_d = nc.dram_tensor("ve_d", [T, H * 65], BF16).ap()
    nm_d = nc.dram_tensor("nm_d", [2, 8], F32).ap()

    ofm = [E(nc.sbuf_tensor(f"ofm{c}", [128, TLOC], BF16)) for c in range(8)]
    mcol = E(nc.sbuf_tensor("mcol", [128, H], F32))
    onesr = E(nc.sbuf_tensor("onesr_ld", [1, 128], F32))
    ones64 = E(nc.sbuf_tensor("ones64", [1, 64], BF16))
    s12 = ExitStack()
    qsb = [s12.enter_context(nc.sbuf_tensor(f"q{c}", [128, TLOC], BF16))
           for c in range(8)]
    msk = s12.enter_context(
        nc.sbuf_tensor("msk", [128, NTILE * MAXMASK * 128], BF16))
    P.op("v", lambda e: e.memset(onesr[:], 1.0), writes=["onesr"])
    P.op("v", lambda e: e.memset(ones64[:], 1.0), writes=["ones64"])
    P.op("yd", lambda e: e.dma_start(
        out=msk[:].rearrange("p (a b) -> p a b", b=128),
        in_=maskin.rearrange("(a p) b -> p a b", p=128)), writes=["msk"])

    s1 = ExitStack()
    E1 = s1.enter_context
    tmp21 = E1(nc.sbuf_tensor("tmp21", [2, 1], F32))
    cmax = E1(nc.sbuf_tensor("cmax", [2, 8], F32))
    qmax = E1(nc.sbuf_tensor("qmax", [2, 8], F32))
    mm8 = E1(nc.sbuf_tensor("mm8", [2, 8], F32))
    negm = E1(nc.sbuf_tensor("negm", [1, 16], F32))
    hsel = E1(nc.sbuf_tensor("hsel", [128, 2], F32))
    bqs = E1(nc.sbuf_tensor("bqs", [128, 8], F32))
    bvb = E1(nc.sbuf_tensor("bvb", [128, D], BF16))
    bvrow_f = E1(nc.sbuf_tensor("bvrow_f", [1, D], F32))
    eighth = E1(nc.sbuf_tensor("eighth", [128, 1], F32))
    t2a = [E1(nc.sbuf_tensor(f"t2a{c}", [128, T], BF16)) for c in range(8)]
    wsb = E1(nc.sbuf_tensor("wsb", [128, 8 * 1024], BF16))
    sqf = E1(nc.sbuf_tensor("sqf", [128, 512], F32))
    pk = [E1(nc.psum_tensor(f"pkx{i}", [128, 512], F32)) for i in range(2)]
    pn = E1(nc.psum_tensor("pn", [2, 512], F32))
    pvv = E1(nc.psum_tensor("pvv", [128, 1024], F32))
    pb1 = E1(nc.psum_tensor("pb1", [128, 512], F32))
    s1k = ExitStack()
    kst = [s1k.enter_context(nc.sbuf_tensor(f"kst{i}", [128, T], BF16))
           for i in range(2)]

    P.op("v", lambda e: e.memset(cmax[:], 0.0), writes=["cmax"])
    P.op("v", lambda e: e.memset(qmax[:], 0.0), writes=["qmax"])
    P.op("v", lambda e: e.memset(hsel[:], 0.0), writes=["hsel"])
    P.op("v", lambda e: e.memset(hsel[0:64, 0:1], 1.0), writes=["hsel"])
    P.op("v", lambda e: e.memset(hsel[64:128, 1:2], 1.0), writes=["hsel"])
    P.op("v", lambda e: e.memset(eighth[:], 0.125), writes=["eighth"])
    P.op("yd", lambda e: e.dma_start(out=bqs[:], in_=bq_r), writes=["bqs"])
    P.op("s", lambda e: e.mul(bqs[:], bqs[:], 0.125), reads=["bqs"],
         writes=["bqs"])
    P.op("yd", lambda e: e.dma_start(out=bvrow_f[:], in_=bv_row),
         writes=["bvrow"])
    for c in range(8):
        for rk in range(2):
            P.op("yd", lambda e, c=c, rk=rk: e.dma_start(
                out=t2a[c][:].rearrange("p (a c2) -> p a c2", c2=256)[
                    :, :, rk * 128:(rk + 1) * 128],
                in_=t2ag[rk * D + c * 128:rk * D + (c + 1) * 128, :].rearrange(
                    "p (a b) -> p a b", b=128)),
                writes=[f"t2a{c}"])
    for nn in range(2):
        P.op("t", lambda e, nn=nn: e.matmul(
            pb1[:], onesr[:], bvrow_f[:, nn * 512:(nn + 1) * 512],
            start=True, stop=True),
            reads=["bvrow", "onesr"], writes=["pb1"])
        P.op("s", lambda e, nn=nn: e.copy(bvb[:, nn * 512:(nn + 1) * 512],
                                          pb1[:]),
             reads=["pb1"], writes=["bvb"])

    def load_w(src):
        for kc in range(8):
            P.op("pd", lambda e, kc=kc, src=src: e.dma_start(
                out=wsb[:, kc * 1024:(kc + 1) * 1024],
                in_=src[kc * 128:(kc + 1) * 128, :]), writes=["wsb"])

    load_w(wk_i)
    for oc in range(8):
        sg = kst[oc % 2]
        for tc in range(8):
            ps = pk[tc % 2]
            sl = slice(tc * 512, (tc + 1) * 512)
            for kc in range(8):
                P.op("t", lambda e, ps=ps, kc=kc, oc=oc, sl=sl: e.matmul(
                    ps[:], wsb[:, kc * 1024 + oc * 128:kc * 1024 + oc * 128 + 128],
                    t2a[kc][:, sl], start=(kc == 0), stop=(kc == 7)),
                    reads=[f"t2a{kc}", "wsb"], writes=[f"pkx{tc % 2}"])
            P.op("s", lambda e, sg=sg, sl=sl, ps=ps: e.copy(sg[:, sl], ps[:]),
                 reads=[f"pkx{tc % 2}"], writes=[f"kst{oc % 2}"])
        P.op("yd", lambda e, oc=oc, sg=sg: e.dma_start(
            out=k_d[oc * 128:(oc + 1) * 128, :], in_=sg[:]),
            reads=[f"kst{oc % 2}"])
        for tc in range(8):
            sl = slice(tc * 512, (tc + 1) * 512)
            P.op("s", lambda e, sg=sg, sl=sl: e.activation(
                sqf[:], sg[:, sl], AF.Square),
                reads=[f"kst{oc % 2}"], writes=["sqf"])
            P.op("t", lambda e: e.matmul(
                pn[:], hsel[:], sqf[:], start=True, stop=True),
                reads=["sqf", "hsel"], writes=["pn"])
            P.op("v", lambda e: e.reduce_max(
                tmp21[:], pn[:], axis=mybir.AxisListType.X),
                reads=["pn"], writes=["tmp21"])
            P.op("v", lambda e, oc=oc: e.tensor_max(
                cmax[:, oc:oc + 1], cmax[:, oc:oc + 1], tmp21[:]),
                reads=["tmp21", "cmax"], writes=["cmax"])

    s1k.close()
    P.barrier()
    s1v = ExitStack()
    stg = [s1v.enter_context(nc.sbuf_tensor(f"stg{i}", [128, 2080], BF16))
           for i in range(2)]
    load_w(wv_i)
    for tk in range(32):
        vst = stg[tk % 2]
        for kc in range(8):
            for nn in range(2):
                P.op("t", lambda e, tk=tk, kc=kc, nn=nn: e.matmul(
                    pvv[:, nn * 512:(nn + 1) * 512],
                    t2a[kc][:, tk * 128:(tk + 1) * 128],
                    wsb[:, kc * 1024 + nn * 512:kc * 1024 + (nn + 1) * 512],
                    start=(kc == 0), stop=(kc == 7)),
                    reads=[f"t2a{kc}", "wsb"], writes=[f"pvv{nn}"])
        vview = vst[:, 0:1040].rearrange("p (a b) -> p a b", b=65)
        P.op("v", lambda e, vview=vview: e.tensor_add(
            vview[:, :, 0:64],
            pvv[:].rearrange("p (a b) -> p a b", b=64),
            bvb[:].rearrange("p (a b) -> p a b", b=64)),
            reads=["pvv0", "pvv1", "bvb"], writes=[f"stg{tk % 2}"])
        P.op("v", lambda e, vview=vview: e.memset(vview[:, :, 64:65], 1.0),
             reads=[], writes=[f"stg{tk % 2}"])
        P.op("yd", lambda e, tk=tk, vst=vst: e.dma_start(
            out=ve_d[tk * 128:(tk + 1) * 128, :], in_=vst[:, 0:1040]),
            reads=[f"stg{tk % 2}"])

    s1v.close()
    P.barrier()
    sqfq = E1(nc.sbuf_tensor("sqfq", [128, 512], F32))
    pnq = E1(nc.psum_tensor("pnq", [2, 512], F32))
    tmp21q = E1(nc.sbuf_tensor("tmp21q", [2, 1], F32))
    qt2 = [E1(nc.sbuf_tensor(f"qt2{c}", [128, TLOC], BF16)) for c in range(8)]
    for c in range(8):
        P.op("yd", lambda e, c=c: e.dma_start(
            out=qt2[c][:], in_=t2ao[c * 128:(c + 1) * 128, :]),
            writes=[f"qt2{c}"])
    load_w(wq_i)
    for oc in range(8):
        for tc in range(4):
            ps = pk[tc % 2]
            sl = slice(tc * 512, (tc + 1) * 512)
            for kc in range(8):
                P.op("t", lambda e, ps=ps, kc=kc, oc=oc, sl=sl: e.matmul(
                    ps[:], wsb[:, kc * 1024 + oc * 128:kc * 1024 + oc * 128 + 128],
                    qt2[kc][:, sl], start=(kc == 0), stop=(kc == 7)),
                    reads=[f"qt2{kc}", "wsb"], writes=[f"pkx{tc % 2}"])
            P.op("v", lambda e, oc=oc, sl=sl, ps=ps: e.tensor_scalar(
                qsb[oc][:, sl], ps[:], eighth[:, 0:1], bqs[:, oc:oc + 1],
                ALU.mult, ALU.add),
                reads=[f"pkx{tc % 2}", "eighth", "bqs"], writes=[f"q{oc}"])
        for tc in range(4):
            sl = slice(tc * 512, (tc + 1) * 512)
            P.op("s", lambda e, oc=oc, sl=sl: e.activation(
                sqfq[:], qsb[oc][:, sl], AF.Square),
                reads=[f"q{oc}"], writes=["sqfq"])
            P.op("t", lambda e: e.matmul(
                pnq[:], hsel[:], sqfq[:], start=True, stop=True),
                reads=["sqfq", "hsel"], writes=["pnq"])
            P.op("v", lambda e: e.reduce_max(
                tmp21q[:], pnq[:], axis=mybir.AxisListType.X),
                reads=["pnq"], writes=["tmp21q"])
            P.op("v", lambda e, oc=oc: e.tensor_max(
                qmax[:, oc:oc + 1], qmax[:, oc:oc + 1], tmp21q[:]),
                reads=["tmp21q", "qmax"], writes=["qmax"])

    P.op("v", lambda e: e.tensor_mul(mm8[:], cmax[:], qmax[:]),
         reads=["cmax", "qmax"], writes=["mm8"])
    P.op("s", lambda e: e.activation(mm8[:], mm8[:], AF.Sqrt),
         reads=["mm8"], writes=["mm8"])
    P.op("v", lambda e: e.tensor_scalar_mul(mm8[:], mm8[:], -1.0),
         reads=["mm8"], writes=["mm8"])
    P.op("yd", lambda e: e.dma_start(out=nm_d, in_=mm8[:]),
         reads=["mm8"], writes=["nm_d"])
    P.op("yd", lambda e: e.dma_start(
        out=negm[:].rearrange("q (a b) -> q a b", b=2),
        in_=nm_d.rearrange("(q b) a -> q a b", q=1),
        allow_slow_non_contiguous=True),
        reads=["nm_d"], writes=["negm"])
    for h in range(H):
        P.op("t", lambda e, h=h: e.matmul(
            pb1[:, 0:1], onesr[:], negm[:, h:h + 1], start=True, stop=True),
            reads=["negm", "onesr"], writes=["pb1"])
        P.op("v", lambda e, h=h: e.tensor_copy(mcol[:, h:h + 1], pb1[:, 0:1]),
             reads=["pb1"], writes=["mcol"])
    s1.close()
    P.barrier()

    s2 = ExitStack()
    E2 = s2.enter_context
    khb = [E2(nc.sbuf_tensor(f"khb{i}", [128, T], BF16)) for i in range(2)]
    veh = [E2(nc.sbuf_tensor(f"veh{i}", [128, 32 * 65], BF16)) for i in range(2)]
    ptb = [E2(nc.sbuf_tensor(f"ptb{i}", [128, 32 * 128], BF16)) for i in range(2)]
    rs_sb = [E2(nc.sbuf_tensor(f"rs{i}", [1, 128], F32)) for i in range(2)]
    obf = [E2(nc.sbuf_tensor(f"obf{i}", [64, 128], BF16)) for i in range(2)]
    psS = [E2(nc.psum_tensor(f"psS{i}", [128, 512], F32)) for i in range(2)]
    pO = [E2(nc.psum_tensor(f"pO{i}", [65, 512], F32)) for i in range(2)]
    prs = E2(nc.psum_tensor("prs", [64, 512], F32))
    for h in range(H):
        hb = h % 2
        hc = h // 2
        kh = khb[h % 2]
        ve = veh[h % 2]
        P.op("yd", lambda e, h=h, kh=kh, hb=hb: e.dma_start(
            out=kh[hb * 64:hb * 64 + 64, :],
            in_=k_d[h * 64:h * 64 + 64, :]), writes=[f"khb{h % 2}"])
        P.op("yd", lambda e, h=h, ve=ve: e.dma_start(
            out=ve[:].rearrange("p (a b) -> p a b", b=65),
            in_=ve_d.rearrange("(a p) o -> p a o", p=128)[
                :, :, h * 65:(h + 1) * 65]),
            writes=[f"veh{h % 2}"])
        for qt in range(NTILE):
            nkt = N_KT[qt]
            pss = psS[qt % 2]
            pt = ptb[qt % 2]
            ptk = f"ptb{qt % 2}"
            qv = qsb[hc][hb * 64:hb * 64 + 64, qt * 128:(qt + 1) * 128]
            for kt in range(nkt):
                qq = kt % 4
                P.op("t", lambda e, pss=pss, kh=kh, hb=hb, kt=kt, qq=qq,
                     qv=qv: e.matmul(
                         pss[:, qq * 128:(qq + 1) * 128],
                         kh[hb * 64:hb * 64 + 64, kt * 128:(kt + 1) * 128],
                         qv, start=True, stop=True),
                     reads=[f"khb{h % 2}", f"q{hc}"],
                     writes=[f"psS{qt % 2}_{qq}"])
                j = kt - FFA[qt]
                if j >= 0:
                    moff = (qt * MAXMASK + j) * 128
                    P.op("v", lambda e, pss=pss, qq=qq, moff=moff: e.tensor_add(
                        pss[:, qq * 128:(qq + 1) * 128],
                        pss[:, qq * 128:(qq + 1) * 128],
                        msk[:, moff:moff + 128]),
                        reads=[f"psS{qt % 2}_{qq}", "msk"],
                        writes=[f"psS{qt % 2}_{qq}"])
                if qq == 3 or kt == nkt - 1:
                    g0 = (kt // 4) * 4
                    w = (qq + 1) * 128
                    rd = [f"psS{qt % 2}_{x}" for x in range(qq + 1)] + ["mcol"]
                    P.op("s", lambda e, pss=pss, pt=pt, g0=g0, w=w, h=h:
                         e.activation(pt[:, g0 * 128:g0 * 128 + w],
                                      pss[:, 0:w], AF.Exp,
                                      bias=mcol[:, h:h + 1]),
                         reads=rd, writes=[ptk])
            po = pO[qt % 2]
            for kt in range(nkt):
                P.op("t", lambda e, po=po, ve=ve, pt=pt, kt=kt,
                     nkt=nkt: e.matmul(
                         po[:, 0:128], ve[:, kt * 65:(kt + 1) * 65],
                         pt[:, kt * 128:(kt + 1) * 128],
                         start=(kt == 0), stop=(kt == nkt - 1)),
                     reads=[f"veh{h % 2}", ptk], writes=[f"pO{qt % 2}"])
            rs = rs_sb[qt % 2]
            ob = obf[qt % 2]
            P.op("v", lambda e, po=po, rs=rs: e.reciprocal(
                rs[:], po[64:65, 0:128]),
                reads=[f"pO{qt % 2}"], writes=[f"rs{qt % 2}"])
            P.op("t", lambda e, rs=rs: e.matmul(
                prs[:, 0:128], onesr[0:1, 0:64], rs[:], start=True, stop=True),
                reads=[f"rs{qt % 2}", "onesr"], writes=["prs"])
            P.op("s", lambda e, po=po, ob=ob: e.copy(ob[:], po[0:64, 0:128]),
                 reads=[f"pO{qt % 2}"], writes=[f"obf{qt % 2}"])
            P.op("v", lambda e, ob=ob, hc=hc, hb=hb, qt=qt: e.tensor_mul(
                ofm[hc][hb * 64:hb * 64 + 64, qt * 128:(qt + 1) * 128],
                ob[:], prs[0:64, 0:128]),
                reads=[f"obf{qt % 2}", "prs"], writes=[f"ofm{hc}"])
    s2.close()
    s12.close()
    P.barrier()

    s3 = ExitStack()
    E3 = s3.enter_context
    ao = [E3(nc.sbuf_tensor(f"ao{c}", [128, TLOC], F32)) for c in range(8)]
    aok = [f"ao{c}" for c in range(8)]
    wsb3 = E3(nc.sbuf_tensor("wsb3", [128, 8 * 1024], BF16))
    bo_sb = E3(nc.sbuf_tensor("bo_sb", [128, 8], F32))
    pz = [E3(nc.psum_tensor(f"pz3{i}", [128, 512], F32)) for i in range(2)]
    P.op("yd", lambda e: e.dma_start(out=bo_sb[:], in_=bo_r), writes=["bo"])
    for kc in range(8):
        P.op("pd", lambda e, kc=kc: e.dma_start(
            out=wsb3[:, kc * 1024:(kc + 1) * 1024],
            in_=wo_i[kc * 128:(kc + 1) * 128, :]), writes=["wsb3"])
    for oc in range(8):
        for tc in range(4):
            ps = pz[tc % 2]
            sl = slice(tc * 512, (tc + 1) * 512)
            for kc in range(8):
                P.op("t", lambda e, ps=ps, kc=kc, oc=oc, sl=sl: e.matmul(
                    ps[:], wsb3[:, kc * 1024 + oc * 128:kc * 1024 + oc * 128 + 128],
                    ofm[kc][:, sl], start=(kc == 0), stop=(kc == 7)),
                    reads=[f"ofm{kc}", "wsb3"], writes=[f"pz3{tc % 2}"])
            P.op("v", lambda e, ps=ps, oc=oc, sl=sl: e.tensor_scalar(
                ao[oc][:, sl], ps[:], bo_sb[:, oc:oc + 1], None, ALU.add),
                reads=[f"pz3{tc % 2}", "bo"], writes=[aok[oc]])
    x2l = [E3(nc.sbuf_tensor(f"x2l{i}", [128, TLOC], F32)) for i in range(2)]
    tm3 = [E3(nc.sbuf_tensor(f"tm3_{i}", [128, TLOC], F32)) for i in range(2)]
    s3x = ExitStack()
    rows = ln_rows(P, nc, s3x, "ld1", [a[:] for a in ao], aok, TLOC)
    bc = bcast_rows(P, nc, s3x, "ld1", rows, TLOC, ["r", "mr"])
    rb1, rbk1 = bc["r"]
    mrb1, mrbk1 = bc["mr"]
    for c in range(8):
        tb = tm3[c % 2]
        P.op("yd", lambda e, c=c: e.dma_start(
            out=x2l[c % 2][:], in_=x2i[c * 128:(c + 1) * 128, :]),
            writes=[f"x2l{c % 2}"])
        P.op("v", lambda e, c=c, tb=tb, rb1=rb1: e.tensor_mul(
            tb[:], ao[c][:], rb1[:]),
            reads=[aok[c], rbk1], writes=[f"tm3_{c % 2}"])
        P.op("v", lambda e, c=c, tb=tb, mrb1=mrb1: e.tensor_sub(
            tb[:], tb[:], mrb1[:]),
            reads=[f"tm3_{c % 2}", mrbk1], writes=[f"tm3_{c % 2}"])
        P.op("v", lambda e, c=c, tb=tb: e.tensor_add(
            ao[c][:], tb[:], x2l[c % 2][:]),
            reads=[f"tm3_{c % 2}", f"x2l{c % 2}"], writes=[aok[c]])
        P.op("yd", lambda e, c=c: e.dma_start(
            out=x3_o[c * 128:(c + 1) * 128, :], in_=ao[c][:]),
            reads=[aok[c]])
    s3x.close()
    P.barrier()
    s3x = ExitStack()
    rows = ln_rows(P, nc, s3x, "ld2", [a[:] for a in ao], aok, TLOC)
    bc = bcast_rows(P, nc, s3x, "ld2", rows, TLOC, ["r", "mr"])
    rb2, rbk2 = bc["r"]
    mrb2, mrbk2 = bc["mr"]
    t3s = [s3x.enter_context(nc.sbuf_tensor(f"t3s{i}", [128, TLOC], BF16))
           for i in range(2)]
    for c in range(8):
        tb = tm3[c % 2]
        t3b = t3s[c % 2]
        P.op("v", lambda e, c=c, tb=tb, rb2=rb2: e.tensor_mul(
            tb[:], ao[c][:], rb2[:]),
            reads=[aok[c], rbk2], writes=[f"tm3_{c % 2}"])
        P.op("v", lambda e, c=c, tb=tb, t3b=t3b, mrb2=mrb2: e.tensor_sub(
            t3b[:], tb[:], mrb2[:]),
            reads=[f"tm3_{c % 2}", mrbk2], writes=[f"t3s{c % 2}"])
        P.op("yd", lambda e, c=c, t3b=t3b: e.dma_start(
            out=t3_o[c * 128:(c + 1) * 128, :], in_=t3b[:]),
            reads=[f"t3s{c % 2}"])
    s3x.close()
    s3.close()


def ld_in_maps(ins, t2a_cores, x2_cores):
    Wq = np.ascontiguousarray(ins["L1_attn_Wq"]).astype(np.float32)
    Wk = np.ascontiguousarray(ins["L1_attn_Wk"]).astype(np.float32)
    Wv = np.ascontiguousarray(ins["L1_attn_Wv"]).astype(np.float32)
    Wo = np.ascontiguousarray(ins["L1_attn_Wo"]).astype(np.float32)
    bq = np.asarray(ins["L1_attn_bq"], np.float32)
    bv = np.asarray(ins["L1_attn_bv"], np.float32)
    bo = np.asarray(ins["L1_attn_bo"], np.float32)
    maps = []
    for c in range(NCHIP):
        b, sub = c // 2, c % 2
        t2ag = np.concatenate([t2a_cores[2 * b], t2a_cores[2 * b + 1]], axis=0)
        masks = np.zeros((NTILE, MAXMASK, 128, 128), np.float32)
        for i in range(NTILE):
            g = 2 * i + sub
            qpos = 128 * g + np.arange(128)
            bnd = np.array([bound_of(int(q)) for q in qpos])
            for j in range(N_KT[i] - FFA[i]):
                kt = FFA[i] + j
                kpos = 128 * kt + np.arange(128)
                masks[i, j] = np.where(kpos[:, None] <= bnd[None, :],
                                       0.0, NEG)
        maps.append({
            "t2ag": np.ascontiguousarray(t2ag), "t2a_own": t2a_cores[c],
            "x2_ld": x2_cores[c],
            "Wq": Wq, "Wk": Wk, "Wv": Wv, "Wo": Wo,
            "bq_r": np.ascontiguousarray(bq.reshape(8, 128).T),
            "bo_r": np.ascontiguousarray(bo.reshape(8, 128).T),
            "bv_row": bv.reshape(1, D),
            "masks": np.ascontiguousarray(
                masks.reshape(NTILE * MAXMASK * 128, 128)).astype(BF),
        })
    return maps



def decl_LE(nc, I, O):
    return dict(
        x3=nc.dram_tensor("x3_le", [D, TLOC], F32, kind=I).ap(),
        t3=nc.dram_tensor("t3_le", [D, TLOC], BF16, kind=I).ap(),
        w1T=nc.dram_tensor("w1T", [D, 4096], F32, kind=I).ap(),
        w2T=nc.dram_tensor("w2T", [D, 4096], F32, kind=I).ap(),
        h1T=nc.dram_tensor("h1T", [4096, 4096], F32, kind=I).ap(),
        h2T=nc.dram_tensor("h2T", [4096, 4096], F32, kind=I).ap(),
        w3T=nc.dram_tensor("w3T", [4096, D], F32, kind=I).ap(),
        s1_r=nc.dram_tensor("s1_r", [128, 32], F32, kind=I).ap(),
        s2_r=nc.dram_tensor("s2_r", [128, 32], F32, kind=I).ap(),
        h1s_r=nc.dram_tensor("h1s_r", [128, 32], F32, kind=I).ap(),
        h2s_r=nc.dram_tensor("h2s_r", [128, 32], F32, kind=I).ap(),
        w3s_r=nc.dram_tensor("w3s_r", [128, 8], F32, kind=I).ap(),
        x4=nc.dram_tensor("x4", [D, TLOC], F32, kind=O).ap(),
    )


def build_LE():
    P = Prog("LE")
    st = ExitStack()
    dd = decl_LE(P.nc, "ExternalInput", "ExternalOutput")
    body_LE(P, st, dd)
    P.fence_outputs()
    return P, st


def body_LE(P, st, dd):
    nc = NcProxy(P.nc, "le_")
    E = st.enter_context
    x3i, t3i = dd["x3"], dd["t3"]
    w1t, w2t = dd["w1T"], dd["w2T"]
    h1t, h2t, w3t = dd["h1T"], dd["h2T"], dd["w3T"]
    s1r, s2r, h1sr, h2sr, w3sr = (dd["s1_r"], dd["s2_r"], dd["h1s_r"],
                                  dd["h2s_r"], dd["w3s_r"])
    x4_o = dd["x4"]

    t3 = [E(nc.sbuf_tensor(f"t3_{c}", [128, TLOC], BF16)) for c in range(8)]
    sc = {}
    for nm, ap_ in [("s1", s1r), ("s2", s2r), ("h1s", h1sr), ("h2s", h2sr),
                    ("w3s", w3sr)]:
        tsb = E(nc.sbuf_tensor("sc_" + nm, [128, ap_.shape[1]], F32))
        P.op("yd", lambda e, tsb=tsb, ap_=ap_: e.dma_start(out=tsb[:], in_=ap_),
             writes=["sc" + nm])
        sc[nm] = tsb
    for c in range(8):
        P.op("yd", lambda e, c=c: e.dma_start(
            out=t3[c][:], in_=t3i[c * 128:(c + 1) * 128, :]),
            writes=[f"t3_{c}"])

    TH = 1024
    for th in range(2):
        sTh = ExitStack()
        ETh = sTh.enter_context
        tsl = slice(th * TH, (th + 1) * TH)
        g = [ETh(nc.sbuf_tensor(f"g{th}_{k}", [128, TH], BF16))
             for k in range(32)]
        gk = [f"g{th}_{k}" for k in range(32)]
        wp = [ETh(nc.sbuf_tensor(f"wp{th}_{i}", [128, 32 * 128], BF16))
              for i in range(2)]
        ps2 = [ETh(nc.psum_tensor(f"ps{th}_{i}", [128, 512], F32))
               for i in range(2)]
        sU = ExitStack()
        u = [sU.enter_context(nc.sbuf_tensor(f"u{th}_{k}", [128, TH], BF16))
             for k in range(32)]
        uk = [f"u{th}_{k}" for k in range(32)]

        sA = ExitStack()
        a = [sA.enter_context(nc.sbuf_tensor(f"a{th}_{k}", [128, TH], BF16))
             for k in range(2)]
        for oc in range(32):
            ar = a[oc % 2]
            akey = f"a{th}_{oc % 2}"
            for wi, wt_ in ((0, w1t), (1, w2t)):
                w = wp[wi]
                wkey = f"wp{th}_{wi}"
                P.op("pd", lambda e, w=w, oc=oc, wt_=wt_: e.dma_start(
                    out=w[:, 0:8 * 128].rearrange("p (a b) -> p a b", b=128),
                    in_=wt_.rearrange("(a p) o -> p a o", p=128)[
                        :, :, oc * 128:(oc + 1) * 128]), writes=[wkey])
                for tc in range(2):
                    ps = ps2[tc % 2]
                    sl = slice(th * TH + tc * 512, th * TH + (tc + 1) * 512)
                    ol = slice(tc * 512, (tc + 1) * 512)
                    for kc in range(8):
                        P.op("t", lambda e, ps=ps, w=w, kc=kc, sl=sl: e.matmul(
                            ps[:], w[:, kc * 128:(kc + 1) * 128],
                            t3[kc][:, sl], start=(kc == 0), stop=(kc == 7)),
                            reads=[wkey, f"t3_{kc}"],
                            writes=[f"ps{th}_{tc % 2}"])
                    if wi == 0:
                        P.op("s", lambda e, ps=ps, oc=oc, ol=ol, ar=ar:
                             e.activation(ar[:, ol], ps[:], AF.Silu,
                                          scale=sc["s1"][:, oc:oc + 1]),
                             reads=[f"ps{th}_{tc % 2}", "scs1"], writes=[akey])
                    else:
                        P.op("v", lambda e, ps=ps, oc=oc, ol=ol, ar=ar:
                             e.scalar_tensor_tensor(
                                 g[oc][:, ol], ps[:], sc["s2"][:, oc:oc + 1],
                                 ar[:, ol], ALU.mult, ALU.mult),
                             reads=[f"ps{th}_{tc % 2}", "scs2", akey],
                             writes=[gk[oc]])
        sA.close()
        P.barrier()
        for oc in range(32):
            w = wp[oc % 2]
            wkey = f"wp{th}_{oc % 2}"
            P.op("pd", lambda e, w=w, oc=oc: e.dma_start(
                out=w[:].rearrange("p (a b) -> p a b", b=128),
                in_=h1t.rearrange("(a p) o -> p a o", p=128)[
                    :, :, oc * 128:(oc + 1) * 128]), writes=[wkey])
            for tc in range(2):
                ps = ps2[tc % 2]
                ol = slice(tc * 512, (tc + 1) * 512)
                for kc in range(32):
                    P.op("t", lambda e, ps=ps, w=w, kc=kc, ol=ol: e.matmul(
                        ps[:], w[:, kc * 128:(kc + 1) * 128],
                        g[kc][:, ol], start=(kc == 0), stop=(kc == 31)),
                        reads=[wkey, gk[kc]], writes=[f"ps{th}_{tc % 2}"])
                P.op("s", lambda e, ps=ps, oc=oc, ol=ol: e.activation(
                    u[oc][:, ol], ps[:], AF.Gelu, scale=sc["h1s"][:, oc:oc + 1]),
                    reads=[f"ps{th}_{tc % 2}", "sch1s"], writes=[uk[oc]])
        for oc in range(32):
            w = wp[oc % 2]
            wkey = f"wp{th}_{oc % 2}"
            P.op("pd", lambda e, w=w, oc=oc: e.dma_start(
                out=w[:].rearrange("p (a b) -> p a b", b=128),
                in_=h2t.rearrange("(a p) o -> p a o", p=128)[
                    :, :, oc * 128:(oc + 1) * 128]), writes=[wkey])
            for tc in range(2):
                ps = ps2[tc % 2]
                ol = slice(tc * 512, (tc + 1) * 512)
                for kc in range(32):
                    P.op("t", lambda e, ps=ps, w=w, kc=kc, ol=ol: e.matmul(
                        ps[:], w[:, kc * 128:(kc + 1) * 128],
                        u[kc][:, ol], start=(kc == 0), stop=(kc == 31)),
                        reads=[wkey, uk[kc]], writes=[f"ps{th}_{tc % 2}"])
                P.op("s", lambda e, ps=ps, oc=oc, ol=ol: e.activation(
                    g[oc][:, ol], ps[:], AF.Gelu, scale=sc["h2s"][:, oc:oc + 1]),
                    reads=[f"ps{th}_{tc % 2}", "sch2s"], writes=[gk[oc]])
        sU.close()
        P.barrier()
        sZ = ExitStack()
        z3 = [sZ.enter_context(nc.sbuf_tensor(f"z3{th}_{c}", [128, TH], BF16))
              for c in range(8)]
        zk = [f"z3{th}_{c}" for c in range(8)]
        for oc in range(8):
            w = wp[oc % 2]
            wkey = f"wp{th}_{oc % 2}"
            P.op("pd", lambda e, w=w, oc=oc: e.dma_start(
                out=w[:].rearrange("p (a b) -> p a b", b=128),
                in_=w3t.rearrange("(a p) o -> p a o", p=128)[
                    :, :, oc * 128:(oc + 1) * 128]), writes=[wkey])
            for tc in range(2):
                ps = ps2[tc % 2]
                ol = slice(tc * 512, (tc + 1) * 512)
                for kc in range(32):
                    P.op("t", lambda e, ps=ps, w=w, kc=kc, ol=ol: e.matmul(
                        ps[:], w[:, kc * 128:(kc + 1) * 128],
                        g[kc][:, ol], start=(kc == 0), stop=(kc == 31)),
                        reads=[wkey, gk[kc]], writes=[f"ps{th}_{tc % 2}"])
                P.op("v", lambda e, ps=ps, oc=oc, ol=ol: e.tensor_scalar(
                    z3[oc][:, ol], ps[:], sc["w3s"][:, oc:oc + 1], None,
                    ALU.mult),
                    reads=[f"ps{th}_{tc % 2}", "scw3s"], writes=[zk[oc]])
        rows = ln_rows(P, nc, sZ, f"le{th}", [z[:] for z in z3], zk, TH,
                       in_bf16=True)
        bc = bcast_rows(P, nc, sZ, f"le{th}", rows, TH, ["r", "mr"])
        rbz, rbkz = bc["r"]
        mrbz, mrbkz = bc["mr"]
        x3l = [sZ.enter_context(nc.sbuf_tensor(f"x3l{th}_{i}", [128, TH], F32))
               for i in range(2)]
        x4s = [sZ.enter_context(nc.sbuf_tensor(f"x4s{th}_{i}", [128, TH], F32))
               for i in range(2)]
        for c in range(8):
            xb = x3l[c % 2]
            ob = x4s[c % 2]
            P.op("yd", lambda e, c=c, xb=xb, tsl=tsl: e.dma_start(
                out=xb[:], in_=x3i[c * 128:(c + 1) * 128, tsl]),
                writes=[f"x3l{th}_{c % 2}"])
            P.op("v", lambda e, c=c, rbz=rbz: e.tensor_mul(
                z3[c][:], z3[c][:], rbz[:]),
                reads=[zk[c], rbkz], writes=[zk[c]])
            P.op("v", lambda e, c=c, mrbz=mrbz: e.tensor_sub(
                z3[c][:], z3[c][:], mrbz[:]),
                reads=[zk[c], mrbkz], writes=[zk[c]])
            P.op("v", lambda e, c=c, xb=xb, ob=ob: e.tensor_add(
                ob[:], z3[c][:], xb[:]),
                reads=[zk[c], f"x3l{th}_{c % 2}"], writes=[f"x4s{th}_{c % 2}"])
            P.op("yd", lambda e, c=c, ob=ob, tsl=tsl: e.dma_start(
                out=x4_o[c * 128:(c + 1) * 128, tsl], in_=ob[:]),
                reads=[f"x4s{th}_{c % 2}"])
        sZ.close()
        sTh.close()
        P.barrier()


def le_in_maps(ins, x3_cores, t3_cores):
    w1t = np.ascontiguousarray(np.asarray(ins["L1_ffn_w1_w"], np.float32).T)
    w2t = np.ascontiguousarray(np.asarray(ins["L1_ffn_w2_w"], np.float32).T)
    h1t = np.ascontiguousarray(np.asarray(ins["L1_ffn_h1_w"], np.float32).T)
    h2t = np.ascontiguousarray(np.asarray(ins["L1_ffn_h2_w"], np.float32).T)
    w3t = np.ascontiguousarray(np.asarray(ins["L1_ffn_w3_w"], np.float32).T)
    rs = lambda v, nch: np.ascontiguousarray(
        np.asarray(v, np.float32).reshape(nch, 128).T)
    common = {
        "w1T": w1t, "w2T": w2t, "h1T": h1t, "h2T": h2t, "w3T": w3t,
        "s1_r": rs(ins["L1_ffn_w1_s"], 32), "s2_r": rs(ins["L1_ffn_w2_s"], 32),
        "h1s_r": rs(ins["L1_ffn_h1_s"], 32), "h2s_r": rs(ins["L1_ffn_h2_s"], 32),
        "w3s_r": rs(ins["L1_ffn_w3_s"], 8),
    }
    maps = []
    for c in range(NCHIP):
        m = dict(common)
        m["x3_le"] = x3_cores[c]
        m["t3_le"] = t3_cores[c]
        maps.append(m)
    return maps



def la_in_maps(ins):
    x = ins["L0_attn_hidden_states"]
    nw_r = np.ascontiguousarray(
        ins["L0_attn_norm_w"].reshape(8, 128).T).astype(np.float32)
    W_in = np.ascontiguousarray(ins["L0_attn_W_in"]).astype(np.float32)
    maps = []
    for c in range(NCHIP):
        b, sub = c // 2, c % 2
        xfm = np.ascontiguousarray(np.asarray(x[b])[OWN_IDX[sub]].T,
                                   dtype=np.float32)
        maps.append({"xfm": xfm, "norm_w_r": nw_r, "W_in": W_in})
    return maps


def lb_perm(sub):
    h0 = np.arange(sub * DHALF, (sub + 1) * DHALF)
    h1 = np.arange((1 - sub) * DHALF, (2 - sub) * DHALF)
    return np.concatenate([h0, h1])


def lb_in_maps(ins, xin_cores):
    conv_w = np.asarray(ins["L0_attn_conv_w"], np.float32)
    conv_b = np.asarray(ins["L0_attn_conv_b"], np.float32)
    W_x = np.asarray(ins["L0_attn_W_x"], np.float32)
    W_dt = np.asarray(ins["L0_attn_W_dt"], np.float32)
    b_dt = np.asarray(ins["L0_attn_b_dt"], np.float32)
    A = np.asarray(ins["L0_attn_A"], np.float32)
    Dv = np.asarray(ins["L0_attn_D"], np.float32)
    R16 = np.zeros((128, 16 * 128), np.float32)
    for trm in range(16):
        for p in range(128):
            R16[trm * 8 + p // 16, trm * 128 + p] = 1.0
    RB0 = np.zeros((16, 128), np.float32)
    for p in range(128):
        RB0[p % 16, p] = 1.0
    G4 = np.zeros((128, 32), np.float32)
    for k in range(128):
        G4[k, k // 16] = 1.0
    maps = []
    cwr = np.zeros((128, 64), np.float32)
    cbr = np.zeros((128, 16), np.float32)
    for kc in range(16):
        cwr[:, kc * 4:(kc + 1) * 4] = conv_w[kc * 128:(kc + 1) * 128]
        cbr[:, kc] = conv_b[kc * 128:(kc + 1) * 128]
    for c in range(NCHIP):
        b, sub = c // 2, c % 2
        half = slice(sub * DHALF, (sub + 1) * DHALF)
        xin_full = np.empty((DI, T), dtype=BF)
        for g in range(32):
            src = xin_cores[2 * b + (g % 2)]
            i = g // 2
            xin_full[:, g * 128:(g + 1) * 128] = src[:, i * 128:(i + 1) * 128]
        xin_pad = np.zeros((DI, T + 3), dtype=BF)
        xin_pad[:, 3:] = xin_full
        A_h = A[half]
        A_sc = np.zeros((128, 128), np.float32)
        for p in range(128):
            A_sc[p, :] = A_h[np.arange(128) * 8 + p // 16, p % 16]
        msel = np.zeros((128, 2), np.float32)
        msel[:, 0] = 1.0 - sub
        msel[:, 1] = float(sub)
        maps.append({
            "xin_pad": xin_pad,
            "conv_w_r": cwr, "conv_b_r": cbr,
            "wx_dlt": np.ascontiguousarray(W_x[:, 0:64]),
            "wx_b": np.ascontiguousarray(W_x[:, 64:80]),
            "wx_c": np.ascontiguousarray(W_x[:, 80:96]),
            "wdt_half": np.ascontiguousarray(W_dt[:, half]),
            "bdt_r": np.ascontiguousarray(b_dt[half].reshape(8, 128).T),
            "A_sc": A_sc,
            "D_sc": np.ascontiguousarray(Dv[half].reshape(8, 128).T),
            "R16": R16, "RB0": RB0, "G4": G4, "msel": msel,
        })
    return maps


def lc_in_maps(ins, y2_cores, res_cores, t1_cores):
    W_out = np.ascontiguousarray(ins["L0_attn_W_out"]).astype(np.float32)
    fwt = np.ascontiguousarray(np.asarray(
        ins["L0_ffn_weight"], np.float32).T)
    x = ins["L0_attn_hidden_states"]
    maps = []
    for c in range(NCHIP):
        b, sub = c // 2, c % 2
        y2f = np.concatenate([y2_cores[2 * b], y2_cores[2 * b + 1]], axis=0)
        msel = np.zeros((128, 2), np.float32)
        msel[:, 0] = 1.0 - sub
        msel[:, 1] = float(sub)
        maps.append({
            "y2g": np.ascontiguousarray(y2f),
            "res": res_cores[c], "t1": t1_cores[c],
            "xfm": np.ascontiguousarray(np.asarray(x[b])[OWN_IDX[sub]].T,
                                        dtype=np.float32),
            "W_out": W_out, "ffn_wT": fwt, "msel": msel,
        })
    return maps


_PROG_CACHE = {}

def get_prog(name, builder):
    if name not in _PROG_CACHE:
        P, st = builder()
        P.emit(st)
        st.close()
        _PROG_CACHE[name] = P
    return _PROG_CACHE[name]


def run_phase(P, in_maps):
    res = run_bass_kernel_spmd(P.nc, in_maps, core_ids=list(range(NCHIP)))
    return res.results


RG_PAIRS = [[0, 1], [2, 3], [4, 5], [6, 7]]


def _dbg_out(P, nc, src_ap, shape, dt):
    dbg = nc.dram_tensor("dbgout", shape, dt, kind="ExternalOutput").ap()
    stX = ExitStack()
    tile = stX.enter_context(nc.sbuf_tensor("dbg_t", [128, shape[1]], dt))
    for ch in range(shape[0] // 128):
        P.op("yd", lambda e, ch=ch: e.dma_start(
            out=tile[:], in_=src_ap[ch * 128:(ch + 1) * 128, :]),
            writes=["dbgt"])
        P.op("yd", lambda e, ch=ch: e.dma_start(
            out=dbg[ch * 128:(ch + 1) * 128, :], in_=tile[:]),
            reads=["dbgt"])
    P.fence_outputs()
    stX.close()


def build_FUSED(upto="LE"):
    P = Prog("FUSED")
    nc = P.nc
    I, N = "ExternalInput", "Internal"
    dram = lambda n, s, dt, k: nc.dram_tensor(n, s, dt, kind=k)

    xfm = dram("xfm", [D, TLOC], F32, I).ap()
    nw = dram("norm_w_r", [128, 8], F32, I).ap()
    w_in = dram("W_in", [D, 2 * DI], F32, I).ap()
    msel = dram("msel", [128, 2], F32, I).ap()
    t1_h = dram("t1", [D, TLOC], BF16, N)
    xin_h = dram("xin", [DI, TLOC], BF16, N)
    res_h = dram("res", [DI, TLOC], BF16, N)
    xing_h = dram("xing", [2 * DI, TLOC], BF16, N)
    xinpad_h = dram("xin_pad", [DI, T + 3], BF16, N)
    y2_h = dram("y2", [DHALF, T], BF16, N)
    y2g_h = dram("y2g", [DI, T], BF16, N)
    x2_h = dram("x2", [D, TLOC], F32, N)
    t2a_h = dram("t2a", [D, TLOC], BF16, N)
    t2ag_h = dram("t2ag", [DI, TLOC], BF16, N)
    x3_h = dram("x3", [D, TLOC], F32, N)
    t3_h = dram("t3", [D, TLOC], BF16, N)
    x4 = dram("x4", [D, TLOC], F32, "ExternalOutput").ap()

    stA = ExitStack()
    body_LA(P, stA, dict(xfm=xfm, norm_w_r=nw, W_in=w_in, t1=t1_h.ap(),
                         xin=xin_h.ap(), res=res_h.ap()))
    P.barrier()
    stA.close()

    P.op("cc", lambda e: e.collective_compute(
        "AllGather", ALU.bypass, RG_PAIRS,
        ins=[xin_h.ap().opt()], outs=[xing_h.ap().opt()]))
    P.barrier()
    stR = ExitStack()
    z3sb = stR.enter_context(nc.sbuf_tensor("fz3", [128, 3], BF16))
    P.op("v", lambda e: e.memset(z3sb[:], 0.0), writes=["fz3"])
    xing = xing_h.ap()
    xinpad = xinpad_h.ap()
    for ch in range(16):
        P.op("yd", lambda e, ch=ch: e.dma_start(
            out=xinpad[ch * 128:(ch + 1) * 128, 0:3], in_=z3sb[:]),
            reads=["fz3"])
        for rk in range(2):
            P.op("yd", lambda e, ch=ch, rk=rk: e.dma_start(
                out=xinpad[ch * 128:(ch + 1) * 128, 3:].rearrange(
                    "p (a c) -> p a c", c=256)[:, :, rk * 128:(rk + 1) * 128],
                in_=xing[rk * DI + ch * 128:rk * DI + (ch + 1) * 128,
                         :].rearrange("p (a b) -> p a b", b=128)))
    P.barrier()
    stR.close()
    if upto == "REPACK":
        _dbg_out(P, nc, xinpad, [DI, T + 3], BF16)
        return P, ExitStack()
    if upto == "LA":
        _dbg_out(P, nc, xin_h.ap(), [DI, TLOC], BF16)
        return P, ExitStack()

    stB = ExitStack()
    ddB = dict(
        xin_pad=xinpad,
        conv_w_r=dram("conv_w_r", [128, 64], F32, I).ap(),
        conv_b_r=dram("conv_b_r", [128, 16], F32, I).ap(),
        wx_dlt=dram("wx_dlt", [DI, 64], F32, I).ap(),
        wx_b=dram("wx_b", [DI, 16], F32, I).ap(),
        wx_c=dram("wx_c", [DI, 16], F32, I).ap(),
        wdt_half=dram("wdt_half", [64, DHALF], F32, I).ap(),
        bdt_r=dram("bdt_r", [128, 8], F32, I).ap(),
        A_sc=dram("A_sc", [128, 128], F32, I).ap(),
        D_sc=dram("D_sc", [128, 8], F32, I).ap(),
        R16=dram("R16", [128, 16 * 128], F32, I).ap(),
        RB0=dram("RB0", [16, 128], F32, I).ap(),
        G4=dram("G4", [128, 32], F32, I).ap(),
        msel=msel,
        y2=y2_h.ap(),
    )
    body_LB(P, stB, ddB)
    P.barrier()
    stB.close()
    if upto == "LB":
        _dbg_out(P, nc, y2_h.ap(), [DHALF, T], BF16)
        return P, ExitStack()

    P.op("cc", lambda e: e.collective_compute(
        "AllGather", ALU.bypass, RG_PAIRS,
        ins=[y2_h.ap().opt()], outs=[y2g_h.ap().opt()]))
    P.barrier()

    stC = ExitStack()
    ddC = dict(
        y2g=y2g_h.ap(), res=res_h.ap(), t1=t1_h.ap(), xfm=xfm,
        W_out=dram("W_out", [DI, D], F32, I).ap(),
        ffn_wT=dram("ffn_wT", [D, D], F32, I).ap(),
        msel=msel,
        x2=x2_h.ap(), t2a=t2a_h.ap(),
    )
    body_LC(P, stC, ddC)
    P.barrier()
    stC.close()
    if upto == "LC":
        _dbg_out(P, nc, x2_h.ap(), [D, TLOC], F32)
        return P, ExitStack()

    P.op("cc", lambda e: e.collective_compute(
        "AllGather", ALU.bypass, RG_PAIRS,
        ins=[t2a_h.ap().opt()], outs=[t2ag_h.ap().opt()]))
    P.barrier()

    stD = ExitStack()
    ddD = dict(
        t2ag=t2ag_h.ap(), t2a_own=t2a_h.ap(), x2=x2_h.ap(),
        Wq=dram("Wq", [D, D], F32, I).ap(),
        Wk=dram("Wk", [D, D], F32, I).ap(),
        Wv=dram("Wv", [D, D], F32, I).ap(),
        Wo=dram("Wo", [D, D], F32, I).ap(),
        bq_r=dram("bq_r", [128, 8], F32, I).ap(),
        bo_r=dram("bo_r", [128, 8], F32, I).ap(),
        bv_row=dram("bv_row", [1, D], F32, I).ap(),
        masks=dram("masks", [NTILE * MAXMASK * 128, 128], BF16, I).ap(),
        x3=x3_h.ap(), t3=t3_h.ap(),
    )
    body_LD(P, stD, ddD)
    P.barrier()
    stD.close()
    if upto == "LD":
        _dbg_out(P, nc, x3_h.ap(), [D, TLOC], F32)
        return P, ExitStack()

    stE = ExitStack()
    ddE = dict(
        x3=x3_h.ap(), t3=t3_h.ap(),
        w1T=dram("w1T", [D, 4096], F32, I).ap(),
        w2T=dram("w2T", [D, 4096], F32, I).ap(),
        h1T=dram("h1T", [4096, 4096], F32, I).ap(),
        h2T=dram("h2T", [4096, 4096], F32, I).ap(),
        w3T=dram("w3T", [4096, D], F32, I).ap(),
        s1_r=dram("s1_r", [128, 32], F32, I).ap(),
        s2_r=dram("s2_r", [128, 32], F32, I).ap(),
        h1s_r=dram("h1s_r", [128, 32], F32, I).ap(),
        h2s_r=dram("h2s_r", [128, 32], F32, I).ap(),
        w3s_r=dram("w3s_r", [128, 8], F32, I).ap(),
        x4=x4,
    )
    body_LE(P, stE, ddE)
    P.fence_outputs()
    stE.close()
    st = ExitStack()
    return P, st


def fused_in_maps(ins):
    la = la_in_maps(ins)
    lbw = lb_in_maps(ins, [np.zeros((DI, TLOC), BF)] * NCHIP)
    lcw = lc_in_maps(ins, [np.zeros((DHALF, T), BF)] * NCHIP,
                     [None] * NCHIP, [None] * NCHIP)
    ldw = ld_in_maps(ins, [np.zeros((D, TLOC), BF)] * NCHIP, [None] * NCHIP)
    lew = le_in_maps(ins, [None] * NCHIP, [None] * NCHIP)
    maps = []
    for c in range(NCHIP):
        m = {}
        m.update(la[c])
        for k, v in lbw[c].items():
            if k != "xin_pad":
                m[k] = v
        for k in ("W_out", "ffn_wT"):
            m[k] = lcw[c][k]
        for k, v in ldw[c].items():
            if k not in ("t2ag", "t2a_own", "x2_ld"):
                m[k] = v
        for k, v in lew[c].items():
            if k not in ("x3_le", "t3_le"):
                m[k] = v
        maps.append(m)
    return maps


def kernel(**inputs):
    P = get_prog("FUSED", build_FUSED)
    maps = fused_in_maps(inputs)
    res = run_phase(P, maps)
    out = np.empty((B, T, D), np.float32)
    for c in range(NCHIP):
        b, sub = c // 2, c % 2
        out[b][OWN_IDX[sub]] = res[c]["x4"].T
    return out


def kernel_unfused(**inputs):
    ins = inputs
    PA = get_prog("LA", build_LA)
    ra = run_phase(PA, la_in_maps(ins))
    xin_cores = [r["xin"] for r in ra]
    res_cores = [r["res"] for r in ra]
    t1_cores = [r["t1"] for r in ra]
    PB = get_prog("LB", build_LB)
    rb = run_phase(PB, lb_in_maps(ins, xin_cores))
    y2_cores = [r["y2"] for r in rb]
    PC = get_prog("LC", build_LC)
    rc = run_phase(PC, lc_in_maps(ins, y2_cores, res_cores, t1_cores))
    x2_cores = [r["x2"] for r in rc]
    t2a_cores = [r["t2a"] for r in rc]
    PD = get_prog("LD", build_LD)
    rd = run_phase(PD, ld_in_maps(ins, t2a_cores, x2_cores))
    x3_cores = [r["x3"] for r in rd]
    t3_cores = [r["t3"] for r in rd]
    PE_ = get_prog("LE", build_LE)
    re_ = run_phase(PE_, le_in_maps(ins, x3_cores, t3_cores))
    out = np.empty((B, T, D), np.float32)
    for c in range(NCHIP):
        b, sub = c // 2, c % 2
        out[b][OWN_IDX[sub]] = re_[c]["x4"].T
    return out


# version 04
# speedup vs baseline: 1.1307x; 1.1307x over previous
import math
import numpy as np
import ml_dtypes
from contextlib import ExitStack

import concourse.bass as bass
import concourse.mybir as mybir
from concourse.bass_utils import run_bass_kernel_spmd

F32 = mybir.dt.float32
BF16 = mybir.dt.bfloat16
AF = mybir.ActivationFunctionType
ALU = mybir.AluOpType
BF = ml_dtypes.bfloat16

B, T, D = 4, 4096, 1024
DI, NST, DTR = 2048, 16, 64
H, HD = 16, 64
EPS = 1e-5
TLOC, NTILE, DHALF = 2048, 16, 1024
NEG = -1e30
NCHIP = 8

def bound_of(q):
    return min(T - 1, max(100 * (q // 100) + 99, q + 50))

def attn_tables():
    n_kt, ffa = [], []
    for i in range(NTILE):
        g_hi = 2 * i + 1
        n_kt.append((bound_of(128 * g_hi + 127) + 1 + 127) // 128)
        g_lo = 2 * i
        ffa.append((bound_of(128 * g_lo) + 1) // 128)
    return n_kt, ffa

N_KT, FFA = attn_tables()
MAXMASK = max(n - f for n, f in zip(N_KT, FFA))
NKT_G = [N_KT[4 * g + 3] for g in range(4)]
FFA_G = [FFA[4 * g] for g in range(4)]
MAXMASK_G = max(n - f for n, f in zip(NKT_G, FFA_G))


_STREAM = {"t": "tensor", "v": "vector", "s": "scalar", "p": "gpsimd",
           "yd": "sync", "pd": "gpsimd", "sd": "scalar", "cc": "gpsimd"}
_AMT = {"t": 1, "v": 1, "s": 1, "p": 1, "yd": 16, "pd": 16, "sd": 16, "cc": 1}
_COMPUTE = {"t", "v", "s", "p"}


NDMASLOT = 24


class Prog:

    def __init__(self, name="prog"):
        self.name = name
        self.nc = bass.Bass("TRN2", target_bir_lowering=False, debug=False)
        self.ops = []
        self.lastw = {}
        self.readers = {}
        self.barrier_idx = None
        self.slot_rr = 0
        self.slot_prev = {}
        self.slot_uses = {}

    def _record(self, eng, fn, deps):
        i = len(self.ops)
        o = dict(eng=eng, fn=fn, deps=deps, inc=False, cnt=None,
                 sem=eng, amt=1, consumers=[])
        if eng == "cc":
            n = sum(1 for oo in self.ops if oo["eng"] == "cc")
            o["sem"] = f"cc{n}"
            o["amt"] = None
            o["inc"] = True
            o["cnt"] = 1
        elif eng not in _COMPUTE:
            slot = self.slot_rr % NDMASLOT
            self.slot_rr += 1
            prev = self.slot_prev.get(slot)
            if prev is not None:
                deps.add(prev)
                deps.update(self.ops[prev]["consumers"])
            self.slot_prev[slot] = i
            self.slot_uses[slot] = self.slot_uses.get(slot, 0) + 1
            o["sem"] = f"d{slot}"
            o["amt"] = 16
            o["inc"] = True
            o["cnt"] = self.slot_uses[slot] * o["amt"]
        deps.discard(i)
        self.ops.append(o)
        for j in deps:
            self.ops[j]["consumers"].append(i)
        return i

    def op(self, eng, fn, reads=(), writes=()):
        i = len(self.ops)
        deps = set()

        def add_dep(j):
            if j is None or j == i:
                return
            je = self.ops[j]["eng"]
            if je == eng and eng in _COMPUTE and je in _COMPUTE:
                return
            deps.add(j)

        add_dep(self.barrier_idx)
        for r in reads:
            add_dep(self.lastw.get(r))
        for w in writes:
            add_dep(self.lastw.get(w))
            for j in self.readers.get(w, ()):
                add_dep(j)
        idx = self._record(eng, fn, deps)
        assert idx == i
        for r in reads:
            self.readers.setdefault(r, []).append(i)
        for w in writes:
            self.lastw[w] = i
            self.readers[w] = []
        return i

    def barrier(self):
        deps = set()
        lastc = {}
        start = -1 if self.barrier_idx is None else self.barrier_idx
        for j, o in enumerate(self.ops):
            if o["eng"] in _COMPUTE:
                lastc[o["eng"]] = j
            elif j > start:
                deps.add(j)
        deps.update(lastc.values())
        i = self._record("v", lambda e: e.engine_nop(), deps)
        self.barrier_idx = i
        self.lastw = {}
        self.readers = {}
        return i

    def fence_outputs(self):
        self.barrier()

    def emit(self, stack):
        nc = self.nc
        for o in self.ops:
            for j in o["deps"]:
                self.ops[j]["inc"] = True
        cnt = {}
        for o in self.ops:
            if o["inc"] and o["eng"] in _COMPUTE:
                cnt[o["sem"]] = cnt.get(o["sem"], 0) + o["amt"]
                o["cnt"] = cnt[o["sem"]]
        semnames = sorted({o["sem"] for o in self.ops if o["inc"]})
        sems = {s: stack.enter_context(nc.semaphore("sem_" + s))
                for s in semnames}
        streams = {}
        for idx, o in enumerate(self.ops):
            streams.setdefault(_STREAM[o["eng"]], []).append(idx)
        block = stack.enter_context(nc.Block())
        for sname, idxs in streams.items():
            def body(e, idxs=idxs):
                water = {}
                for idx in idxs:
                    o = self.ops[idx]
                    for j in sorted(o["deps"]):
                        dj = self.ops[j]
                        s, need = dj["sem"], dj["cnt"]
                        assert need is not None, (idx, j, dj)
                        if water.get(s, 0) < need:
                            e.wait_ge(sems[s], need)
                            water[s] = need
                    inst = o["fn"](e)
                    if o["inc"]:
                        if o["eng"] == "cc":
                            inst.then_inc(sems[o["sem"]])
                        else:
                            inst.then_inc(sems[o["sem"]], o["amt"])
            getattr(block, sname)(body)


class NcProxy:

    def __init__(self, nc, prefix):
        self._nc = nc
        self._prefix = prefix

    def __getattr__(self, name):
        return getattr(self._nc, name)

    def sbuf_tensor(self, name, *a, **k):
        return self._nc.sbuf_tensor(self._prefix + name, *a, **k)

    def psum_tensor(self, name, *a, **k):
        return self._nc.psum_tensor(self._prefix + name, *a, **k)


def ln_rows(P, nc, st, tag, x_tiles, xkeys, Tn, want_rms=False, in_bf16=False):
    E = st.enter_context
    nck = len(x_tiles)
    m_sb = E(nc.sbuf_tensor(f"{tag}_m", [1, Tn], F32))
    q_sb = E(nc.sbuf_tensor(f"{tag}_q", [1, Tn], F32))
    r_sb = E(nc.sbuf_tensor(f"{tag}_r", [1, Tn], F32))
    mr_sb = E(nc.sbuf_tensor(f"{tag}_mr", [1, Tn], F32))
    rms_sb = E(nc.sbuf_tensor(f"{tag}_rms", [1, Tn], F32)) if want_rms else None
    var_sb = q_sb
    tmp_sb = E(nc.sbuf_tensor(f"{tag}_tmp", [1, Tn], F32))
    xsq = [E(nc.sbuf_tensor(f"{tag}_xsq{i}", [128, 512], F32)) for i in range(2)]
    pm = E(nc.psum_tensor(f"{tag}_pm", [1, 512], F32))
    pq = E(nc.psum_tensor(f"{tag}_pq", [1, 512], F32))
    ones = E(nc.sbuf_tensor(f"{tag}_ones", [128, 1], BF16 if in_bf16 else F32))
    onesf = E(nc.sbuf_tensor(f"{tag}_onesf", [128, 1], F32))
    epsc = E(nc.sbuf_tensor(f"{tag}_eps", [1, 1], F32))
    P.op("v", lambda e: e.memset(ones[:], 1.0 / D), writes=[f"{tag}ones"])
    P.op("v", lambda e: e.memset(onesf[:], 1.0 / D), writes=[f"{tag}onesf"])
    P.op("v", lambda e: e.memset(epsc[:], EPS), writes=[f"{tag}eps"])

    ntc = Tn // 512
    for tc in range(ntc):
        for kc in range(nck):
            xi = x_tiles[kc][:, tc * 512:(tc + 1) * 512]
            sq = xsq[kc % 2]
            P.op("s", lambda e, xi=xi, sq=sq: e.activation(sq[:], xi, AF.Square),
                 reads=[xkeys[kc]], writes=[f"{tag}sq{kc % 2}"])
            P.op("t", lambda e, xi=xi, kc=kc: e.matmul(
                pm[:], ones[:], xi, start=(kc == 0), stop=(kc == nck - 1)),
                reads=[xkeys[kc], f"{tag}ones"], writes=[f"{tag}pm"])
            P.op("t", lambda e, sq=sq, kc=kc: e.matmul(
                pq[:], onesf[:], sq[:], start=(kc == 0), stop=(kc == nck - 1)),
                reads=[f"{tag}sq{kc % 2}", f"{tag}onesf"], writes=[f"{tag}pq"])
        sl = slice(tc * 512, (tc + 1) * 512)
        P.op("v", lambda e, sl=sl: e.tensor_copy(m_sb[:, sl], pm[:]),
             reads=[f"{tag}pm"], writes=[f"{tag}m"])
        P.op("v", lambda e, sl=sl: e.tensor_copy(q_sb[:, sl], pq[:]),
             reads=[f"{tag}pq"], writes=[f"{tag}q"])
    P.op("v", lambda e: e.tensor_mul(tmp_sb[:], m_sb[:], m_sb[:]),
         reads=[f"{tag}m"], writes=[f"{tag}tmp"])
    P.op("v", lambda e: e.tensor_sub(var_sb[:], q_sb[:], tmp_sb[:]),
         reads=[f"{tag}q", f"{tag}tmp"], writes=[f"{tag}q", f"{tag}var"])
    P.op("s", lambda e: e.activation(tmp_sb[:], var_sb[:], AF.Sqrt,
         bias=epsc[0:1, 0:1]), reads=[f"{tag}var", f"{tag}eps"],
         writes=[f"{tag}tmp"])
    P.op("v", lambda e: e.reciprocal(r_sb[:], tmp_sb[:]),
         reads=[f"{tag}tmp"], writes=[f"{tag}r"])
    P.op("v", lambda e: e.tensor_mul(mr_sb[:], m_sb[:], r_sb[:]),
         reads=[f"{tag}m", f"{tag}r"], writes=[f"{tag}mr"])
    if want_rms:
        P.op("v", lambda e: e.tensor_scalar(tmp_sb[:], var_sb[:],
             epsc[0:1, 0:1], None, ALU.add),
             reads=[f"{tag}var", f"{tag}eps"], writes=[f"{tag}tmp"])
        P.op("v", lambda e: e.reciprocal(tmp_sb[:], tmp_sb[:]),
             reads=[f"{tag}tmp"], writes=[f"{tag}tmp"])
        P.op("v", lambda e: e.tensor_mul(tmp_sb[:], var_sb[:], tmp_sb[:]),
             reads=[f"{tag}var", f"{tag}tmp"], writes=[f"{tag}tmp"])
        P.op("s", lambda e: e.activation(tmp_sb[:], tmp_sb[:], AF.Sqrt,
             bias=epsc[0:1, 0:1]), reads=[f"{tag}tmp", f"{tag}eps"],
             writes=[f"{tag}tmp"])
        P.op("v", lambda e: e.reciprocal(rms_sb[:], tmp_sb[:]),
             reads=[f"{tag}tmp"], writes=[f"{tag}rms"])
    rows = dict(r=(r_sb, f"{tag}r"), mr=(mr_sb, f"{tag}mr"))
    if want_rms:
        rows["rms"] = (rms_sb, f"{tag}rms")
    return rows


def bcast_rows(P, nc, st, tag, rows, Tn, names):
    E = st.enter_context
    onesr = E(nc.sbuf_tensor(f"{tag}_onesr", [1, 128], F32))
    P.op("v", lambda e: e.memset(onesr[:], 1.0), writes=[f"{tag}onesr"])
    pb = E(nc.psum_tensor(f"{tag}_pb", [128, 512], F32))
    out = {}
    for nm in names:
        row, rkey = rows[nm]
        dst = E(nc.sbuf_tensor(f"{tag}_b_{nm}", [128, Tn], BF16))
        for tc in range(Tn // 512):
            sl = slice(tc * 512, (tc + 1) * 512)
            P.op("t", lambda e, row=row, sl=sl: e.matmul(
                pb[:], onesr[:], row[:, sl], start=True, stop=True),
                reads=[rkey, f"{tag}onesr"], writes=[f"{tag}pb"])
            P.op("s", lambda e, dst=dst, sl=sl: e.copy(dst[:, sl], pb[:]),
                 reads=[f"{tag}pb"], writes=[f"{tag}b{nm}"])
        out[nm] = (dst, f"{tag}b{nm}")
    return out


def decl_LA(nc, I, O):
    return dict(
        xfm=nc.dram_tensor("xfm", [D, TLOC], F32, kind=I).ap(),
        norm_w_r=nc.dram_tensor("norm_w_r", [128, 8], F32, kind=I).ap(),
        W_in=nc.dram_tensor("W_in", [D, 2 * DI], F32, kind=I).ap(),
        t1=nc.dram_tensor("t1", [D, TLOC], BF16, kind=O).ap(),
        xin=nc.dram_tensor("xin", [DI, TLOC], BF16, kind=O).ap(),
        res=nc.dram_tensor("res", [DI, TLOC], BF16, kind=O).ap(),
    )


def build_LA():
    P = Prog("LA")
    st = ExitStack()
    body_LA(P, st, decl_LA(P.nc, "ExternalInput", "ExternalOutput"))
    P.fence_outputs()
    return P, st


def body_LA(P, st, dd):
    nc = NcProxy(P.nc, "la_")
    s1 = ExitStack()
    E = st.enter_context
    E1 = s1.enter_context
    xfm, nw, w_in = dd["xfm"], dd["norm_w_r"], dd["W_in"]
    t1_o, xin_o, res_o = dd["t1"], dd["xin"], dd["res"]

    nw_sb = E(nc.sbuf_tensor("nw", [128, 8], F32))
    t1 = [E(nc.sbuf_tensor(f"t1_{c}", [128, TLOC], BF16)) for c in range(8)]
    h2 = [E(nc.sbuf_tensor(f"h2_{c}", [128, TLOC], BF16)) for c in range(8)]
    x = [E1(nc.sbuf_tensor(f"x{c}", [128, TLOC], F32)) for c in range(8)]
    tmp = [E1(nc.sbuf_tensor(f"tmp{i}", [128, 512], F32)) for i in range(2)]

    for c in range(8):
        P.op("yd", lambda e, c=c: e.dma_start(
            out=x[c][:], in_=xfm[c * 128:(c + 1) * 128, :]), writes=[f"x{c}"])
    P.op("yd", lambda e: e.dma_start(out=nw_sb[:], in_=nw), writes=["nw"])

    rows = ln_rows(P, nc, s1, "la", [x[c][:] for c in range(8)],
                   [f"x{c}" for c in range(8)], TLOC, want_rms=True)
    bc = bcast_rows(P, nc, s1, "la", rows, TLOC, ["r", "mr", "rms"])
    rb, rbk = bc["r"]
    mrb, mrbk = bc["mr"]
    rmsb, rmsbk = bc["rms"]

    for c in range(8):
        for tc in range(TLOC // 512):
            sl = slice(tc * 512, (tc + 1) * 512)
            tb = tmp[tc % 2]
            P.op("v", lambda e, c=c, sl=sl, tb=tb: e.tensor_mul(
                tb[:], x[c][:, sl], rb[:, sl]),
                reads=[f"x{c}", rbk], writes=[f"tmp{tc % 2}"])
            P.op("v", lambda e, c=c, sl=sl, tb=tb: e.tensor_sub(
                t1[c][:, sl], tb[:], mrb[:, sl]),
                reads=[f"tmp{tc % 2}", mrbk], writes=[f"t1_{c}"])
            P.op("v", lambda e, c=c, sl=sl: e.scalar_tensor_tensor(
                h2[c][:, sl], t1[c][:, sl], nw_sb[:, c:c + 1], rmsb[:, sl],
                ALU.mult, ALU.mult),
                reads=[f"t1_{c}", "nw", rmsbk], writes=[f"h2_{c}"])
        P.op("yd", lambda e, c=c: e.dma_start(
            out=t1_o[c * 128:(c + 1) * 128, :], in_=t1[c][:]), reads=[f"t1_{c}"])

    s1.close()
    P.barrier()
    wbf = E(nc.sbuf_tensor("wbf", [128, 8 * 2048], BF16))
    stage = [E(nc.sbuf_tensor(f"stage{i}", [128, TLOC], BF16)) for i in range(2)]
    pw = [E(nc.psum_tensor(f"pw{i}", [128, 512], F32)) for i in range(2)]
    for half, dst in ((0, xin_o), (1, res_o)):
        for kc in range(8):
            P.op("pd", lambda e, kc=kc, half=half: e.dma_start(
                out=wbf[:, kc * 2048:(kc + 1) * 2048],
                in_=w_in[kc * 128:(kc + 1) * 128,
                         half * 2048:(half + 1) * 2048]),
                writes=[f"wbf{kc}"])
        for oc in range(16):
            sg = stage[oc % 2]
            for tc in range(4):
                ps = pw[tc % 2]
                for kc in range(8):
                    P.op("t", lambda e, kc=kc, oc=oc, tc=tc, ps=ps: e.matmul(
                        ps[:], wbf[:, kc * 2048 + oc * 128: kc * 2048 + oc * 128 + 128],
                        h2[kc][:, tc * 512:(tc + 1) * 512],
                        start=(kc == 0), stop=(kc == 7)),
                        reads=[f"wbf{kc}", f"h2_{kc}"], writes=[f"pw{tc % 2}"])
                P.op("s", lambda e, tc=tc, sg=sg, ps=ps: e.copy(
                    sg[:, tc * 512:(tc + 1) * 512], ps[:]),
                    reads=[f"pw{tc % 2}"], writes=[f"stage{oc % 2}"])
            P.op("yd", lambda e, oc=oc, dst=dst, sg=sg: e.dma_start(
                out=dst[oc * 128:(oc + 1) * 128, :], in_=sg[:]),
                reads=[f"stage{oc % 2}"])


def decl_LB(nc, I, O):
    d = dict(
        xin_pad=nc.dram_tensor("xin_pad", [DI, T + 3], BF16, kind=I).ap(),
        conv_w_r=nc.dram_tensor("conv_w_r", [128, 64], F32, kind=I).ap(),
        conv_b_r=nc.dram_tensor("conv_b_r", [128, 16], F32, kind=I).ap(),
        wx_dlt=nc.dram_tensor("wx_dlt", [DI, 64], F32, kind=I).ap(),
        wx_b=nc.dram_tensor("wx_b", [DI, 16], F32, kind=I).ap(),
        wx_c=nc.dram_tensor("wx_c", [DI, 16], F32, kind=I).ap(),
        wdt_half=nc.dram_tensor("wdt_half", [64, DHALF], F32, kind=I).ap(),
        bdt_r=nc.dram_tensor("bdt_r", [128, 8], F32, kind=I).ap(),
        A_sc=nc.dram_tensor("A_sc", [128, 128], F32, kind=I).ap(),
        D_sc=nc.dram_tensor("D_sc", [128, 8], F32, kind=I).ap(),
        R16=nc.dram_tensor("R16", [128, 16 * 128], F32, kind=I).ap(),
        RB0=nc.dram_tensor("RB0", [16, 128], F32, kind=I).ap(),
        G4=nc.dram_tensor("G4", [128, 32], F32, kind=I).ap(),
        msel=nc.dram_tensor("msel", [128, 2], F32, kind=I).ap(),
        y2=nc.dram_tensor("y2", [DHALF, T], BF16, kind=O).ap(),
    )
    return d


def build_LB():
    P = Prog("LB")
    st = ExitStack()
    dd = decl_LB(P.nc, "ExternalInput", "ExternalOutput")
    body_LB(P, st, dd)
    P.fence_outputs()
    return P, st


def body_LB(P, st, dd):
    nc = NcProxy(P.nc, "lb_")
    E = st.enter_context
    xin = dd["xin_pad"]
    cwr, cbr = dd["conv_w_r"], dd["conv_b_r"]
    wxd, wxb, wxc = dd["wx_dlt"], dd["wx_b"], dd["wx_c"]
    wdt, bdt = dd["wdt_half"], dd["bdt_r"]
    asc, dsc = dd["A_sc"], dd["D_sc"]
    r16, rb0, g4 = dd["R16"], dd["RB0"], dd["G4"]
    mseli = dd["msel"]
    y2_o = dd["y2"]
    xc_d = nc.dram_tensor("lb_xc_d", [DI, T], BF16).ap()
    dl_d = nc.dram_tensor("lb_dl_d", [DHALF, T], BF16).ap()
    u_d = nc.dram_tensor("lb_u_d", [DHALF, T], BF16).ap()
    y_d = nc.dram_tensor("lb_y_d", [DHALF, T], BF16).ap()
    msel = E(nc.sbuf_tensor("msel", [128, 2], F32))
    P.op("yd", lambda e: e.dma_start(out=msel[:], in_=mseli), writes=["msel"])

    cw_sb = E(nc.sbuf_tensor("cw", [128, 64], F32))
    cb_sb = E(nc.sbuf_tensor("cb", [128, 16], F32))
    bdt_sb = E(nc.sbuf_tensor("bdt", [128, 8], F32))
    asc_sb = E(nc.sbuf_tensor("asc", [128, 128], F32))
    dsc_sb = E(nc.sbuf_tensor("dsc", [128, 8], F32))
    r16_sb = E(nc.sbuf_tensor("r16", [128, 16 * 128], BF16))
    rb0_sb = E(nc.sbuf_tensor("rb0", [16, 128], BF16))
    g4_sb = E(nc.sbuf_tensor("g4", [128, 32], BF16))
    brep = E(nc.sbuf_tensor("brep", [128, T], BF16))
    crep = E(nc.sbuf_tensor("crep", [128, T], BF16))
    for t_, ap_, nm in [(cw_sb, cwr, "cw"), (cb_sb, cbr, "cb"),
                        (bdt_sb, bdt, "bdt"), (asc_sb, asc, "asc"),
                        (dsc_sb, dsc, "dsc")]:
        P.op("yd", lambda e, t_=t_, ap_=ap_: e.dma_start(out=t_[:], in_=ap_),
             writes=[nm])
    for t_, ap_, nm in [(r16_sb, r16, "r16"), (rb0_sb, rb0, "rb0"),
                        (g4_sb, g4, "g4")]:
        P.op("pd", lambda e, t_=t_, ap_=ap_: e.dma_start(out=t_[:], in_=ap_),
             writes=[nm])

    s1 = ExitStack()
    E1 = s1.enter_context
    dlt_bf = E(nc.sbuf_tensor("dlt_bf", [64, T], BF16))
    b_bf = E(nc.sbuf_tensor("b_bf", [16, T], BF16))
    c_bf = E(nc.sbuf_tensor("c_bf", [16, T], BF16))
    wxd_sb = E1(nc.sbuf_tensor("wxd", [128, 16 * 64], BF16))
    wxb_sb = E1(nc.sbuf_tensor("wxb", [128, 16 * 16], BF16))
    wxc_sb = E1(nc.sbuf_tensor("wxc", [128, 16 * 16], BF16))
    for kc in range(16):
        P.op("pd", lambda e, kc=kc: e.dma_start(
            out=wxd_sb[:, kc * 64:(kc + 1) * 64],
            in_=wxd[kc * 128:(kc + 1) * 128, :]), writes=["wxd"])
        P.op("pd", lambda e, kc=kc: e.dma_start(
            out=wxb_sb[:, kc * 16:(kc + 1) * 16],
            in_=wxb[kc * 128:(kc + 1) * 128, :]), writes=["wxb"])
        P.op("pd", lambda e, kc=kc: e.dma_start(
            out=wxc_sb[:, kc * 16:(kc + 1) * 16],
            in_=wxc[kc * 128:(kc + 1) * 128, :]), writes=["wxc"])
    xw = [E1(nc.sbuf_tensor(f"xw{i}", [128, 2048 + 3], BF16)) for i in range(2)]
    acc = [E1(nc.sbuf_tensor(f"acc{i}", [128, 2048], F32)) for i in range(2)]
    xct = [E1(nc.sbuf_tensor(f"xct{k}", [128, 2048], BF16)) for k in range(16)]
    pdlt = E1(nc.psum_tensor("pdlt", [64, 512], F32))
    pbx = E1(nc.psum_tensor("pbx", [16, 512], F32))
    pcx = E1(nc.psum_tensor("pcx", [16, 512], F32))
    for th in range(2):
        bx = th * 2048
        for kc in range(16):
            w = xw[kc % 2]
            a = acc[kc % 2]
            xc = xct[kc]
            P.op("yd", lambda e, kc=kc, bx=bx, w=w: e.dma_start(
                out=w[:], in_=xin[kc * 128:(kc + 1) * 128, bx:bx + 2051]),
                writes=[f"xw{kc % 2}"])
            P.op("v", lambda e, w=w, a=a, kc=kc: e.tensor_scalar(
                a[:], w[:, 0:2048], cw_sb[:, kc * 4:kc * 4 + 1], None, ALU.mult),
                reads=[f"xw{kc % 2}", "cw"], writes=[f"acc{kc % 2}"])
            for j in range(1, 4):
                P.op("v", lambda e, w=w, a=a, kc=kc, j=j: e.scalar_tensor_tensor(
                    a[:], w[:, j:j + 2048], cw_sb[:, kc * 4 + j:kc * 4 + j + 1],
                    a[:], ALU.mult, ALU.add),
                    reads=[f"xw{kc % 2}", "cw", f"acc{kc % 2}"],
                    writes=[f"acc{kc % 2}"])
            P.op("s", lambda e, a=a, xc=xc, kc=kc: e.activation(
                xc[:], a[:], AF.Silu, bias=cb_sb[:, kc:kc + 1]),
                reads=[f"acc{kc % 2}", "cb"], writes=[f"xct{kc}"])
            P.op("yd", lambda e, kc=kc, bx=bx, xc=xc: e.dma_start(
                out=xc_d[kc * 128:(kc + 1) * 128, bx:bx + 2048], in_=xc[:]),
                reads=[f"xct{kc}"])
        for tq in range(4):
            sl = slice(tq * 512, (tq + 1) * 512)
            gsl = slice(bx + tq * 512, bx + (tq + 1) * 512)
            for kc in range(16):
                st_, sp_ = (kc == 0), (kc == 15)
                for ps, wsb, wk, pk, wd in [
                        (pdlt, wxd_sb, "wxd", "pdlt", 64),
                        (pbx, wxb_sb, "wxb", "pbx", 16),
                        (pcx, wxc_sb, "wxc", "pcx", 16)]:
                    P.op("t", lambda e, ps=ps, wsb=wsb, kc=kc, sl=sl, wd=wd,
                         st_=st_, sp_=sp_: e.matmul(
                             ps[:], wsb[:, kc * wd:(kc + 1) * wd],
                             xct[kc][:, sl], start=st_, stop=sp_),
                         reads=[f"xct{kc}", wk], writes=[pk])
            P.op("s", lambda e, gsl=gsl: e.copy(dlt_bf[:, gsl], pdlt[:]),
                 reads=["pdlt"], writes=["dlt_bf"])
            P.op("s", lambda e, gsl=gsl: e.copy(b_bf[:, gsl], pbx[:]),
                 reads=["pbx"], writes=["b_bf"])
            P.op("s", lambda e, gsl=gsl: e.copy(c_bf[:, gsl], pcx[:]),
                 reads=["pcx"], writes=["c_bf"])
    s1.close()
    P.barrier()

    s2 = ExitStack()
    E2 = s2.enter_context
    wdt_sb = E2(nc.sbuf_tensor("wdt", [64, DHALF], BF16))
    P.op("pd", lambda e: e.dma_start(out=wdt_sb[:], in_=wdt), writes=["wdt"])
    u0 = [E2(nc.sbuf_tensor(f"u0_{i}", [128, 512], F32)) for i in range(2)]
    spl = [E2(nc.sbuf_tensor(f"spl{i}", [128, 512], F32)) for i in range(2)]
    rl = [E2(nc.sbuf_tensor(f"rl{i}", [128, 512], F32)) for i in range(2)]
    dstage = [E2(nc.sbuf_tensor(f"dstage{i}", [128, T], BF16)) for i in range(2)]
    ustage = [E2(nc.sbuf_tensor(f"ustage{i}", [128, T], BF16)) for i in range(2)]
    xcre = [E2(nc.sbuf_tensor(f"xcre{i}", [128, T], BF16)) for i in range(2)]
    pdl = [E2(nc.psum_tensor(f"pdl{i}", [128, 512], F32)) for i in range(2)]
    pbr = [E2(nc.psum_tensor(f"pbr{i}", [128, 512], F32)) for i in range(2)]
    for tc in range(8):
        sl = slice(tc * 512, (tc + 1) * 512)
        ps = pbr[0]
        P.op("t", lambda e, sl=sl, ps=ps: e.matmul(
            ps[:], rb0_sb[:], b_bf[:, sl], start=True, stop=True),
            reads=["rb0", "b_bf"], writes=["pbr0"])
        P.op("v", lambda e, sl=sl, ps=ps: e.tensor_copy(brep[:, sl], ps[:]),
             reads=["pbr0"], writes=["brep"])
        ps2_ = pbr[1]
        P.op("t", lambda e, sl=sl, ps2_=ps2_: e.matmul(
            ps2_[:], rb0_sb[:], c_bf[:, sl], start=True, stop=True),
            reads=["rb0", "c_bf"], writes=["pbr1"])
        P.op("v", lambda e, sl=sl, ps2_=ps2_: e.tensor_copy(crep[:, sl], ps2_[:]),
             reads=["pbr1"], writes=["crep"])
    xlo = [E2(nc.sbuf_tensor(f"xlo{i}", [128, T], BF16)) for i in range(2)]
    xhi = [E2(nc.sbuf_tensor(f"xhi{i}", [128, T], BF16)) for i in range(2)]
    xtm = [E2(nc.sbuf_tensor(f"xtm{i}", [128, T], BF16)) for i in range(2)]

    def emit_xc_own(oc, xr, xrkey):
        i2 = oc % 2
        P.op("yd", lambda e, oc=oc, i2=i2: e.dma_start(
            out=xlo[i2][:], in_=xc_d[oc * 128:(oc + 1) * 128, :]),
            writes=[f"xlo{i2}"])
        P.op("yd", lambda e, oc=oc, i2=i2: e.dma_start(
            out=xhi[i2][:], in_=xc_d[(8 + oc) * 128:(9 + oc) * 128, :]),
            writes=[f"xhi{i2}"])
        P.op("v", lambda e, i2=i2: e.tensor_scalar(
            xtm[i2][:], xlo[i2][:], msel[:, 0:1], None, ALU.mult),
            reads=[f"xlo{i2}", "msel"], writes=[f"xtm{i2}"])
        P.op("v", lambda e, i2=i2, xr=xr: e.scalar_tensor_tensor(
            xr[:], xhi[i2][:], msel[:, 1:2], xtm[i2][:], ALU.mult, ALU.add),
            reads=[f"xhi{i2}", f"xtm{i2}", "msel"], writes=[xrkey])

    for oc in range(8):
        ds = dstage[oc % 2]
        us = ustage[oc % 2]
        xr = xcre[oc % 2]
        emit_xc_own(oc, xr, f"xcre{oc % 2}")
        for tc in range(8):
            sl = slice(tc * 512, (tc + 1) * 512)
            ps = pdl[tc % 2]
            b0, b1 = u0[tc % 2], spl[tc % 2]
            r0 = rl[tc % 2]
            P.op("t", lambda e, oc=oc, sl=sl, ps=ps: e.matmul(
                ps[:], wdt_sb[:, oc * 128:(oc + 1) * 128], dlt_bf[:, sl],
                start=True, stop=True),
                reads=["wdt", "dlt_bf"], writes=[f"pdl{tc % 2}"])
            P.op("v", lambda e, ps=ps, b0=b0, oc=oc: e.tensor_scalar(
                b0[:], ps[:], bdt_sb[:, oc:oc + 1], None, ALU.add),
                reads=[f"pdl{tc % 2}", "bdt"], writes=[f"u0_{tc % 2}"])
            P.op("s", lambda e, b0=b0, b1=b1: e.activation(b1[:], b0[:], AF.Abs),
                 reads=[f"u0_{tc % 2}"], writes=[f"spl{tc % 2}"])
            P.op("s", lambda e, b1=b1: e.activation(b1[:], b1[:], AF.Exp,
                                                    scale=-1.0),
                 reads=[f"spl{tc % 2}"], writes=[f"spl{tc % 2}"])
            P.op("s", lambda e, b1=b1: e.activation(b1[:], b1[:], AF.Ln,
                                                    bias=1.0),
                 reads=[f"spl{tc % 2}"], writes=[f"spl{tc % 2}"])
            P.op("v", lambda e, b0=b0, r0=r0: e.tensor_scalar_max(
                r0[:], b0[:], 0.0),
                reads=[f"u0_{tc % 2}"], writes=[f"rl{tc % 2}"])
            P.op("v", lambda e, ds=ds, sl=sl, r0=r0, b1=b1: e.tensor_add(
                ds[:, sl], r0[:], b1[:]),
                reads=[f"rl{tc % 2}", f"spl{tc % 2}"], writes=[f"dstage{oc % 2}"])
            P.op("v", lambda e, us=us, ds=ds, xr=xr, sl=sl: e.tensor_mul(
                us[:, sl], ds[:, sl], xr[:, sl]),
                reads=[f"dstage{oc % 2}", f"xcre{oc % 2}"],
                writes=[f"ustage{oc % 2}"])
        P.op("yd", lambda e, oc=oc, ds=ds: e.dma_start(
            out=dl_d[oc * 128:(oc + 1) * 128, :], in_=ds[:]),
            reads=[f"dstage{oc % 2}"])
        P.op("yd", lambda e, oc=oc, us=us: e.dma_start(
            out=u_d[oc * 128:(oc + 1) * 128, :], in_=us[:]),
            reads=[f"ustage{oc % 2}"])
    s2.close()
    P.barrier()

    s3 = ExitStack()
    E3 = s3.enter_context
    dch = [E3(nc.sbuf_tensor(f"dch{i}", [128, T], BF16)) for i in range(2)]
    uch = [E3(nc.sbuf_tensor(f"uch{i}", [128, T], BF16)) for i in range(2)]
    dA = [E3(nc.sbuf_tensor(f"dA{i}", [128, T], BF16)) for i in range(2)]
    db = [E3(nc.sbuf_tensor(f"db{i}", [128, T], BF16)) for i in range(2)]
    s_t = [E3(nc.sbuf_tensor(f"s_t{i}", [128, T], BF16)) for i in range(2)]
    sC = [E3(nc.sbuf_tensor(f"sC{v}", [128, T], BF16)) for v in range(4)]
    ygrp = [E3(nc.sbuf_tensor(f"ygrp{i}", [128, T], BF16)) for i in range(2)]
    prd = [E3(nc.psum_tensor(f"prd{i}", [128, 512], F32)) for i in range(2)]
    pru = [E3(nc.psum_tensor(f"pru{i}", [128, 512], F32)) for i in range(2)]
    pG = [E3(nc.psum_tensor(f"pG{i}", [128, 512], F32)) for i in range(2)]
    for tr in range(128):
        ch = tr // 16
        if tr % 16 == 0:
            P.op("yd", lambda e, ch=ch: e.dma_start(
                out=dch[ch % 2][:], in_=dl_d[ch * 128:(ch + 1) * 128, :]),
                writes=[f"dch{ch % 2}"])
            P.op("yd", lambda e, ch=ch: e.dma_start(
                out=uch[ch % 2][:], in_=u_d[ch * 128:(ch + 1) * 128, :]),
                writes=[f"uch{ch % 2}"])
        bb = tr % 2
        rsl = slice((tr % 16) * 128, (tr % 16) * 128 + 128)
        dc, uc = dch[ch % 2], uch[ch % 2]
        dck, uck = f"dch{ch % 2}", f"uch{ch % 2}"
        for q in range(8):
            s0 = slice(q * 512, (q + 1) * 512)
            pd_ = prd[q % 2]
            pu_ = pru[q % 2]
            P.op("t", lambda e, s0=s0, pd_=pd_, rsl=rsl, dc=dc:
                 e.matmul(pd_[:], r16_sb[:, rsl], dc[:, s0],
                          start=True, stop=True),
                 reads=["r16", dck], writes=[f"prd{q % 2}"])
            P.op("t", lambda e, s0=s0, pu_=pu_, rsl=rsl, uc=uc:
                 e.matmul(pu_[:], r16_sb[:, rsl], uc[:, s0],
                          start=True, stop=True),
                 reads=["r16", uck], writes=[f"pru{q % 2}"])
            P.op("s", lambda e, bb=bb, s0=s0, tr=tr, pd_=pd_: e.activation(
                dA[bb][:, s0], pd_[:], AF.Exp,
                scale=asc_sb[:, tr:tr + 1]),
                reads=[f"prd{q % 2}", "asc"], writes=[f"dA{bb}"])
            P.op("v", lambda e, bb=bb, s0=s0, pu_=pu_: e.tensor_mul(
                db[bb][:, s0], pu_[:], brep[:, s0]),
                reads=[f"pru{q % 2}", "brep"], writes=[f"db{bb}"])
        P.op("v", lambda e, bb=bb: e.tensor_tensor_scan(
            s_t[bb][:], dA[bb][:], db[bb][:], 0.0, ALU.mult, ALU.add),
            reads=[f"dA{bb}", f"db{bb}"], writes=[f"s_t{bb}"])
        v = tr % 4
        P.op("p", lambda e, bb=bb, v=v: e.tensor_mul(
            sC[v][:], s_t[bb][:], crep[:]),
            reads=[f"s_t{bb}", "crep"], writes=[f"sC{v}"])
        if v == 3:
            grp = tr // 4
            yg = ygrp[grp % 2]
            for tc in range(8):
                sl = slice(tc * 512, (tc + 1) * 512)
                pg = pG[tc % 2]
                for vv in range(4):
                    P.op("t", lambda e, pg=pg, vv=vv, sl=sl: e.matmul(
                        pg[32 * vv:32 * vv + 32, :], g4_sb[:], sC[vv][:, sl],
                        start=True, stop=True, tile_position=(0, 32 * vv)),
                        reads=[f"sC{vv}", "g4"], writes=[f"pG{tc % 2}"])
                P.op("v", lambda e, yg=yg, sl=sl, pg=pg: e.tensor_copy(
                    yg[:, sl], pg[:]),
                    reads=[f"pG{tc % 2}"], writes=[f"ygrp{grp % 2}"])
            for vv in range(4):
                P.op("yd", lambda e, grp=grp, vv=vv, yg=yg: e.dma_start(
                    out=y_d[(grp * 4 + vv) * 8:(grp * 4 + vv) * 8 + 8, :],
                    in_=yg[32 * vv:32 * vv + 8, :]),
                    reads=[f"ygrp{grp % 2}"])
    s3.close()
    P.barrier()

    s4 = ExitStack()
    E4 = s4.enter_context
    ya = [E4(nc.sbuf_tensor(f"ya{i}", [128, T], BF16)) for i in range(2)]
    xa = [E4(nc.sbuf_tensor(f"xa{i}", [128, T], BF16)) for i in range(2)]
    y2s = [E4(nc.sbuf_tensor(f"y2s{i}", [128, T], BF16)) for i in range(2)]
    xlo4 = [E4(nc.sbuf_tensor(f"xlo4{i}", [128, T], BF16)) for i in range(2)]
    xtm4 = [E4(nc.sbuf_tensor(f"xtm4{i}", [128, T], BF16)) for i in range(2)]
    for oc in range(8):
        i2 = oc % 2
        P.op("yd", lambda e, oc=oc, i2=i2: e.dma_start(
            out=ya[i2][:], in_=y_d[oc * 128:(oc + 1) * 128, :]),
            writes=[f"ya{i2}"])
        P.op("yd", lambda e, oc=oc, i2=i2: e.dma_start(
            out=xlo4[i2][:], in_=xc_d[oc * 128:(oc + 1) * 128, :]),
            writes=[f"xlo4{i2}"])
        P.op("yd", lambda e, oc=oc, i2=i2: e.dma_start(
            out=xtm4[i2][:], in_=xc_d[(8 + oc) * 128:(9 + oc) * 128, :]),
            writes=[f"xtm4{i2}"])
        P.op("v", lambda e, i2=i2: e.tensor_scalar(
            xlo4[i2][:], xlo4[i2][:], msel[:, 0:1], None, ALU.mult),
            reads=[f"xlo4{i2}", "msel"], writes=[f"xlo4{i2}"])
        P.op("v", lambda e, i2=i2: e.scalar_tensor_tensor(
            xa[i2][:], xtm4[i2][:], msel[:, 1:2], xlo4[i2][:],
            ALU.mult, ALU.add),
            reads=[f"xtm4{i2}", f"xlo4{i2}", "msel"], writes=[f"xa{i2}"])
        P.op("v", lambda e, oc=oc, i2=i2: e.scalar_tensor_tensor(
            y2s[i2][:], xa[i2][:], dsc_sb[:, oc:oc + 1], ya[i2][:],
            ALU.mult, ALU.add),
            reads=[f"ya{i2}", f"xa{i2}", "dsc"], writes=[f"y2s{i2}"])
        P.op("yd", lambda e, oc=oc, i2=i2: e.dma_start(
            out=y2_o[oc * 128:(oc + 1) * 128, :], in_=y2s[i2][:]),
            reads=[f"y2s{i2}"])
    s4.close()


def own_token_idx(sub):
    idx = []
    for i in range(NTILE):
        g = 2 * i + sub
        idx.extend(range(128 * g, 128 * g + 128))
    return np.array(idx)

OWN_IDX = [own_token_idx(0), own_token_idx(1)]


def decl_LC(nc, I, O):
    return dict(
        y2g=nc.dram_tensor("y2g", [DI, T], BF16, kind=I).ap(),
        res=nc.dram_tensor("res", [DI, TLOC], BF16, kind=I).ap(),
        t1=nc.dram_tensor("t1", [D, TLOC], BF16, kind=I).ap(),
        xfm=nc.dram_tensor("xfm", [D, TLOC], F32, kind=I).ap(),
        W_out=nc.dram_tensor("W_out", [DI, D], F32, kind=I).ap(),
        ffn_wT=nc.dram_tensor("ffn_wT", [D, D], F32, kind=I).ap(),
        msel=nc.dram_tensor("msel", [128, 2], F32, kind=I).ap(),
        x2=nc.dram_tensor("x2", [D, TLOC], F32, kind=O).ap(),
        t2a=nc.dram_tensor("t2a", [D, TLOC], BF16, kind=O).ap(),
    )


def build_LC():
    P = Prog("LC")
    st = ExitStack()
    dd = decl_LC(P.nc, "ExternalInput", "ExternalOutput")
    dd["msel_shared"] = False
    body_LC(P, st, dd)
    P.fence_outputs()
    return P, st


def body_LC(P, st, dd):
    nc = NcProxy(P.nc, "lc_")
    E = st.enter_context
    y2gi, resi, t1i, xfm = dd["y2g"], dd["res"], dd["t1"], dd["xfm"]
    wout, fwt = dd["W_out"], dd["ffn_wT"]
    x2_o, t2a_o = dd["x2"], dd["t2a"]
    mselc = E(nc.sbuf_tensor("mselc", [128, 2], F32))
    P.op("yd", lambda e: e.dma_start(out=mselc[:], in_=dd["msel"]),
         writes=["mselc"])

    mix = [E(nc.sbuf_tensor(f"mix{c}", [128, TLOC], F32)) for c in range(8)]
    mixk = [f"mix{c}" for c in range(8)]

    s1 = ExitStack()
    E1 = s1.enter_context
    t1 = [E1(nc.sbuf_tensor(f"t1_{c}", [128, TLOC], BF16)) for c in range(8)]
    y3 = [E1(nc.sbuf_tensor(f"y3_{k}", [128, TLOC], BF16)) for k in range(16)]
    wo_sb = E1(nc.sbuf_tensor("wo", [128, 16 * 1024], BF16))
    ld = [E1(nc.sbuf_tensor("ld0", [128, TLOC], BF16))] * 2
    ld2 = E1(nc.sbuf_tensor("ld2", [128, TLOC], BF16))
    sr = [E1(nc.sbuf_tensor("sr0", [128, TLOC], BF16))] * 2
    pw = [E1(nc.psum_tensor(f"pw{i}", [128, 512], F32)) for i in range(2)]
    for c in range(8):
        P.op("yd", lambda e, c=c: e.dma_start(
            out=t1[c][:], in_=t1i[c * 128:(c + 1) * 128, :]), writes=[f"t1_{c}"])
    for kc in range(16):
        P.op("pd", lambda e, kc=kc: e.dma_start(
            out=wo_sb[:, kc * 1024:(kc + 1) * 1024],
            in_=wout[kc * 128:(kc + 1) * 128, :]), writes=["wo"])
        P.op("yd", lambda e, kc=kc: e.dma_start(
            out=ld[kc % 2][:], in_=resi[kc * 128:(kc + 1) * 128, :]),
            writes=["ld0"])
        P.op("s", lambda e, kc=kc: e.activation(
            sr[kc % 2][:], ld[kc % 2][:], AF.Silu),
            reads=["ld0"], writes=["sr0"])
        P.op("yd", lambda e, kc=kc: e.dma_start(
            out=ld[kc % 2][:],
            in_=y2gi[kc * 128:(kc + 1) * 128, :].rearrange(
                "p (a c) -> p a c", c=256)[:, :, 0:128],
            ),
            writes=["ld0"])
        P.op("yd", lambda e, kc=kc: e.dma_start(
            out=ld2[:],
            in_=y2gi[kc * 128:(kc + 1) * 128, :].rearrange(
                "p (a c) -> p a c", c=256)[:, :, 128:256],
            ),
            writes=["ld2"])
        P.op("v", lambda e, kc=kc: e.tensor_scalar(
            ld[kc % 2][:], ld[kc % 2][:], mselc[:, 0:1], None, ALU.mult),
            reads=["ld0", "mselc"], writes=["ld0"])
        P.op("v", lambda e, kc=kc: e.scalar_tensor_tensor(
            ld2[:], ld2[:], mselc[:, 1:2], ld[kc % 2][:], ALU.mult, ALU.add),
            reads=["ld2", "ld0", "mselc"], writes=["ld2"])
        P.op("v", lambda e, kc=kc: e.tensor_mul(
            y3[kc][:], ld2[:], sr[kc % 2][:]),
            reads=["ld2", "sr0"], writes=[f"y3_{kc}"])
    for oc in range(8):
        for tc in range(4):
            ps = pw[tc % 2]
            sl = slice(tc * 512, (tc + 1) * 512)
            for kc in range(16):
                P.op("t", lambda e, ps=ps, kc=kc, oc=oc, sl=sl: e.matmul(
                    ps[:], wo_sb[:, kc * 1024 + oc * 128:kc * 1024 + oc * 128 + 128],
                    y3[kc][:, sl], start=(kc == 0), stop=(kc == 15)),
                    reads=[f"y3_{kc}", "wo"], writes=[f"pw{tc % 2}"])
            P.op("v", lambda e, ps=ps, oc=oc, sl=sl: e.tensor_add(
                mix[oc][:, sl], ps[:], t1[oc][:, sl]),
                reads=[f"pw{tc % 2}", f"t1_{oc}"], writes=[mixk[oc]])
    s1.close()
    P.barrier()

    s2 = ExitStack()
    rows = ln_rows(P, nc, s2, "lc1", [m[:] for m in mix], mixk, TLOC)
    bc = bcast_rows(P, nc, s2, "lc1", rows, TLOC, ["r", "mr"])
    rb, rbk = bc["r"]
    mrb, mrbk = bc["mr"]
    xl = [s2.enter_context(nc.sbuf_tensor(f"xl{i}", [128, TLOC], F32))
          for i in range(2)]
    tm2 = [s2.enter_context(nc.sbuf_tensor(f"tm2_{i}", [128, TLOC], F32))
           for i in range(2)]
    for c in range(8):
        tb = tm2[c % 2]
        P.op("yd", lambda e, c=c: e.dma_start(
            out=xl[c % 2][:], in_=xfm[c * 128:(c + 1) * 128, :]),
            writes=[f"xl{c % 2}"])
        P.op("v", lambda e, c=c, tb=tb, rb=rb: e.tensor_mul(
            tb[:], mix[c][:], rb[:]),
             reads=[mixk[c], rbk], writes=[f"tm2_{c % 2}"])
        P.op("v", lambda e, c=c, tb=tb, mrb=mrb: e.tensor_sub(
            tb[:], tb[:], mrb[:]),
             reads=[f"tm2_{c % 2}", mrbk], writes=[f"tm2_{c % 2}"])
        P.op("v", lambda e, c=c, tb=tb: e.tensor_add(
            mix[c][:], tb[:], xl[c % 2][:]),
            reads=[f"tm2_{c % 2}", f"xl{c % 2}"], writes=[mixk[c]])
    s2.close()
    P.barrier()

    t2 = [E(nc.sbuf_tensor(f"t2_{c}", [128, TLOC], BF16)) for c in range(8)]
    s3a = ExitStack()
    rows = ln_rows(P, nc, s3a, "lc2", [m[:] for m in mix], mixk, TLOC)
    bc = bcast_rows(P, nc, s3a, "lc2", rows, TLOC, ["r", "mr"])
    rb, rbk = bc["r"]
    mrb, mrbk = bc["mr"]
    tmp = [s3a.enter_context(nc.sbuf_tensor(f"tmpc{i}", [128, TLOC], F32))
           for i in range(2)]
    for c in range(8):
        tb = tmp[c % 2]
        P.op("v", lambda e, c=c, tb=tb, rb=rb: e.tensor_mul(
            tb[:], mix[c][:], rb[:]),
             reads=[mixk[c], rbk], writes=[f"tmpc{c % 2}"])
        P.op("v", lambda e, c=c, tb=tb, mrb=mrb: e.tensor_sub(
            t2[c][:], tb[:], mrb[:]),
             reads=[f"tmpc{c % 2}", mrbk], writes=[f"t2_{c}"])
    s3a.close()
    P.barrier()

    s3 = ExitStack()
    E3 = s3.enter_context
    wsg = E3(nc.sbuf_tensor("wsg", [128, 8 * 1024], BF16))
    wld = [E3(nc.sbuf_tensor(f"wld{i}", [128, 1024], F32)) for i in range(2)]
    z = [E3(nc.sbuf_tensor(f"z{c}", [128, TLOC], BF16)) for c in range(8)]
    zk = [f"z{c}" for c in range(8)]
    pz = [E3(nc.psum_tensor(f"pz{i}", [128, 512], F32)) for i in range(2)]
    for kc in range(8):
        P.op("yd", lambda e, kc=kc: e.dma_start(
            out=wld[kc % 2][:], in_=fwt[kc * 128:(kc + 1) * 128, :]),
            writes=[f"wld{kc % 2}"])
        P.op("s", lambda e, kc=kc: e.sign(
            wsg[:, kc * 1024:(kc + 1) * 1024], wld[kc % 2][:]),
            reads=[f"wld{kc % 2}"], writes=["wsg"])
    for oc in range(8):
        for tc in range(4):
            ps = pz[tc % 2]
            sl = slice(tc * 512, (tc + 1) * 512)
            for kc in range(8):
                P.op("t", lambda e, ps=ps, kc=kc, oc=oc, sl=sl: e.matmul(
                    ps[:], wsg[:, kc * 1024 + oc * 128:kc * 1024 + oc * 128 + 128],
                    t2[kc][:, sl], start=(kc == 0), stop=(kc == 7)),
                    reads=[f"t2_{kc}", "wsg"], writes=[f"pz{tc % 2}"])
            P.op("v", lambda e, ps=ps, oc=oc, sl=sl: e.tensor_copy(
                z[oc][:, sl], ps[:]),
                reads=[f"pz{tc % 2}"], writes=[zk[oc]])
    rows = ln_rows(P, nc, s3, "lc3", [zz[:] for zz in z], zk, TLOC,
                   in_bf16=True)
    bc = bcast_rows(P, nc, s3, "lc3", rows, TLOC, ["r", "mr"])
    rb3, rbk3 = bc["r"]
    mrb3, mrbk3 = bc["mr"]
    for c in range(8):
        P.op("v", lambda e, c=c: e.tensor_mul(z[c][:], z[c][:], rb3[:]),
             reads=[zk[c], rbk3], writes=[zk[c]])
        P.op("v", lambda e, c=c: e.tensor_sub(z[c][:], z[c][:], mrb3[:]),
             reads=[zk[c], mrbk3], writes=[zk[c]])
        P.op("v", lambda e, c=c: e.tensor_add(mix[c][:], mix[c][:], z[c][:]),
             reads=[mixk[c], zk[c]], writes=[mixk[c]])
        P.op("yd", lambda e, c=c: e.dma_start(
            out=x2_o[c * 128:(c + 1) * 128, :], in_=mix[c][:]),
            reads=[mixk[c]])
    s3.close()
    P.barrier()

    s4 = ExitStack()
    E4 = s4.enter_context
    rows = ln_rows(P, nc, s4, "lc4", [m[:] for m in mix], mixk, TLOC)
    bc = bcast_rows(P, nc, s4, "lc4", rows, TLOC, ["r", "mr"])
    rb4, rbk4 = bc["r"]
    mrb4, mrbk4 = bc["mr"]
    tmp4 = [E4(nc.sbuf_tensor(f"tmp4_{i}", [128, TLOC], F32)) for i in range(2)]
    t2a = [E4(nc.sbuf_tensor(f"t2a{i}", [128, TLOC], BF16)) for i in range(2)]
    for c in range(8):
        tb = tmp4[c % 2]
        ta = t2a[c % 2]
        P.op("v", lambda e, c=c, tb=tb: e.tensor_mul(tb[:], mix[c][:], rb4[:]),
             reads=[mixk[c], rbk4], writes=[f"tmp4_{c % 2}"])
        P.op("v", lambda e, c=c, tb=tb, ta=ta: e.tensor_sub(ta[:], tb[:], mrb4[:]),
             reads=[f"tmp4_{c % 2}", mrbk4], writes=[f"t2a{c % 2}"])
        P.op("yd", lambda e, c=c, ta=ta: e.dma_start(
            out=t2a_o[c * 128:(c + 1) * 128, :], in_=ta[:]),
            reads=[f"t2a{c % 2}"])
    s4.close()


def decl_LD(nc, I, O):
    return dict(
        t2ag=nc.dram_tensor("t2ag", [DI, TLOC], BF16, kind=I).ap(),
        t2a_own=nc.dram_tensor("t2a_own", [D, TLOC], BF16, kind=I).ap(),
        x2=nc.dram_tensor("x2_ld", [D, TLOC], F32, kind=I).ap(),
        Wq=nc.dram_tensor("Wq", [D, D], F32, kind=I).ap(),
        Wk=nc.dram_tensor("Wk", [D, D], F32, kind=I).ap(),
        Wv=nc.dram_tensor("Wv", [D, D], F32, kind=I).ap(),
        Wo=nc.dram_tensor("Wo", [D, D], F32, kind=I).ap(),
        bq_r=nc.dram_tensor("bq_r", [128, 8], F32, kind=I).ap(),
        bo_r=nc.dram_tensor("bo_r", [128, 8], F32, kind=I).ap(),
        bv_row=nc.dram_tensor("bv_row", [1, D], F32, kind=I).ap(),
        masks=nc.dram_tensor("masks", [4 * MAXMASK_G * 128, 512], BF16,
                             kind=I).ap(),
        x3=nc.dram_tensor("x3", [D, TLOC], F32, kind=O).ap(),
        t3=nc.dram_tensor("t3", [D, TLOC], BF16, kind=O).ap(),
    )


def build_LD():
    P = Prog("LD")
    st = ExitStack()
    dd = decl_LD(P.nc, "ExternalInput", "ExternalOutput")
    body_LD(P, st, dd)
    P.fence_outputs()
    return P, st


def body_LD(P, st, dd):
    nc = NcProxy(P.nc, "ld_")
    E = st.enter_context
    t2ag = dd["t2ag"]
    t2ao = dd["t2a_own"]
    x2i = dd["x2"]
    wq_i, wk_i, wv_i, wo_i = dd["Wq"], dd["Wk"], dd["Wv"], dd["Wo"]
    bq_r, bo_r, bv_row = dd["bq_r"], dd["bo_r"], dd["bv_row"]
    maskin = dd["masks"]
    x3_o, t3_o = dd["x3"], dd["t3"]
    k_d = nc.dram_tensor("k_d", [D, T], BF16).ap()
    ve_d = nc.dram_tensor("ve_d", [T, H * 65], BF16).ap()
    nm_d = nc.dram_tensor("nm_d", [2, 8], F32).ap()

    ofm = [E(nc.sbuf_tensor(f"ofm{c}", [128, TLOC], BF16)) for c in range(8)]
    mcol = E(nc.sbuf_tensor("mcol", [128, H], F32))
    onesr = E(nc.sbuf_tensor("onesr_ld", [1, 128], F32))
    ones64 = E(nc.sbuf_tensor("ones64", [1, 64], BF16))
    s12 = ExitStack()
    qsb = [s12.enter_context(nc.sbuf_tensor(f"q{c}", [128, TLOC], BF16))
           for c in range(8)]
    P.op("v", lambda e: e.memset(onesr[:], 1.0), writes=["onesr"])
    P.op("v", lambda e: e.memset(ones64[:], 1.0), writes=["ones64"])

    s1 = ExitStack()
    E1 = s1.enter_context
    tmp21 = E1(nc.sbuf_tensor("tmp21", [2, 1], F32))
    cmax = E1(nc.sbuf_tensor("cmax", [2, 8], F32))
    qmax = E1(nc.sbuf_tensor("qmax", [2, 8], F32))
    mm8 = E1(nc.sbuf_tensor("mm8", [2, 8], F32))
    negm = E1(nc.sbuf_tensor("negm", [1, 16], F32))
    hsel = E1(nc.sbuf_tensor("hsel", [128, 2], F32))
    bqs = E1(nc.sbuf_tensor("bqs", [128, 8], F32))
    bvb = E1(nc.sbuf_tensor("bvb", [128, D], BF16))
    bvrow_f = E1(nc.sbuf_tensor("bvrow_f", [1, D], F32))
    eighth = E1(nc.sbuf_tensor("eighth", [128, 1], F32))
    t2a = [E1(nc.sbuf_tensor(f"t2a{c}", [128, T], BF16)) for c in range(8)]
    wsb = E1(nc.sbuf_tensor("wsb", [128, 8 * 1024], BF16))
    sqf = E1(nc.sbuf_tensor("sqf", [128, 512], F32))
    pk = [E1(nc.psum_tensor(f"pkx{i}", [128, 512], F32)) for i in range(2)]
    pn = E1(nc.psum_tensor("pn", [2, 512], F32))
    pvv = E1(nc.psum_tensor("pvv", [128, 1024], F32))
    pb1 = E1(nc.psum_tensor("pb1", [128, 512], F32))
    s1k = ExitStack()
    kst = [s1k.enter_context(nc.sbuf_tensor(f"kst{i}", [128, T], BF16))
           for i in range(2)]

    P.op("v", lambda e: e.memset(cmax[:], 0.0), writes=["cmax"])
    P.op("v", lambda e: e.memset(qmax[:], 0.0), writes=["qmax"])
    P.op("v", lambda e: e.memset(hsel[:], 0.0), writes=["hsel"])
    P.op("v", lambda e: e.memset(hsel[0:64, 0:1], 1.0), writes=["hsel"])
    P.op("v", lambda e: e.memset(hsel[64:128, 1:2], 1.0), writes=["hsel"])
    P.op("v", lambda e: e.memset(eighth[:], 0.125), writes=["eighth"])
    P.op("yd", lambda e: e.dma_start(out=bqs[:], in_=bq_r), writes=["bqs"])
    P.op("s", lambda e: e.mul(bqs[:], bqs[:], 0.125), reads=["bqs"],
         writes=["bqs"])
    P.op("yd", lambda e: e.dma_start(out=bvrow_f[:], in_=bv_row),
         writes=["bvrow"])
    for c in range(8):
        for rk in range(2):
            P.op("yd", lambda e, c=c, rk=rk: e.dma_start(
                out=t2a[c][:].rearrange("p (a c2) -> p a c2", c2=256)[
                    :, :, rk * 128:(rk + 1) * 128],
                in_=t2ag[rk * D + c * 128:rk * D + (c + 1) * 128, :].rearrange(
                    "p (a b) -> p a b", b=128)),
                writes=[f"t2a{c}"])
    for nn in range(2):
        P.op("t", lambda e, nn=nn: e.matmul(
            pb1[:], onesr[:], bvrow_f[:, nn * 512:(nn + 1) * 512],
            start=True, stop=True),
            reads=["bvrow", "onesr"], writes=["pb1"])
        P.op("s", lambda e, nn=nn: e.copy(bvb[:, nn * 512:(nn + 1) * 512],
                                          pb1[:]),
             reads=["pb1"], writes=["bvb"])

    def load_w(src):
        for kc in range(8):
            P.op("pd", lambda e, kc=kc, src=src: e.dma_start(
                out=wsb[:, kc * 1024:(kc + 1) * 1024],
                in_=src[kc * 128:(kc + 1) * 128, :]), writes=["wsb"])

    load_w(wk_i)
    for oc in range(8):
        sg = kst[oc % 2]
        for tc in range(8):
            ps = pk[tc % 2]
            sl = slice(tc * 512, (tc + 1) * 512)
            for kc in range(8):
                P.op("t", lambda e, ps=ps, kc=kc, oc=oc, sl=sl: e.matmul(
                    ps[:], wsb[:, kc * 1024 + oc * 128:kc * 1024 + oc * 128 + 128],
                    t2a[kc][:, sl], start=(kc == 0), stop=(kc == 7)),
                    reads=[f"t2a{kc}", "wsb"], writes=[f"pkx{tc % 2}"])
            P.op("s", lambda e, sg=sg, sl=sl, ps=ps: e.copy(sg[:, sl], ps[:]),
                 reads=[f"pkx{tc % 2}"], writes=[f"kst{oc % 2}"])
        P.op("yd", lambda e, oc=oc, sg=sg: e.dma_start(
            out=k_d[oc * 128:(oc + 1) * 128, :], in_=sg[:]),
            reads=[f"kst{oc % 2}"])
        for tc in range(8):
            sl = slice(tc * 512, (tc + 1) * 512)
            P.op("s", lambda e, sg=sg, sl=sl: e.activation(
                sqf[:], sg[:, sl], AF.Square),
                reads=[f"kst{oc % 2}"], writes=["sqf"])
            P.op("t", lambda e: e.matmul(
                pn[:], hsel[:], sqf[:], start=True, stop=True),
                reads=["sqf", "hsel"], writes=["pn"])
            P.op("v", lambda e: e.reduce_max(
                tmp21[:], pn[:], axis=mybir.AxisListType.X),
                reads=["pn"], writes=["tmp21"])
            P.op("v", lambda e, oc=oc: e.tensor_max(
                cmax[:, oc:oc + 1], cmax[:, oc:oc + 1], tmp21[:]),
                reads=["tmp21", "cmax"], writes=["cmax"])

    s1k.close()
    P.barrier()
    s1v = ExitStack()
    stg = [s1v.enter_context(nc.sbuf_tensor(f"stg{i}", [128, 2080], BF16))
           for i in range(2)]
    load_w(wv_i)
    for tk in range(32):
        vst = stg[tk % 2]
        for kc in range(8):
            for nn in range(2):
                P.op("t", lambda e, tk=tk, kc=kc, nn=nn: e.matmul(
                    pvv[:, nn * 512:(nn + 1) * 512],
                    t2a[kc][:, tk * 128:(tk + 1) * 128],
                    wsb[:, kc * 1024 + nn * 512:kc * 1024 + (nn + 1) * 512],
                    start=(kc == 0), stop=(kc == 7)),
                    reads=[f"t2a{kc}", "wsb"], writes=[f"pvv{nn}"])
        vview = vst[:, 0:1040].rearrange("p (a b) -> p a b", b=65)
        P.op("v", lambda e, vview=vview: e.tensor_add(
            vview[:, :, 0:64],
            pvv[:].rearrange("p (a b) -> p a b", b=64),
            bvb[:].rearrange("p (a b) -> p a b", b=64)),
            reads=["pvv0", "pvv1", "bvb"], writes=[f"stg{tk % 2}"])
        P.op("v", lambda e, vview=vview: e.memset(vview[:, :, 64:65], 1.0),
             reads=[], writes=[f"stg{tk % 2}"])
        P.op("yd", lambda e, tk=tk, vst=vst: e.dma_start(
            out=ve_d[tk * 128:(tk + 1) * 128, :], in_=vst[:, 0:1040]),
            reads=[f"stg{tk % 2}"])

    s1v.close()
    P.barrier()
    sqfq = E1(nc.sbuf_tensor("sqfq", [128, 512], F32))
    pnq = E1(nc.psum_tensor("pnq", [2, 512], F32))
    tmp21q = E1(nc.sbuf_tensor("tmp21q", [2, 1], F32))
    qt2 = [E1(nc.sbuf_tensor(f"qt2{c}", [128, TLOC], BF16)) for c in range(8)]
    for c in range(8):
        P.op("yd", lambda e, c=c: e.dma_start(
            out=qt2[c][:], in_=t2ao[c * 128:(c + 1) * 128, :]),
            writes=[f"qt2{c}"])
    load_w(wq_i)
    for oc in range(8):
        for tc in range(4):
            ps = pk[tc % 2]
            sl = slice(tc * 512, (tc + 1) * 512)
            for kc in range(8):
                P.op("t", lambda e, ps=ps, kc=kc, oc=oc, sl=sl: e.matmul(
                    ps[:], wsb[:, kc * 1024 + oc * 128:kc * 1024 + oc * 128 + 128],
                    qt2[kc][:, sl], start=(kc == 0), stop=(kc == 7)),
                    reads=[f"qt2{kc}", "wsb"], writes=[f"pkx{tc % 2}"])
            P.op("v", lambda e, oc=oc, sl=sl, ps=ps: e.tensor_scalar(
                qsb[oc][:, sl], ps[:], eighth[:, 0:1], bqs[:, oc:oc + 1],
                ALU.mult, ALU.add),
                reads=[f"pkx{tc % 2}", "eighth", "bqs"], writes=[f"q{oc}"])
        for tc in range(4):
            sl = slice(tc * 512, (tc + 1) * 512)
            P.op("s", lambda e, oc=oc, sl=sl: e.activation(
                sqfq[:], qsb[oc][:, sl], AF.Square),
                reads=[f"q{oc}"], writes=["sqfq"])
            P.op("t", lambda e: e.matmul(
                pnq[:], hsel[:], sqfq[:], start=True, stop=True),
                reads=["sqfq", "hsel"], writes=["pnq"])
            P.op("v", lambda e: e.reduce_max(
                tmp21q[:], pnq[:], axis=mybir.AxisListType.X),
                reads=["pnq"], writes=["tmp21q"])
            P.op("v", lambda e, oc=oc: e.tensor_max(
                qmax[:, oc:oc + 1], qmax[:, oc:oc + 1], tmp21q[:]),
                reads=["tmp21q", "qmax"], writes=["qmax"])

    P.op("v", lambda e: e.tensor_mul(mm8[:], cmax[:], qmax[:]),
         reads=["cmax", "qmax"], writes=["mm8"])
    P.op("s", lambda e: e.activation(mm8[:], mm8[:], AF.Sqrt),
         reads=["mm8"], writes=["mm8"])
    P.op("v", lambda e: e.tensor_scalar_mul(mm8[:], mm8[:], -1.0),
         reads=["mm8"], writes=["mm8"])
    P.op("yd", lambda e: e.dma_start(out=nm_d, in_=mm8[:]),
         reads=["mm8"], writes=["nm_d"])
    P.op("yd", lambda e: e.dma_start(
        out=negm[:].rearrange("q (a b) -> q a b", b=2),
        in_=nm_d.rearrange("(q b) a -> q a b", q=1),
        allow_slow_non_contiguous=True),
        reads=["nm_d"], writes=["negm"])
    for h in range(H):
        P.op("t", lambda e, h=h: e.matmul(
            pb1[:, 0:1], onesr[:], negm[:, h:h + 1], start=True, stop=True),
            reads=["negm", "onesr"], writes=["pb1"])
        P.op("v", lambda e, h=h: e.tensor_copy(mcol[:, h:h + 1], pb1[:, 0:1]),
             reads=["pb1"], writes=["mcol"])
    s1.close()
    P.barrier()

    s2 = ExitStack()
    E2 = s2.enter_context
    msk = E2(nc.sbuf_tensor("msk", [128, 4 * MAXMASK_G * 512], BF16))
    P.op("yd", lambda e: e.dma_start(
        out=msk[:].rearrange("p (a b) -> p a b", b=512),
        in_=maskin.rearrange("(a p) b -> p a b", p=128)), writes=["msk"])
    khb = [E2(nc.sbuf_tensor(f"khb{i}", [128, T], BF16)) for i in range(2)]
    veh = [E2(nc.sbuf_tensor(f"veh{i}", [128, 32 * 65], BF16)) for i in range(2)]
    ptb = [E2(nc.sbuf_tensor(f"ptb{i}", [128, 32 * 512], BF16)) for i in range(2)]
    rs_sb = [E2(nc.sbuf_tensor(f"rs{i}", [1, 512], F32)) for i in range(2)]
    obf = [E2(nc.sbuf_tensor(f"obf{i}", [64, 512], BF16)) for i in range(2)]
    psS = [E2(nc.psum_tensor(f"psS{i}", [128, 512], F32)) for i in range(4)]
    pO = [E2(nc.psum_tensor(f"pO{i}", [65, 512], F32)) for i in range(2)]
    prs = [E2(nc.psum_tensor(f"prs{i}", [64, 512], F32)) for i in range(2)]
    for h in range(H):
        hb = h % 2
        hc = h // 2
        kh = khb[h % 2]
        ve = veh[h % 2]
        P.op("yd", lambda e, h=h, kh=kh, hb=hb: e.dma_start(
            out=kh[hb * 64:hb * 64 + 64, :],
            in_=k_d[h * 64:h * 64 + 64, :]), writes=[f"khb{h % 2}"])
        P.op("yd", lambda e, h=h, ve=ve: e.dma_start(
            out=ve[:].rearrange("p (a b) -> p a b", b=65),
            in_=ve_d.rearrange("(a p) o -> p a o", p=128)[
                :, :, h * 65:(h + 1) * 65]),
            writes=[f"veh{h % 2}"])
        for qg in range(4):
            nkt = NKT_G[qg]
            ffa = FFA_G[qg]
            pt = ptb[qg % 2]
            ptk = f"ptb{qg % 2}"
            qv = qsb[hc][hb * 64:hb * 64 + 64, qg * 512:(qg + 1) * 512]
            for kt in range(nkt):
                pss = psS[kt % 4]
                psk = f"psS{kt % 4}"
                P.op("t", lambda e, pss=pss, kh=kh, hb=hb, kt=kt, qv=qv:
                     e.matmul(pss[:],
                              kh[hb * 64:hb * 64 + 64, kt * 128:(kt + 1) * 128],
                              qv, start=True, stop=True),
                     reads=[f"khb{h % 2}", f"q{hc}"], writes=[psk])
                if kt >= ffa:
                    moff = (qg * MAXMASK_G + (kt - ffa)) * 512
                    P.op("v", lambda e, pss=pss, moff=moff: e.tensor_add(
                        pss[:], pss[:], msk[:, moff:moff + 512]),
                        reads=[psk, "msk"], writes=[psk])
                P.op("s", lambda e, pss=pss, pt=pt, kt=kt, h=h: e.activation(
                    pt[:, kt * 512:(kt + 1) * 512], pss[:], AF.Exp,
                    bias=mcol[:, h:h + 1]),
                    reads=[psk, "mcol"], writes=[ptk])
            po = pO[qg % 2]
            pok = f"pO{qg % 2}"
            for kt in range(nkt):
                P.op("t", lambda e, po=po, ve=ve, pt=pt, kt=kt,
                     nkt=nkt: e.matmul(
                         po[:], ve[:, kt * 65:(kt + 1) * 65],
                         pt[:, kt * 512:(kt + 1) * 512],
                         start=(kt == 0), stop=(kt == nkt - 1)),
                     reads=[f"veh{h % 2}", ptk], writes=[pok])
            rs = rs_sb[qg % 2]
            ob = obf[qg % 2]
            pr2 = prs[qg % 2]
            prk = f"prs{qg % 2}"
            P.op("v", lambda e, po=po, rs=rs: e.reciprocal(
                rs[:], po[64:65, :]),
                reads=[pok], writes=[f"rs{qg % 2}"])
            P.op("t", lambda e, rs=rs, pr2=pr2: e.matmul(
                pr2[:], onesr[0:1, 0:64], rs[:], start=True, stop=True),
                reads=[f"rs{qg % 2}", "onesr"], writes=[prk])
            P.op("s", lambda e, po=po, ob=ob: e.copy(ob[:], po[0:64, :]),
                 reads=[pok], writes=[f"obf{qg % 2}"])
            P.op("v", lambda e, ob=ob, hc=hc, hb=hb, qg=qg, pr2=pr2:
                 e.tensor_mul(
                     ofm[hc][hb * 64:hb * 64 + 64, qg * 512:(qg + 1) * 512],
                     ob[:], pr2[:]),
                 reads=[f"obf{qg % 2}", prk], writes=[f"ofm{hc}"])
    s2.close()
    s12.close()
    P.barrier()

    s3 = ExitStack()
    E3 = s3.enter_context
    ao = [E3(nc.sbuf_tensor(f"ao{c}", [128, TLOC], F32)) for c in range(8)]
    aok = [f"ao{c}" for c in range(8)]
    wsb3 = E3(nc.sbuf_tensor("wsb3", [128, 8 * 1024], BF16))
    bo_sb = E3(nc.sbuf_tensor("bo_sb", [128, 8], F32))
    pz = [E3(nc.psum_tensor(f"pz3{i}", [128, 512], F32)) for i in range(2)]
    P.op("yd", lambda e: e.dma_start(out=bo_sb[:], in_=bo_r), writes=["bo"])
    for kc in range(8):
        P.op("pd", lambda e, kc=kc: e.dma_start(
            out=wsb3[:, kc * 1024:(kc + 1) * 1024],
            in_=wo_i[kc * 128:(kc + 1) * 128, :]), writes=["wsb3"])
    for oc in range(8):
        for tc in range(4):
            ps = pz[tc % 2]
            sl = slice(tc * 512, (tc + 1) * 512)
            for kc in range(8):
                P.op("t", lambda e, ps=ps, kc=kc, oc=oc, sl=sl: e.matmul(
                    ps[:], wsb3[:, kc * 1024 + oc * 128:kc * 1024 + oc * 128 + 128],
                    ofm[kc][:, sl], start=(kc == 0), stop=(kc == 7)),
                    reads=[f"ofm{kc}", "wsb3"], writes=[f"pz3{tc % 2}"])
            P.op("v", lambda e, ps=ps, oc=oc, sl=sl: e.tensor_scalar(
                ao[oc][:, sl], ps[:], bo_sb[:, oc:oc + 1], None, ALU.add),
                reads=[f"pz3{tc % 2}", "bo"], writes=[aok[oc]])
    x2l = [E3(nc.sbuf_tensor(f"x2l{i}", [128, TLOC], F32)) for i in range(2)]
    tm3 = [E3(nc.sbuf_tensor(f"tm3_{i}", [128, TLOC], F32)) for i in range(2)]
    s3x = ExitStack()
    rows = ln_rows(P, nc, s3x, "ld1", [a[:] for a in ao], aok, TLOC)
    bc = bcast_rows(P, nc, s3x, "ld1", rows, TLOC, ["r", "mr"])
    rb1, rbk1 = bc["r"]
    mrb1, mrbk1 = bc["mr"]
    for c in range(8):
        tb = tm3[c % 2]
        P.op("yd", lambda e, c=c: e.dma_start(
            out=x2l[c % 2][:], in_=x2i[c * 128:(c + 1) * 128, :]),
            writes=[f"x2l{c % 2}"])
        P.op("v", lambda e, c=c, tb=tb, rb1=rb1: e.tensor_mul(
            tb[:], ao[c][:], rb1[:]),
            reads=[aok[c], rbk1], writes=[f"tm3_{c % 2}"])
        P.op("v", lambda e, c=c, tb=tb, mrb1=mrb1: e.tensor_sub(
            tb[:], tb[:], mrb1[:]),
            reads=[f"tm3_{c % 2}", mrbk1], writes=[f"tm3_{c % 2}"])
        P.op("v", lambda e, c=c, tb=tb: e.tensor_add(
            ao[c][:], tb[:], x2l[c % 2][:]),
            reads=[f"tm3_{c % 2}", f"x2l{c % 2}"], writes=[aok[c]])
        P.op("yd", lambda e, c=c: e.dma_start(
            out=x3_o[c * 128:(c + 1) * 128, :], in_=ao[c][:]),
            reads=[aok[c]])
    s3x.close()
    P.barrier()
    s3x = ExitStack()
    rows = ln_rows(P, nc, s3x, "ld2", [a[:] for a in ao], aok, TLOC)
    bc = bcast_rows(P, nc, s3x, "ld2", rows, TLOC, ["r", "mr"])
    rb2, rbk2 = bc["r"]
    mrb2, mrbk2 = bc["mr"]
    t3s = [s3x.enter_context(nc.sbuf_tensor(f"t3s{i}", [128, TLOC], BF16))
           for i in range(2)]
    for c in range(8):
        tb = tm3[c % 2]
        t3b = t3s[c % 2]
        P.op("v", lambda e, c=c, tb=tb, rb2=rb2: e.tensor_mul(
            tb[:], ao[c][:], rb2[:]),
            reads=[aok[c], rbk2], writes=[f"tm3_{c % 2}"])
        P.op("v", lambda e, c=c, tb=tb, t3b=t3b, mrb2=mrb2: e.tensor_sub(
            t3b[:], tb[:], mrb2[:]),
            reads=[f"tm3_{c % 2}", mrbk2], writes=[f"t3s{c % 2}"])
        P.op("yd", lambda e, c=c, t3b=t3b: e.dma_start(
            out=t3_o[c * 128:(c + 1) * 128, :], in_=t3b[:]),
            reads=[f"t3s{c % 2}"])
    s3x.close()
    s3.close()


def ld_in_maps(ins, t2a_cores, x2_cores):
    Wq = np.ascontiguousarray(ins["L1_attn_Wq"]).astype(np.float32)
    Wk = np.ascontiguousarray(ins["L1_attn_Wk"]).astype(np.float32)
    Wv = np.ascontiguousarray(ins["L1_attn_Wv"]).astype(np.float32)
    Wo = np.ascontiguousarray(ins["L1_attn_Wo"]).astype(np.float32)
    bq = np.asarray(ins["L1_attn_bq"], np.float32)
    bv = np.asarray(ins["L1_attn_bv"], np.float32)
    bo = np.asarray(ins["L1_attn_bo"], np.float32)
    maps = []
    for c in range(NCHIP):
        b, sub = c // 2, c % 2
        t2ag = np.concatenate([t2a_cores[2 * b], t2a_cores[2 * b + 1]], axis=0)
        masks = np.zeros((4, MAXMASK_G, 128, 512), np.float32)
        for qg in range(4):
            qpos = np.empty(512, np.int64)
            for ci in range(4):
                i = 4 * qg + ci
                g = 2 * i + sub
                qpos[ci * 128:(ci + 1) * 128] = 128 * g + np.arange(128)
            bnd = np.array([bound_of(int(q)) for q in qpos])
            for j in range(NKT_G[qg] - FFA_G[qg]):
                kt = FFA_G[qg] + j
                kpos = 128 * kt + np.arange(128)
                masks[qg, j] = np.where(kpos[:, None] <= bnd[None, :],
                                        0.0, NEG)
        maps.append({
            "t2ag": np.ascontiguousarray(t2ag), "t2a_own": t2a_cores[c],
            "x2_ld": x2_cores[c],
            "Wq": Wq, "Wk": Wk, "Wv": Wv, "Wo": Wo,
            "bq_r": np.ascontiguousarray(bq.reshape(8, 128).T),
            "bo_r": np.ascontiguousarray(bo.reshape(8, 128).T),
            "bv_row": bv.reshape(1, D),
            "masks": np.ascontiguousarray(
                masks.reshape(4 * MAXMASK_G * 128, 512)).astype(BF),
        })
    return maps



def decl_LE(nc, I, O):
    return dict(
        x3=nc.dram_tensor("x3_le", [D, TLOC], F32, kind=I).ap(),
        t3=nc.dram_tensor("t3_le", [D, TLOC], BF16, kind=I).ap(),
        w1T=nc.dram_tensor("w1T", [D, 4096], F32, kind=I).ap(),
        w2T=nc.dram_tensor("w2T", [D, 4096], F32, kind=I).ap(),
        h1T=nc.dram_tensor("h1T", [4096, 4096], F32, kind=I).ap(),
        h2T=nc.dram_tensor("h2T", [4096, 4096], F32, kind=I).ap(),
        w3T=nc.dram_tensor("w3T", [4096, D], F32, kind=I).ap(),
        s1_r=nc.dram_tensor("s1_r", [128, 32], F32, kind=I).ap(),
        s2_r=nc.dram_tensor("s2_r", [128, 32], F32, kind=I).ap(),
        h1s_r=nc.dram_tensor("h1s_r", [128, 32], F32, kind=I).ap(),
        h2s_r=nc.dram_tensor("h2s_r", [128, 32], F32, kind=I).ap(),
        w3s_r=nc.dram_tensor("w3s_r", [128, 8], F32, kind=I).ap(),
        x4=nc.dram_tensor("x4", [D, TLOC], F32, kind=O).ap(),
    )


def build_LE():
    P = Prog("LE")
    st = ExitStack()
    dd = decl_LE(P.nc, "ExternalInput", "ExternalOutput")
    body_LE(P, st, dd)
    P.fence_outputs()
    return P, st


def body_LE(P, st, dd):
    nc = NcProxy(P.nc, "le_")
    E = st.enter_context
    x3i, t3i = dd["x3"], dd["t3"]
    w1t, w2t = dd["w1T"], dd["w2T"]
    h1t, h2t, w3t = dd["h1T"], dd["h2T"], dd["w3T"]
    s1r, s2r, h1sr, h2sr, w3sr = (dd["s1_r"], dd["s2_r"], dd["h1s_r"],
                                  dd["h2s_r"], dd["w3s_r"])
    x4_o = dd["x4"]

    t3 = [E(nc.sbuf_tensor(f"t3_{c}", [128, TLOC], BF16)) for c in range(8)]
    sc = {}
    for nm, ap_ in [("s1", s1r), ("s2", s2r), ("h1s", h1sr), ("h2s", h2sr),
                    ("w3s", w3sr)]:
        tsb = E(nc.sbuf_tensor("sc_" + nm, [128, ap_.shape[1]], F32))
        P.op("yd", lambda e, tsb=tsb, ap_=ap_: e.dma_start(out=tsb[:], in_=ap_),
             writes=["sc" + nm])
        sc[nm] = tsb
    for c in range(8):
        P.op("yd", lambda e, c=c: e.dma_start(
            out=t3[c][:], in_=t3i[c * 128:(c + 1) * 128, :]),
            writes=[f"t3_{c}"])

    TH = 1024
    for th in range(2):
        sTh = ExitStack()
        ETh = sTh.enter_context
        tsl = slice(th * TH, (th + 1) * TH)
        g = [ETh(nc.sbuf_tensor(f"g{th}_{k}", [128, TH], BF16))
             for k in range(32)]
        gk = [f"g{th}_{k}" for k in range(32)]
        wp = [ETh(nc.sbuf_tensor(f"wp{th}_{i}", [128, 32 * 128], BF16))
              for i in range(2)]
        ps2 = [ETh(nc.psum_tensor(f"ps{th}_{i}", [128, 512], F32))
               for i in range(2)]
        sU = ExitStack()
        u = [sU.enter_context(nc.sbuf_tensor(f"u{th}_{k}", [128, TH], BF16))
             for k in range(32)]
        uk = [f"u{th}_{k}" for k in range(32)]

        sA = ExitStack()
        a = [sA.enter_context(nc.sbuf_tensor(f"a{th}_{k}", [128, TH], BF16))
             for k in range(2)]
        for oc in range(32):
            ar = a[oc % 2]
            akey = f"a{th}_{oc % 2}"
            for wi, wt_ in ((0, w1t), (1, w2t)):
                w = wp[wi]
                wkey = f"wp{th}_{wi}"
                P.op("pd", lambda e, w=w, oc=oc, wt_=wt_: e.dma_start(
                    out=w[:, 0:8 * 128].rearrange("p (a b) -> p a b", b=128),
                    in_=wt_.rearrange("(a p) o -> p a o", p=128)[
                        :, :, oc * 128:(oc + 1) * 128]), writes=[wkey])
                for tc in range(2):
                    ps = ps2[tc % 2]
                    sl = slice(th * TH + tc * 512, th * TH + (tc + 1) * 512)
                    ol = slice(tc * 512, (tc + 1) * 512)
                    for kc in range(8):
                        P.op("t", lambda e, ps=ps, w=w, kc=kc, sl=sl: e.matmul(
                            ps[:], w[:, kc * 128:(kc + 1) * 128],
                            t3[kc][:, sl], start=(kc == 0), stop=(kc == 7)),
                            reads=[wkey, f"t3_{kc}"],
                            writes=[f"ps{th}_{tc % 2}"])
                    if wi == 0:
                        P.op("s", lambda e, ps=ps, oc=oc, ol=ol, ar=ar:
                             e.activation(ar[:, ol], ps[:], AF.Silu,
                                          scale=sc["s1"][:, oc:oc + 1]),
                             reads=[f"ps{th}_{tc % 2}", "scs1"], writes=[akey])
                    else:
                        P.op("v", lambda e, ps=ps, oc=oc, ol=ol, ar=ar:
                             e.scalar_tensor_tensor(
                                 g[oc][:, ol], ps[:], sc["s2"][:, oc:oc + 1],
                                 ar[:, ol], ALU.mult, ALU.mult),
                             reads=[f"ps{th}_{tc % 2}", "scs2", akey],
                             writes=[gk[oc]])
        sA.close()
        P.barrier()
        for oc in range(32):
            w = wp[oc % 2]
            wkey = f"wp{th}_{oc % 2}"
            P.op("pd", lambda e, w=w, oc=oc: e.dma_start(
                out=w[:].rearrange("p (a b) -> p a b", b=128),
                in_=h1t.rearrange("(a p) o -> p a o", p=128)[
                    :, :, oc * 128:(oc + 1) * 128]), writes=[wkey])
            for tc in range(2):
                ps = ps2[tc % 2]
                ol = slice(tc * 512, (tc + 1) * 512)
                for kc in range(32):
                    P.op("t", lambda e, ps=ps, w=w, kc=kc, ol=ol: e.matmul(
                        ps[:], w[:, kc * 128:(kc + 1) * 128],
                        g[kc][:, ol], start=(kc == 0), stop=(kc == 31)),
                        reads=[wkey, gk[kc]], writes=[f"ps{th}_{tc % 2}"])
                P.op("s", lambda e, ps=ps, oc=oc, ol=ol: e.activation(
                    u[oc][:, ol], ps[:], AF.Gelu, scale=sc["h1s"][:, oc:oc + 1]),
                    reads=[f"ps{th}_{tc % 2}", "sch1s"], writes=[uk[oc]])
        for oc in range(32):
            w = wp[oc % 2]
            wkey = f"wp{th}_{oc % 2}"
            P.op("pd", lambda e, w=w, oc=oc: e.dma_start(
                out=w[:].rearrange("p (a b) -> p a b", b=128),
                in_=h2t.rearrange("(a p) o -> p a o", p=128)[
                    :, :, oc * 128:(oc + 1) * 128]), writes=[wkey])
            for tc in range(2):
                ps = ps2[tc % 2]
                ol = slice(tc * 512, (tc + 1) * 512)
                for kc in range(32):
                    P.op("t", lambda e, ps=ps, w=w, kc=kc, ol=ol: e.matmul(
                        ps[:], w[:, kc * 128:(kc + 1) * 128],
                        u[kc][:, ol], start=(kc == 0), stop=(kc == 31)),
                        reads=[wkey, uk[kc]], writes=[f"ps{th}_{tc % 2}"])
                P.op("s", lambda e, ps=ps, oc=oc, ol=ol: e.activation(
                    g[oc][:, ol], ps[:], AF.Gelu, scale=sc["h2s"][:, oc:oc + 1]),
                    reads=[f"ps{th}_{tc % 2}", "sch2s"], writes=[gk[oc]])
        sU.close()
        P.barrier()
        sZ = ExitStack()
        z3 = [sZ.enter_context(nc.sbuf_tensor(f"z3{th}_{c}", [128, TH], BF16))
              for c in range(8)]
        zk = [f"z3{th}_{c}" for c in range(8)]
        for oc in range(8):
            w = wp[oc % 2]
            wkey = f"wp{th}_{oc % 2}"
            P.op("pd", lambda e, w=w, oc=oc: e.dma_start(
                out=w[:].rearrange("p (a b) -> p a b", b=128),
                in_=w3t.rearrange("(a p) o -> p a o", p=128)[
                    :, :, oc * 128:(oc + 1) * 128]), writes=[wkey])
            for tc in range(2):
                ps = ps2[tc % 2]
                ol = slice(tc * 512, (tc + 1) * 512)
                for kc in range(32):
                    P.op("t", lambda e, ps=ps, w=w, kc=kc, ol=ol: e.matmul(
                        ps[:], w[:, kc * 128:(kc + 1) * 128],
                        g[kc][:, ol], start=(kc == 0), stop=(kc == 31)),
                        reads=[wkey, gk[kc]], writes=[f"ps{th}_{tc % 2}"])
                P.op("v", lambda e, ps=ps, oc=oc, ol=ol: e.tensor_scalar(
                    z3[oc][:, ol], ps[:], sc["w3s"][:, oc:oc + 1], None,
                    ALU.mult),
                    reads=[f"ps{th}_{tc % 2}", "scw3s"], writes=[zk[oc]])
        rows = ln_rows(P, nc, sZ, f"le{th}", [z[:] for z in z3], zk, TH,
                       in_bf16=True)
        bc = bcast_rows(P, nc, sZ, f"le{th}", rows, TH, ["r", "mr"])
        rbz, rbkz = bc["r"]
        mrbz, mrbkz = bc["mr"]
        x3l = [sZ.enter_context(nc.sbuf_tensor(f"x3l{th}_{i}", [128, TH], F32))
               for i in range(2)]
        x4s = [sZ.enter_context(nc.sbuf_tensor(f"x4s{th}_{i}", [128, TH], F32))
               for i in range(2)]
        for c in range(8):
            xb = x3l[c % 2]
            ob = x4s[c % 2]
            P.op("yd", lambda e, c=c, xb=xb, tsl=tsl: e.dma_start(
                out=xb[:], in_=x3i[c * 128:(c + 1) * 128, tsl]),
                writes=[f"x3l{th}_{c % 2}"])
            P.op("v", lambda e, c=c, rbz=rbz: e.tensor_mul(
                z3[c][:], z3[c][:], rbz[:]),
                reads=[zk[c], rbkz], writes=[zk[c]])
            P.op("v", lambda e, c=c, mrbz=mrbz: e.tensor_sub(
                z3[c][:], z3[c][:], mrbz[:]),
                reads=[zk[c], mrbkz], writes=[zk[c]])
            P.op("v", lambda e, c=c, xb=xb, ob=ob: e.tensor_add(
                ob[:], z3[c][:], xb[:]),
                reads=[zk[c], f"x3l{th}_{c % 2}"], writes=[f"x4s{th}_{c % 2}"])
            P.op("yd", lambda e, c=c, ob=ob, tsl=tsl: e.dma_start(
                out=x4_o[c * 128:(c + 1) * 128, tsl], in_=ob[:]),
                reads=[f"x4s{th}_{c % 2}"])
        sZ.close()
        sTh.close()
        P.barrier()


def le_in_maps(ins, x3_cores, t3_cores):
    w1t = np.ascontiguousarray(np.asarray(ins["L1_ffn_w1_w"], np.float32).T)
    w2t = np.ascontiguousarray(np.asarray(ins["L1_ffn_w2_w"], np.float32).T)
    h1t = np.ascontiguousarray(np.asarray(ins["L1_ffn_h1_w"], np.float32).T)
    h2t = np.ascontiguousarray(np.asarray(ins["L1_ffn_h2_w"], np.float32).T)
    w3t = np.ascontiguousarray(np.asarray(ins["L1_ffn_w3_w"], np.float32).T)
    rs = lambda v, nch: np.ascontiguousarray(
        np.asarray(v, np.float32).reshape(nch, 128).T)
    common = {
        "w1T": w1t, "w2T": w2t, "h1T": h1t, "h2T": h2t, "w3T": w3t,
        "s1_r": rs(ins["L1_ffn_w1_s"], 32), "s2_r": rs(ins["L1_ffn_w2_s"], 32),
        "h1s_r": rs(ins["L1_ffn_h1_s"], 32), "h2s_r": rs(ins["L1_ffn_h2_s"], 32),
        "w3s_r": rs(ins["L1_ffn_w3_s"], 8),
    }
    maps = []
    for c in range(NCHIP):
        m = dict(common)
        m["x3_le"] = x3_cores[c]
        m["t3_le"] = t3_cores[c]
        maps.append(m)
    return maps



def la_in_maps(ins):
    x = ins["L0_attn_hidden_states"]
    nw_r = np.ascontiguousarray(
        ins["L0_attn_norm_w"].reshape(8, 128).T).astype(np.float32)
    W_in = np.ascontiguousarray(ins["L0_attn_W_in"]).astype(np.float32)
    maps = []
    for c in range(NCHIP):
        b, sub = c // 2, c % 2
        xfm = np.ascontiguousarray(np.asarray(x[b])[OWN_IDX[sub]].T,
                                   dtype=np.float32)
        maps.append({"xfm": xfm, "norm_w_r": nw_r, "W_in": W_in})
    return maps


def lb_perm(sub):
    h0 = np.arange(sub * DHALF, (sub + 1) * DHALF)
    h1 = np.arange((1 - sub) * DHALF, (2 - sub) * DHALF)
    return np.concatenate([h0, h1])


def lb_in_maps(ins, xin_cores):
    conv_w = np.asarray(ins["L0_attn_conv_w"], np.float32)
    conv_b = np.asarray(ins["L0_attn_conv_b"], np.float32)
    W_x = np.asarray(ins["L0_attn_W_x"], np.float32)
    W_dt = np.asarray(ins["L0_attn_W_dt"], np.float32)
    b_dt = np.asarray(ins["L0_attn_b_dt"], np.float32)
    A = np.asarray(ins["L0_attn_A"], np.float32)
    Dv = np.asarray(ins["L0_attn_D"], np.float32)
    R16 = np.zeros((128, 16 * 128), np.float32)
    for trm in range(16):
        for p in range(128):
            R16[trm * 8 + p // 16, trm * 128 + p] = 1.0
    RB0 = np.zeros((16, 128), np.float32)
    for p in range(128):
        RB0[p % 16, p] = 1.0
    G4 = np.zeros((128, 32), np.float32)
    for k in range(128):
        G4[k, k // 16] = 1.0
    maps = []
    cwr = np.zeros((128, 64), np.float32)
    cbr = np.zeros((128, 16), np.float32)
    for kc in range(16):
        cwr[:, kc * 4:(kc + 1) * 4] = conv_w[kc * 128:(kc + 1) * 128]
        cbr[:, kc] = conv_b[kc * 128:(kc + 1) * 128]
    for c in range(NCHIP):
        b, sub = c // 2, c % 2
        half = slice(sub * DHALF, (sub + 1) * DHALF)
        xin_full = np.empty((DI, T), dtype=BF)
        for g in range(32):
            src = xin_cores[2 * b + (g % 2)]
            i = g // 2
            xin_full[:, g * 128:(g + 1) * 128] = src[:, i * 128:(i + 1) * 128]
        xin_pad = np.zeros((DI, T + 3), dtype=BF)
        xin_pad[:, 3:] = xin_full
        A_h = A[half]
        A_sc = np.zeros((128, 128), np.float32)
        for p in range(128):
            A_sc[p, :] = A_h[np.arange(128) * 8 + p // 16, p % 16]
        msel = np.zeros((128, 2), np.float32)
        msel[:, 0] = 1.0 - sub
        msel[:, 1] = float(sub)
        maps.append({
            "xin_pad": xin_pad,
            "conv_w_r": cwr, "conv_b_r": cbr,
            "wx_dlt": np.ascontiguousarray(W_x[:, 0:64]),
            "wx_b": np.ascontiguousarray(W_x[:, 64:80]),
            "wx_c": np.ascontiguousarray(W_x[:, 80:96]),
            "wdt_half": np.ascontiguousarray(W_dt[:, half]),
            "bdt_r": np.ascontiguousarray(b_dt[half].reshape(8, 128).T),
            "A_sc": A_sc,
            "D_sc": np.ascontiguousarray(Dv[half].reshape(8, 128).T),
            "R16": R16, "RB0": RB0, "G4": G4, "msel": msel,
        })
    return maps


def lc_in_maps(ins, y2_cores, res_cores, t1_cores):
    W_out = np.ascontiguousarray(ins["L0_attn_W_out"]).astype(np.float32)
    fwt = np.ascontiguousarray(np.asarray(
        ins["L0_ffn_weight"], np.float32).T)
    x = ins["L0_attn_hidden_states"]
    maps = []
    for c in range(NCHIP):
        b, sub = c // 2, c % 2
        y2f = np.concatenate([y2_cores[2 * b], y2_cores[2 * b + 1]], axis=0)
        msel = np.zeros((128, 2), np.float32)
        msel[:, 0] = 1.0 - sub
        msel[:, 1] = float(sub)
        maps.append({
            "y2g": np.ascontiguousarray(y2f),
            "res": res_cores[c], "t1": t1_cores[c],
            "xfm": np.ascontiguousarray(np.asarray(x[b])[OWN_IDX[sub]].T,
                                        dtype=np.float32),
            "W_out": W_out, "ffn_wT": fwt, "msel": msel,
        })
    return maps


_PROG_CACHE = {}

def get_prog(name, builder):
    if name not in _PROG_CACHE:
        P, st = builder()
        P.emit(st)
        st.close()
        _PROG_CACHE[name] = P
    return _PROG_CACHE[name]


def run_phase(P, in_maps):
    res = run_bass_kernel_spmd(P.nc, in_maps, core_ids=list(range(NCHIP)))
    return res.results


RG_PAIRS = [[0, 1], [2, 3], [4, 5], [6, 7]]


def _dbg_out(P, nc, src_ap, shape, dt):
    dbg = nc.dram_tensor("dbgout", shape, dt, kind="ExternalOutput").ap()
    stX = ExitStack()
    tile = stX.enter_context(nc.sbuf_tensor("dbg_t", [128, shape[1]], dt))
    for ch in range(shape[0] // 128):
        P.op("yd", lambda e, ch=ch: e.dma_start(
            out=tile[:], in_=src_ap[ch * 128:(ch + 1) * 128, :]),
            writes=["dbgt"])
        P.op("yd", lambda e, ch=ch: e.dma_start(
            out=dbg[ch * 128:(ch + 1) * 128, :], in_=tile[:]),
            reads=["dbgt"])
    P.fence_outputs()
    stX.close()


def build_FUSED(upto="LE"):
    P = Prog("FUSED")
    nc = P.nc
    I, N = "ExternalInput", "Internal"
    dram = lambda n, s, dt, k: nc.dram_tensor(n, s, dt, kind=k)

    xfm = dram("xfm", [D, TLOC], F32, I).ap()
    nw = dram("norm_w_r", [128, 8], F32, I).ap()
    w_in = dram("W_in", [D, 2 * DI], F32, I).ap()
    msel = dram("msel", [128, 2], F32, I).ap()
    t1_h = dram("t1", [D, TLOC], BF16, N)
    xin_h = dram("xin", [DI, TLOC], BF16, N)
    res_h = dram("res", [DI, TLOC], BF16, N)
    xing_h = dram("xing", [2 * DI, TLOC], BF16, N)
    xinpad_h = dram("xin_pad", [DI, T + 3], BF16, N)
    y2_h = dram("y2", [DHALF, T], BF16, N)
    y2g_h = dram("y2g", [DI, T], BF16, N)
    x2_h = dram("x2", [D, TLOC], F32, N)
    t2a_h = dram("t2a", [D, TLOC], BF16, N)
    t2ag_h = dram("t2ag", [DI, TLOC], BF16, N)
    x3_h = dram("x3", [D, TLOC], F32, N)
    t3_h = dram("t3", [D, TLOC], BF16, N)
    x4 = dram("x4", [D, TLOC], F32, "ExternalOutput").ap()

    stA = ExitStack()
    body_LA(P, stA, dict(xfm=xfm, norm_w_r=nw, W_in=w_in, t1=t1_h.ap(),
                         xin=xin_h.ap(), res=res_h.ap()))
    P.barrier()
    stA.close()

    P.op("cc", lambda e: e.collective_compute(
        "AllGather", ALU.bypass, RG_PAIRS,
        ins=[xin_h.ap().opt()], outs=[xing_h.ap().opt()]))
    P.barrier()
    stR = ExitStack()
    z3sb = stR.enter_context(nc.sbuf_tensor("fz3", [128, 3], BF16))
    P.op("v", lambda e: e.memset(z3sb[:], 0.0), writes=["fz3"])
    xing = xing_h.ap()
    xinpad = xinpad_h.ap()
    for ch in range(16):
        P.op("yd", lambda e, ch=ch: e.dma_start(
            out=xinpad[ch * 128:(ch + 1) * 128, 0:3], in_=z3sb[:]),
            reads=["fz3"])
        for rk in range(2):
            P.op("yd", lambda e, ch=ch, rk=rk: e.dma_start(
                out=xinpad[ch * 128:(ch + 1) * 128, 3:].rearrange(
                    "p (a c) -> p a c", c=256)[:, :, rk * 128:(rk + 1) * 128],
                in_=xing[rk * DI + ch * 128:rk * DI + (ch + 1) * 128,
                         :].rearrange("p (a b) -> p a b", b=128)))
    P.barrier()
    stR.close()
    if upto == "REPACK":
        _dbg_out(P, nc, xinpad, [DI, T + 3], BF16)
        return P, ExitStack()
    if upto == "LA":
        _dbg_out(P, nc, xin_h.ap(), [DI, TLOC], BF16)
        return P, ExitStack()

    stB = ExitStack()
    ddB = dict(
        xin_pad=xinpad,
        conv_w_r=dram("conv_w_r", [128, 64], F32, I).ap(),
        conv_b_r=dram("conv_b_r", [128, 16], F32, I).ap(),
        wx_dlt=dram("wx_dlt", [DI, 64], F32, I).ap(),
        wx_b=dram("wx_b", [DI, 16], F32, I).ap(),
        wx_c=dram("wx_c", [DI, 16], F32, I).ap(),
        wdt_half=dram("wdt_half", [64, DHALF], F32, I).ap(),
        bdt_r=dram("bdt_r", [128, 8], F32, I).ap(),
        A_sc=dram("A_sc", [128, 128], F32, I).ap(),
        D_sc=dram("D_sc", [128, 8], F32, I).ap(),
        R16=dram("R16", [128, 16 * 128], F32, I).ap(),
        RB0=dram("RB0", [16, 128], F32, I).ap(),
        G4=dram("G4", [128, 32], F32, I).ap(),
        msel=msel,
        y2=y2_h.ap(),
    )
    body_LB(P, stB, ddB)
    P.barrier()
    stB.close()
    if upto == "LB":
        _dbg_out(P, nc, y2_h.ap(), [DHALF, T], BF16)
        return P, ExitStack()

    P.op("cc", lambda e: e.collective_compute(
        "AllGather", ALU.bypass, RG_PAIRS,
        ins=[y2_h.ap().opt()], outs=[y2g_h.ap().opt()]))
    P.barrier()

    stC = ExitStack()
    ddC = dict(
        y2g=y2g_h.ap(), res=res_h.ap(), t1=t1_h.ap(), xfm=xfm,
        W_out=dram("W_out", [DI, D], F32, I).ap(),
        ffn_wT=dram("ffn_wT", [D, D], F32, I).ap(),
        msel=msel,
        x2=x2_h.ap(), t2a=t2a_h.ap(),
    )
    body_LC(P, stC, ddC)
    P.barrier()
    stC.close()
    if upto == "LC":
        _dbg_out(P, nc, x2_h.ap(), [D, TLOC], F32)
        return P, ExitStack()

    P.op("cc", lambda e: e.collective_compute(
        "AllGather", ALU.bypass, RG_PAIRS,
        ins=[t2a_h.ap().opt()], outs=[t2ag_h.ap().opt()]))
    P.barrier()

    stD = ExitStack()
    ddD = dict(
        t2ag=t2ag_h.ap(), t2a_own=t2a_h.ap(), x2=x2_h.ap(),
        Wq=dram("Wq", [D, D], F32, I).ap(),
        Wk=dram("Wk", [D, D], F32, I).ap(),
        Wv=dram("Wv", [D, D], F32, I).ap(),
        Wo=dram("Wo", [D, D], F32, I).ap(),
        bq_r=dram("bq_r", [128, 8], F32, I).ap(),
        bo_r=dram("bo_r", [128, 8], F32, I).ap(),
        bv_row=dram("bv_row", [1, D], F32, I).ap(),
        masks=dram("masks", [4 * MAXMASK_G * 128, 512], BF16, I).ap(),
        x3=x3_h.ap(), t3=t3_h.ap(),
    )
    body_LD(P, stD, ddD)
    P.barrier()
    stD.close()
    if upto == "LD":
        _dbg_out(P, nc, x3_h.ap(), [D, TLOC], F32)
        return P, ExitStack()

    stE = ExitStack()
    ddE = dict(
        x3=x3_h.ap(), t3=t3_h.ap(),
        w1T=dram("w1T", [D, 4096], F32, I).ap(),
        w2T=dram("w2T", [D, 4096], F32, I).ap(),
        h1T=dram("h1T", [4096, 4096], F32, I).ap(),
        h2T=dram("h2T", [4096, 4096], F32, I).ap(),
        w3T=dram("w3T", [4096, D], F32, I).ap(),
        s1_r=dram("s1_r", [128, 32], F32, I).ap(),
        s2_r=dram("s2_r", [128, 32], F32, I).ap(),
        h1s_r=dram("h1s_r", [128, 32], F32, I).ap(),
        h2s_r=dram("h2s_r", [128, 32], F32, I).ap(),
        w3s_r=dram("w3s_r", [128, 8], F32, I).ap(),
        x4=x4,
    )
    body_LE(P, stE, ddE)
    P.fence_outputs()
    stE.close()
    st = ExitStack()
    return P, st


def fused_in_maps(ins):
    la = la_in_maps(ins)
    lbw = lb_in_maps(ins, [np.zeros((DI, TLOC), BF)] * NCHIP)
    lcw = lc_in_maps(ins, [np.zeros((DHALF, T), BF)] * NCHIP,
                     [None] * NCHIP, [None] * NCHIP)
    ldw = ld_in_maps(ins, [np.zeros((D, TLOC), BF)] * NCHIP, [None] * NCHIP)
    lew = le_in_maps(ins, [None] * NCHIP, [None] * NCHIP)
    maps = []
    for c in range(NCHIP):
        m = {}
        m.update(la[c])
        for k, v in lbw[c].items():
            if k != "xin_pad":
                m[k] = v
        for k in ("W_out", "ffn_wT"):
            m[k] = lcw[c][k]
        for k, v in ldw[c].items():
            if k not in ("t2ag", "t2a_own", "x2_ld"):
                m[k] = v
        for k, v in lew[c].items():
            if k not in ("x3_le", "t3_le"):
                m[k] = v
        maps.append(m)
    return maps


def kernel(**inputs):
    P = get_prog("FUSED", build_FUSED)
    maps = fused_in_maps(inputs)
    res = run_phase(P, maps)
    out = np.empty((B, T, D), np.float32)
    for c in range(NCHIP):
        b, sub = c // 2, c % 2
        out[b][OWN_IDX[sub]] = res[c]["x4"].T
    return out


def kernel_unfused(**inputs):
    ins = inputs
    PA = get_prog("LA", build_LA)
    ra = run_phase(PA, la_in_maps(ins))
    xin_cores = [r["xin"] for r in ra]
    res_cores = [r["res"] for r in ra]
    t1_cores = [r["t1"] for r in ra]
    PB = get_prog("LB", build_LB)
    rb = run_phase(PB, lb_in_maps(ins, xin_cores))
    y2_cores = [r["y2"] for r in rb]
    PC = get_prog("LC", build_LC)
    rc = run_phase(PC, lc_in_maps(ins, y2_cores, res_cores, t1_cores))
    x2_cores = [r["x2"] for r in rc]
    t2a_cores = [r["t2a"] for r in rc]
    PD = get_prog("LD", build_LD)
    rd = run_phase(PD, ld_in_maps(ins, t2a_cores, x2_cores))
    x3_cores = [r["x3"] for r in rd]
    t3_cores = [r["t3"] for r in rd]
    PE_ = get_prog("LE", build_LE)
    re_ = run_phase(PE_, le_in_maps(ins, x3_cores, t3_cores))
    out = np.empty((B, T, D), np.float32)
    for c in range(NCHIP):
        b, sub = c // 2, c % 2
        out[b][OWN_IDX[sub]] = re_[c]["x4"].T
    return out
